# version 22
# baseline (speedup 1.0000x reference)
import numpy as np
import ml_dtypes
import concourse.bass as bass
import concourse.mybir as mybir
from concourse.bass_utils import run_bass_kernel_spmd
from contextlib import ExitStack

F32 = mybir.dt.float32
BF16 = mybir.dt.bfloat16
AF = mybir.ActivationFunctionType
AL = mybir.AluOpType
AX = mybir.AxisListType.X

D = 1024
S = 4096
B = 8
NP = 8
NKT = 32
NCORES = 8
EPS = 1e-5


class Chan:

    def __init__(self, st, nc, name, step=1):
        self.sem = st.enter_context(nc.semaphore(name))
        self.name = name
        self.step = step
        self.n = 0
        self.emitted = 0

    def bump(self, k=1):
        self.n += k * self.step
        return self.n

    def inc(self, inst):
        self.emitted += self.step
        inst.then_inc(self.sem, self.step)
        return inst


class Plan:
    ENGS = ("tensor", "scalar", "vector", "gpsimd", "sync")

    def __init__(self):
        self.ops = {k: [] for k in self.ENGS}

    def __call__(self, eng, fn):
        self.ops[eng].append(fn)

    def emit(self, block):
        for eng in self.ENGS:
            ops = self.ops[eng]
            getattr(block, eng)(lambda e, ops=ops: [fn(e) for fn in ops])


def ln_stats(P, ch_sum, ch_rst, xi, sq_junk, sl, s1, s2, mean, var, msq,
             rstd, epst, width):
    def f_sums(v, xi=xi, sl=sl):
        v.drain()
        v.reduce_sum(s1[:, sl], xi, axis=AX)
        v.scalar_tensor_tensor(sq_junk, xi, 1.0, xi, AL.mult, AL.mult,
                               accum_out=s2[:, sl])
        v.drain()
        v.tensor_scalar_mul(mean[:, sl], s1[:, sl], 1.0 / width)
        v.scalar_tensor_tensor(msq[:, sl], s1[:, sl], 1.0 / (width * width),
                               s1[:, sl], AL.mult, AL.mult)
        v.tensor_scalar_mul(var[:, sl], s2[:, sl], 1.0 / width)
        v.drain()
        ch_sum.inc(v.tensor_sub(var[:, sl], var[:, sl], msq[:, sl]))
    P("vector", f_sums)
    ch_sum.bump()

    def f_rstd(sc, sl=sl, sm=ch_sum.n):
        sc.wait_ge(ch_sum.sem, sm)
        sc.activation(msq[:, sl], var[:, sl], AF.Ln, bias=epst)
        sc.drain()
        ch_rst.inc(sc.activation(rstd[:, sl], msq[:, sl], AF.Exp, scale=-0.5))
    P("scalar", f_rstd)
    ch_rst.bump()


def ln_stats_sc(P, ch_rst, xi, sq_junk, sq_junk2, sl, s1, s2, mean, msq,
                rstd, width, wait=None, dr=False):
    def f_stat(sc, xi=xi, sl=sl, wait=wait, sq_junk2=sq_junk2, dr=dr):
        if wait is not None:
            sc.wait_ge(wait[0].sem, wait[1])
        sc.activation(sq_junk, xi, AF.Square, accum_out=s2[:, sl])
        if dr:
            sc.drain()
        sc.activation(sq_junk2, xi, AF.Copy, accum_out=s1[:, sl])
        sc.drain()
        sc.mul(mean[:, sl], s1[:, sl], 1.0 / width)
        sc.activation(msq[:, sl], s1[:, sl], AF.Square, scale=1.0 / width)
        sc.drain()
        sc.activation(msq[:, sl], msq[:, sl], AF.Copy, scale=-1.0, bias=EPS)
        sc.drain()
        sc.activation(rstd[:, sl], s2[:, sl], AF.Ln, scale=1.0 / width,
                      bias=msq[:, sl])
        sc.drain()
        ch_rst.inc(sc.activation(rstd[:, sl], rstd[:, sl], AF.Exp, scale=-0.5))
    P("scalar", f_stat)
    ch_rst.bump()



def build_attn_program(trunc="full"):
    LVL = {"ln1": 0, "qkst": 1, "attn1": 2, "full": 3}[trunc]
    nc = bass.Bass("TRN2", target_bir_lowering=False, debug=False)
    x = nc.dram_tensor("x", [B, 512, D], F32, kind="ExternalInput").ap()
    wqT = nc.dram_tensor("wqT", [D, D], F32, kind="ExternalInput").ap()
    wkT = nc.dram_tensor("wkT", [D, D], F32, kind="ExternalInput").ap()
    wvT = nc.dram_tensor("wvT", [D, D], F32, kind="ExternalInput").ap()
    woT = nc.dram_tensor("woT", [D, D], F32, kind="ExternalInput").ap()
    bo = nc.dram_tensor("bo", [1, D], F32, kind="ExternalInput").ap()
    xr = nc.dram_tensor("xr", [B, 512, D], F32, kind="ExternalInput").ap()
    ident = nc.dram_tensor("ident", [128, 128], F32, kind="ExternalInput").ap()
    bsel = nc.dram_tensor("bsel", [8, 128], F32, kind="ExternalInput").ap()
    rowneg = nc.dram_tensor("rowneg", [1, 128], F32, kind="ExternalInput").ap()
    rowpos = nc.dram_tensor("rowpos", [1, 128], F32, kind="ExternalInput").ap()
    x1 = nc.dram_tensor("x1", [S, D], F32, kind="ExternalOutput").ap()
    v_scr = nc.dram_tensor("v_scr", [2, 512, D], BF16, kind="Internal").ap()
    od2 = nc.dram_tensor("od2", [2, 512 * 16, 64], BF16, kind="Internal").ap()
    dbg = (nc.dram_tensor("dbg", [128, 49152], BF16, kind="ExternalOutput").ap()
           if LVL < 3 else None)

    st = ExitStack()
    sb = lambda name, shape, dt: st.enter_context(nc.sbuf_tensor(name, shape, dt))
    psa = lambda name, shape, dt: st.enter_context(nc.psum_tensor(name, shape, dt))

    wq_sb = sb("wq_sb", [128, 8 * D], BF16)
    wk_sb = sb("wk_sb", [128, 8 * D], BF16)
    wv_sb = sb("wv_sb", [128, 8 * D], BF16)
    wo_sb = sb("wo_sb", [128, 8 * D], BF16)
    xin = sb("xin", [128, 2 * D], F32)
    xln_sb = sb("xln_sb", [128, 2 * D], BF16)
    xlnT_t = sb("xlnT_t", [128, 2 * 1024], BF16)
    qtok = sb("qtok", [128, 2 * D], BF16)
    ktok = sb("ktok", [128, 2 * D], BF16)
    vtok = sb("vtok", [128, 2 * D], BF16)
    qst = sb("qst", [64, 512 * 16], BF16)
    kst = sb("kst", [64, 512 * 16], BF16)
    vst = sb("vst", [128, 2 * 4 * 65], BF16)
    pt_sb = sb("pt_sb", [128, 2 * 512], BF16)
    otn = sb("otn", [64, 512], BF16)
    orow = sb("orow", [128, 2 * 256], BF16)
    rl_sb = sb("rl_sb", [128, 512], BF16)
    rlb_sb = sb("rlb_sb", [64, 512], F32)
    ones64 = sb("ones64", [128, 64], BF16)
    zin = sb("zin", [128, 2 * D], BF16)
    zlT_t = sb("zlT_t", [128, 2 * 1024], BF16)
    zb = sb("zb", [128, 2 * D], F32)
    x1t = sb("x1t", [128, 2 * D], F32)
    sqj = sb("sqj", [128, D], BF16)
    bo_bc = sb("bo_bc", [128, D], F32)
    bo_sb = sb("bo_sb", [128, D], BF16)
    ident_sb = sb("ident_sb", [128, 128], BF16)
    bsel_sb = sb("bsel_sb", [8, 128], BF16)
    rneg_sb = sb("rneg_sb", [1, 128], BF16)
    rpos_sb = sb("rpos_sb", [1, 128], BF16)
    epst = sb("epst", [128, 1], F32)
    s1 = sb("s1", [128, 2], F32)
    s2 = sb("s2", [128, 2], F32)
    mean = sb("mean", [128, 2], F32)
    var = sb("var", [128, 2], F32)
    msq = sb("msq", [128, 2], F32)
    rstd = sb("rstd", [128, 2], F32)

    ps_p0 = psa("ps_p0", [128, 512], F32)
    ps_p1 = psa("ps_p1", [128, 512], F32)
    ps_a = psa("ps_a", [128, 2 * 512], F32)
    ps_ot = psa("ps_ot", [128, 512], F32)
    ps_rl = psa("ps_rl", [128, 512], F32)
    ps_tr = psa("ps_tr", [128, 1024], BF16)

    _chans = []
    def c(name, step=1):
        ch = Chan(st, nc, name, step)
        _chans.append(ch)
        return ch
    ch_xd = (c("xd0", 16), c("xd1", 16))
    ch_cd = c("cd", 16)
    ch_sum = c("sum")
    ch_rst = c("rst")
    ch_xln = c("xln")
    ch_tr = c("tr")
    ch_trev = c("trev")
    ch_pj = c("pj")
    ch_pjev = c("pjev")
    ch_vsp = (c("vsp0", 16), c("vsp1", 16))
    ch_stk = c("stk")
    ch_am = c("am")
    ch_exp = c("exp")
    ch_vld = (c("vld0", 16), c("vld1", 16))
    ch_pv = c("pv")
    ch_rl = c("rl")
    ch_rlb = c("rlb")
    ch_rlbe = c("rlbe")
    ch_nrm = c("nrm")
    ch_orw = c("orw")
    ch_ods = (c("ods0", 16), c("ods1", 16))
    ch_zld = (c("zld0", 16), c("zld1", 16))
    ch_z = c("z")
    ch_zev = c("zev")
    ch_bob = c("bob")
    ch_x1 = c("x1c")
    ch_sto = (c("sto0", 16), c("sto1", 16))
    ch_vfree = c("vfree")

    P = Plan()

    def g_consts(g):
        for out_ap, in_ap in ((ident_sb[:], ident[:]), (bsel_sb[:], bsel[:]),
                              (rneg_sb[:], rowneg[:]), (rpos_sb[:], rowpos[:]),
                              (bo_sb[64:65, :], bo[:])):
            ch_cd.inc(g.dma_start(out=out_ap, in_=in_ap))
        for w_sb_, wdram in ((wq_sb, wqT), (wk_sb, wkT), (wv_sb, wvT),
                             (wo_sb, woT)):
            ch_cd.inc(g.dma_start(
                out=w_sb_[:].rearrange("p (ft m) -> p ft m", ft=8),
                in_=wdram.rearrange("(ft p) m -> p ft m", p=128)))
    P("gpsimd", g_consts)
    ch_cd.bump(9)
    CONSTS = ch_cd.n

    def f_ones_init(v, cd=CONSTS):
        v.wait_ge(ch_cd.sem, cd)
        v.memset(epst[:], EPS)
        v.memset(vst[:].rearrange("p (b e) -> p b e", b=8)[:, :, 64:65], 1.0)
        ch_bob.inc(v.memset(ones64[64:65, :], 1.0))
    P("vector", f_ones_init)
    ch_bob.bump()
    ONES_RDY = ch_bob.n

    for half in (0, 1):
        def f_bob_mm(t, half=half, onr=ONES_RDY, ze=ch_zev.n):
            t.wait_ge(ch_bob.sem, onr)
            if ze:
                t.wait_ge(ch_zev.sem, ze)
            ch_z.inc(t.matmul(ps_rl[0:64, :], ones64[64:65, :],
                              bo_sb[64:65, half * 512:(half + 1) * 512],
                              start=True, stop=True, tile_position=(64, 0)))
        P("tensor", f_bob_mm)
        ch_z.bump()

        def f_bob_ev(sc, half=half, zc=ch_z.n):
            sc.wait_ge(ch_z.sem, zc)
            inst = sc.copy(bo_bc[0:64, half * 512:(half + 1) * 512], ps_rl[0:64, :])
            ch_zev.inc(inst)
        P("scalar", f_bob_ev)
        ch_zev.bump()
    for half in (0, 1):
        def f_bob_mm2(t, half=half, ze=ch_zev.n):
            t.wait_ge(ch_zev.sem, ze)
            ch_z.inc(t.matmul(ps_rl[64:128, :], ones64[64:65, 0:64],
                              bo_sb[64:65, half * 512:(half + 1) * 512],
                              start=True, stop=True, tile_position=(64, 64)))
        P("tensor", f_bob_mm2)
        ch_z.bump()

        def f_bob_ev2(sc, half=half, zc=ch_z.n):
            sc.wait_ge(ch_z.sem, zc)
            inst = sc.copy(bo_bc[64:128, half * 512:(half + 1) * 512],
                           ps_rl[64:128, :])
            ch_zev.inc(inst)
            ch_bob.inc(sc.drain())
        P("scalar", f_bob_ev2)
        ch_zev.bump()
        ch_bob.bump()
    BOB = ch_bob.n
    PS_RL_FREE = ch_zev.n

    NTIL = 4
    NPK = 64
    tr_hist = []
    exp_hist = []
    nrm_hist = []
    PJ_EV = {}

    for n in range(B):
        for tt in range(NTIL):
            gti = n * NTIL + tt
            buf = gti % 2
            xi = xin[:, buf * D:(buf + 1) * D]
            xl = xln_sb[:, buf * D:(buf + 1) * D]
            xt = xlnT_t[:, buf * 1024:(buf + 1) * 1024]
            sl = slice(buf, buf + 1)
            chx = ch_xd[buf]

            def f_load(s, n=n, tt=tt, gti=gti, xi=xi, chx=chx,
                       need=ch_vfree.n - 1):
                if gti >= 2:
                    s.wait_ge(ch_vfree.sem, need)
                chx.inc(s.dma_start(
                    out=xi, in_=x[n, tt * 128:(tt + 1) * 128, :]))
            P("sync", f_load)
            chx.bump()

            def f_prewait(v, chx=chx, xd=chx.n):
                v.wait_ge(chx.sem, xd)
            P("vector", f_prewait)
            ln_stats(P, ch_sum, ch_rst, xi, sqj[:], sl, s1, s2, mean, var,
                     msq, rstd, epst[:], D)

            def f_apply(v, xi=xi, xl=xl, sl=sl, rs=ch_rst.n,
                        trneed=(tr_hist[gti - 2][0] if gti >= 2 else 0)):
                v.wait_ge(ch_rst.sem, rs)
                if trneed:
                    v.wait_ge(ch_tr.sem, trneed)
                ch_xln.inc(v.tensor_scalar(xl, xi, mean[:, sl], rstd[:, sl],
                                           AL.subtract, AL.mult))
                ch_vfree.inc(v.drain())
            P("vector", f_apply)
            ch_xln.bump()
            ch_vfree.bump()

            def f_tr(t, xl=xl, xc=ch_xln.n, tev=ch_trev.n, cd=CONSTS):
                t.wait_ge(ch_xln.sem, xc)
                t.wait_ge(ch_cd.sem, cd)
                if tev:
                    t.wait_ge(ch_trev.sem, tev)
                for ft in range(8):
                    inst = t.transpose(ps_tr[:, ft * 128:(ft + 1) * 128],
                                       xl[:, ft * 128:(ft + 1) * 128],
                                       ident_sb[:])
                ch_tr.inc(inst)
            P("tensor", f_tr)
            ch_tr.bump()
            tr_hist.append((ch_tr.n, 0))
            XLN_TR = ch_tr.n

            def f_trev(v, xt=xt, trc=ch_tr.n):
                v.wait_ge(ch_tr.sem, trc)
                ch_trev.inc(v.tensor_copy(xt, ps_tr[:]))
            P("vector", f_trev)
            ch_trev.bump()
            XT_RDY = ch_trev.n

            for w_sb_, dtok in ((wq_sb, qtok), (wk_sb, ktok), (wv_sb, vtok)):
                for fc in range(2):
                    psp = ps_p0 if fc == 0 else ps_p1
                    def f_pmm(t, w_sb_=w_sb_, fc=fc, psp=psp, xt=xt,
                              xrdy=XT_RDY, pf=PS_RL_FREE):
                        t.wait_ge(ch_trev.sem, xrdy)
                        t.wait_ge(ch_zev.sem, pf)
                        for ft in range(8):
                            inst = t.matmul(
                                psp[:, :],
                                xt[:, ft * 128:(ft + 1) * 128],
                                w_sb_[:, ft * D + fc * 512: ft * D + (fc + 1) * 512],
                                start=(ft == 0), stop=(ft == 7))
                        ch_pj.inc(inst)
                    P("tensor", f_pmm)
                    ch_pj.bump()

                    def f_pev(v, dtok=dtok, buf=buf, fc=fc, psp=psp, pj=ch_pj.n,
                              vspn=(ch_vsp[buf].n if dtok is vtok else 0),
                              chvs=ch_vsp[buf]):
                        v.wait_ge(ch_pj.sem, pj)
                        if vspn:
                            v.wait_ge(chvs.sem, vspn)
                        inst = v.tensor_copy(
                            dtok[:, buf * D + fc * 512: buf * D + (fc + 1) * 512],
                            psp[:, :])
                        ch_zev.inc(inst)
                        ch_pjev.inc(v.drain())
                    P("vector", f_pev)
                    ch_zev.bump()
                    ch_pjev.bump()
                    PS_RL_FREE = ch_zev.n
            TOK_RDY = ch_pjev.n

            def f_vsp(s, n=n, tt=tt, buf=buf, chv=ch_vsp[buf], tok=TOK_RDY):
                s.wait_ge(ch_pjev.sem, tok)
                chv.inc(s.dma_start(
                    out=v_scr[n % 2, tt * 128:(tt + 1) * 128, :],
                    in_=vtok[:, buf * D:(buf + 1) * D]))
            P("sync", f_vsp)
            ch_vsp[buf].bump()

            for src_tok, dst_st in ((qtok, qst), (ktok, kst)):
                for hg in range(2):
                    def f_qtr(t, src_tok=src_tok, buf=buf, hg=hg, tok=TOK_RDY,
                              tev=ch_trev.n):
                        t.wait_ge(ch_pjev.sem, tok)
                        t.wait_ge(ch_trev.sem, tev)
                        for hh in range(8):
                            h = hg * 8 + hh
                            inst = t.transpose(
                                ps_tr[0:64, hh * 128:(hh + 1) * 128],
                                src_tok[:, buf * D + h * 64: buf * D + (h + 1) * 64],
                                ident_sb[:])
                        ch_tr.inc(inst)
                    P("tensor", f_qtr)
                    ch_tr.bump()

                    def f_qtrev(v, dst_st=dst_st, tt=tt, hg=hg, trc=ch_tr.n):
                        v.wait_ge(ch_tr.sem, trc)
                        out_ap = dst_st[0:64, :].rearrange(
                            "p (s h) -> p s h", h=16)[:, tt * 128:(tt + 1) * 128,
                                                      hg * 8:(hg + 1) * 8]
                        in_ap = ps_tr[0:64, :].rearrange(
                            "p (hh s) -> p s hh", hh=8)
                        inst = v.tensor_copy(out_ap, in_ap)
                        ch_trev.inc(inst)
                        ch_stk.inc(v.drain())
                    P("vector", f_qtrev)
                    ch_trev.bump()
                    ch_stk.bump()
        STK_RDY = ch_stk.n
        VSP_DONE = (ch_vsp[0].n, ch_vsp[1].n)

        for grp in range(NPK // 4):
            ggrp = n * 16 + grp
            gbuf = ggrp % 2

            def f_vld(s, n=n, grp=grp, gbuf=gbuf, chv=ch_vld[gbuf],
                      vsp=VSP_DONE, need=(nrm_hist[ggrp - 2] if ggrp >= 2 else 0)):
                s.wait_ge(ch_vsp[0].sem, vsp[0])
                s.wait_ge(ch_vsp[1].sem, vsp[1])
                if need:
                    s.wait_ge(ch_nrm.sem, need)
                row = vst[:].ap[0][0]
                dst = bass.AP(tensor=vst[:].tensor, offset=gbuf * 260,
                              ap=[[row, 128], [65, 4], [1, 64]])
                srcb = bass.AP(tensor=v_scr.tensor,
                               offset=(n % 2) * 512 * D + grp * 32 * D,
                               ap=[[64, 128], [8 * D, 4], [1, 64]])
                chv.inc(s.dma_start(out=dst, in_=srcb))
            P("sync", f_vld)
            ch_vld[gbuf].bump()
            VLD = ch_vld[gbuf].n

            for pk4 in range(4):
                pk = grp * 4 + pk4
                abank = (pk % 8) // 4
                aoff = (pk % 4) * 0
                acol = abank * 512 + pk4 * 128

                def f_amm(t, n=n, pk=pk, acol=acol, stk=STK_RDY,
                          ex=(exp_hist[ggrp - 2] if ggrp >= 2 else 0),
                          onr=ONES_RDY):
                    t.wait_ge(ch_stk.sem, stk)
                    t.wait_ge(ch_bob.sem, onr)
                    if ex:
                        t.wait_ge(ch_exp.sem, ex)
                    t.matmul(ps_a[:, acol:acol + 128], rneg_sb[:], rpos_sb[:],
                             start=True, stop=False)
                    t.matmul(ps_a[:, acol:acol + 128], bsel_sb[:], bsel_sb[:],
                             start=False, stop=False)
                    inst = t.matmul(ps_a[:, acol:acol + 128],
                                    kst[:, pk * 128:(pk + 1) * 128],
                                    qst[:, pk * 128:(pk + 1) * 128],
                                    start=False, stop=True)
                    if pk % 4 == 3:
                        ch_am.inc(inst)
                P("tensor", f_amm)
                if pk % 4 == 3:
                    ch_am.bump()

            def f_exp(sc, gbuf=gbuf, abank=((grp % 2)), am=ch_am.n):
                sc.wait_ge(ch_am.sem, am)
                ch_exp.inc(sc.activation(
                    pt_sb[:, gbuf * 512:(gbuf + 1) * 512],
                    ps_a[:, abank * 512:(abank + 1) * 512],
                    AF.Exp, scale=0.125))
            P("scalar", f_exp)
            ch_exp.bump()
            exp_hist.append(ch_exp.n)
            EXPG = ch_exp.n

            for pk4 in range(4):
                def f_pv(t, pk4=pk4, gbuf=gbuf, expg=EXPG, vld=VLD,
                         nrm=(nrm_hist[ggrp - 1] if ggrp >= 1 else 0)):
                    t.wait_ge(ch_exp.sem, expg)
                    t.wait_ge(ch_vld[gbuf].sem, vld)
                    if pk4 == 0 and nrm:
                        t.wait_ge(ch_nrm.sem, nrm)
                    inst = t.matmul(
                        ps_ot[0:65, pk4 * 128:(pk4 + 1) * 128],
                        vst[:, gbuf * 260 + pk4 * 65: gbuf * 260 + (pk4 + 1) * 65],
                        pt_sb[:, gbuf * 512 + pk4 * 128: gbuf * 512 + (pk4 + 1) * 128],
                        start=True, stop=True)
                    if pk4 == 3:
                        ch_pv.inc(inst)
                P("tensor", f_pv)
                if pk4 == 3:
                    ch_pv.bump()

            def f_recip(v, pv=ch_pv.n):
                v.wait_ge(ch_pv.sem, pv)
                ch_rl.inc(v.reciprocal(rl_sb[64:65, :], ps_ot[64:65, :]))
            P("vector", f_recip)
            ch_rl.bump()

            def f_rlb(t, rlc=ch_rl.n, pf=PS_RL_FREE):
                t.wait_ge(ch_rl.sem, rlc)
                t.wait_ge(ch_zev.sem, pf)
                ch_rlb.inc(t.matmul(ps_rl[0:64, :], ones64[64:65, :],
                                    rl_sb[64:65, :], start=True, stop=True,
                                    tile_position=(64, 0)))
            P("tensor", f_rlb)
            ch_rlb.bump()

            def f_rlbe(sc, rb=ch_rlb.n):
                sc.wait_ge(ch_rlb.sem, rb)
                inst = sc.copy(rlb_sb[:], ps_rl[0:64, :])
                ch_zev.inc(inst)
                ch_rlbe.inc(sc.drain())
            P("scalar", f_rlbe)
            ch_zev.bump()
            ch_rlbe.bump()
            PS_RL_FREE = ch_zev.n

            def f_norm(v, rbe=ch_rlbe.n, orw=ch_orw.n):
                v.wait_ge(ch_rlbe.sem, rbe)
                if orw:
                    v.wait_ge(ch_orw.sem, orw)
                ch_nrm.inc(v.tensor_mul(otn[0:64, :], ps_ot[0:64, :], rlb_sb[:]))
            P("vector", f_norm)
            ch_nrm.bump()
            nrm_hist.append(ch_nrm.n)

            def f_otr(t, nm=ch_nrm.n, tev=ch_trev.n):
                t.wait_ge(ch_nrm.sem, nm)
                t.wait_ge(ch_trev.sem, tev)
                for pk4 in range(4):
                    inst = t.transpose(
                        ps_tr[:, pk4 * 64:(pk4 + 1) * 64],
                        otn[0:64, pk4 * 128:(pk4 + 1) * 128],
                        ident_sb[0:64, 0:64])
                ch_tr.inc(inst)
            P("tensor", f_otr)
            ch_tr.bump()

            def f_orev(v, gbuf=gbuf, trc=ch_tr.n, chod=ch_ods[gbuf],
                       ods=ch_ods[gbuf].n):
                v.wait_ge(ch_tr.sem, trc)
                if ods:
                    v.wait_ge(chod.sem, ods)
                inst = v.tensor_copy(orow[:, gbuf * 256:(gbuf + 1) * 256],
                                     ps_tr[:, 0:256])
                ch_trev.inc(inst)
                ch_orw.inc(v.drain())
            P("vector", f_orev)
            ch_trev.bump()
            ch_orw.bump()

            def f_osp(s, n=n, grp=grp, gbuf=gbuf, cho=ch_ods[gbuf],
                      orw=ch_orw.n):
                s.wait_ge(ch_orw.sem, orw)
                cho.inc(s.dma_start(
                    out=od2[n % 2, grp * 512:(grp + 1) * 512, :].rearrange(
                        "(pk p) e -> p pk e", p=128),
                    in_=orow[:, gbuf * 256:(gbuf + 1) * 256].rearrange(
                        "p (pk e) -> p pk e", pk=4)))
            P("sync", f_osp)
            ch_ods[gbuf].bump()
        ODS_DONE = (ch_ods[0].n, ch_ods[1].n)

        if LVL >= 3:
            for rt in range(4):
                grt = n * 4 + rt
                zbuf = grt % 2
                zi = zin[:, zbuf * D:(zbuf + 1) * D]
                zt_ = zlT_t[:, zbuf * 1024:(zbuf + 1) * 1024]
                zbt = zb[:, zbuf * D:(zbuf + 1) * D]
                ot_ = x1t[:, zbuf * D:(zbuf + 1) * D]
                sl = slice(zbuf, zbuf + 1)
                chz = ch_zld[zbuf]

                def f_zld(s, n=n, rt=rt, zbuf=zbuf, chz=chz, ods=ODS_DONE,
                          need=ch_z.n):
                    s.wait_ge(ch_ods[0].sem, ods[0])
                    s.wait_ge(ch_ods[1].sem, ods[1])
                    if need:
                        s.wait_ge(ch_z.sem, need)
                    rowz = zin[:].ap[0][0]
                    for h1i in range(4):
                        src = bass.AP(
                            tensor=od2.tensor,
                            offset=(n % 2) * 8192 * 64 + rt * 4 * 64 + h1i * 64,
                            ap=[[16384, 32], [1024, 16], [1, 64]])
                        dstz = bass.AP(
                            tensor=zin[:].tensor,
                            offset=h1i * 32 * rowz + zbuf * D,
                            ap=[[rowz, 32], [64, 16], [1, 64]])
                        chz.inc(s.dma_start(out=dstz, in_=src))
                P("sync", f_zld)
                chz.bump(4)
                ZLD = chz.n

                def f_ztr(t, zi=zi, chz=chz, zld=ZLD, tev=ch_trev.n):
                    t.wait_ge(chz.sem, zld)
                    t.wait_ge(ch_trev.sem, tev)
                    for ft in range(8):
                        inst = t.transpose(ps_tr[:, ft * 128:(ft + 1) * 128],
                                           zi[:, ft * 128:(ft + 1) * 128],
                                           ident_sb[:])
                    ch_tr.inc(inst)
                P("tensor", f_ztr)
                ch_tr.bump()

                def f_ztrev(v, zt_=zt_, trc=ch_tr.n):
                    v.wait_ge(ch_tr.sem, trc)
                    ch_trev.inc(v.tensor_copy(zt_, ps_tr[:]))
                P("vector", f_ztrev)
                ch_trev.bump()
                ZT_RDY = ch_trev.n

                for fc in range(2):
                    psp = ps_p0 if fc == 0 else ps_p1
                    def f_zmm(t, zt_=zt_, fc=fc, psp=psp, zrdy=ZT_RDY,
                              pf=PS_RL_FREE, bob=BOB):
                        t.wait_ge(ch_trev.sem, zrdy)
                        t.wait_ge(ch_zev.sem, pf)
                        t.wait_ge(ch_bob.sem, bob)
                        for ft in range(8):
                            inst = t.matmul(
                                psp[:, :],
                                zt_[:, ft * 128:(ft + 1) * 128],
                                wo_sb[:, ft * D + fc * 512: ft * D + (fc + 1) * 512],
                                start=(ft == 0), stop=(ft == 7))
                        ch_z.inc(inst)
                    P("tensor", f_zmm)
                    ch_z.bump()

                    def f_zev(v, zbt=zbt, fc=fc, psp=psp, zc=ch_z.n):
                        v.wait_ge(ch_z.sem, zc)
                        inst = v.tensor_add(zbt[:, fc * 512:(fc + 1) * 512],
                                            psp[:, :],
                                            bo_bc[:, fc * 512:(fc + 1) * 512])
                        ch_zev.inc(inst)
                    P("vector", f_zev)
                    ch_zev.bump()
                    PS_RL_FREE = ch_zev.n

                ln_stats(P, ch_sum, ch_rst, zbt, sqj[:], sl, s1, s2, mean,
                         var, msq, rstd, epst[:], D)

                def f_xl(s, n=n, rt=rt, zbuf=zbuf, chx=ch_xd[zbuf],
                         need=(ch_vfree.n - 1)):
                    if need > 0:
                        s.wait_ge(ch_vfree.sem, need)
                    chx.inc(s.dma_start(
                        out=xin[:, zbuf * D:(zbuf + 1) * D],
                        in_=xr[n, rt * 128:(rt + 1) * 128, :]))
                P("sync", f_xl)
                ch_xd[zbuf].bump()
                XL = (ch_xd[zbuf], ch_xd[zbuf].n)

                def f_out(v, zbt=zbt, ot_=ot_, zbuf=zbuf, sl=sl, rs=ch_rst.n,
                          xd=XL, stneed=(ch_sto[zbuf], ch_sto[zbuf].n)):
                    v.wait_ge(ch_rst.sem, rs)
                    v.wait_ge(xd[0].sem, xd[1])
                    if stneed[1] > 0:
                        v.wait_ge(stneed[0].sem, stneed[1])
                    v.tensor_scalar(zbt, zbt, mean[:, sl], rstd[:, sl],
                                    AL.subtract, AL.mult)
                    v.drain()
                    ch_x1.inc(v.tensor_add(ot_, zbt,
                                           xin[:, zbuf * D:(zbuf + 1) * D]))
                    ch_vfree.inc(v.drain())
                P("vector", f_out)
                ch_x1.bump()
                ch_vfree.bump()

                def f_sto(s, n=n, rt=rt, ot_=ot_, chs=ch_sto[zbuf], xc=ch_x1.n):
                    s.wait_ge(ch_x1.sem, xc)
                    chs.inc(s.dma_start(
                        out=x1[n * 512 + rt * 128: n * 512 + (rt + 1) * 128, :],
                        in_=ot_))
                P("sync", f_sto)
                ch_sto[zbuf].bump()

    if LVL < 3:
        chd = ch_sto[0]
        nd = 3 if LVL >= 1 else 1
        chd.bump(nd)
        _dma_tots = [(ch.sem, ch.n) for ch in
                     (ch_vsp[0], ch_vsp[1], ch_vld[0], ch_vld[1],
                      ch_ods[0], ch_ods[1], ch_xd[0], ch_xd[1])]
        def f_dbg(s, lvl=LVL, chd=chd, tot=chd.n, stk=ch_stk.n,
                  tots=tuple(_dma_tots)):
            s.wait_ge(ch_stk.sem, stk)
            for sem_, v_ in tots:
                if v_:
                    s.wait_ge(sem_, v_)
            chd.inc(s.dma_start(out=dbg[0:64, 0:8192], in_=qst[:]))
            if lvl >= 1:
                chd.inc(s.dma_start(out=dbg[0:64, 8192:16384], in_=kst[:]))
                chd.inc(s.dma_start(
                    out=dbg[:, 16384:16640].rearrange("p (g e) -> p g e", g=4),
                    in_=od2[1, 0:512, :].rearrange("(g p) e -> p g e", p=128)))
            s.wait_ge(chd.sem, tot)
        P("sync", f_dbg)
        with nc.allow_low_precision("debug"):
            with nc.Block() as block:
                P.emit(block)
        for _ch in _chans:
            assert _ch.emitted == _ch.n, (
                f"channel {_ch.name}: emitted {_ch.emitted} != planned {_ch.n}")
        st.close()
        return nc

    def f_fin(s, tots=tuple((ch.sem, ch.n) for ch in ch_sto)):
        for sem, v_ in tots:
            s.wait_ge(sem, v_)
    P("sync", f_fin)

    with nc.allow_low_precision("bf16 matmul operands / attention weights; "
                                "fp32 accumulation and LN stats throughout"):
        with nc.Block() as block:
            P.emit(block)
    for _ch in _chans:
        assert _ch.emitted == _ch.n, (
            f"channel {_ch.name}: emitted {_ch.emitted} != planned {_ch.n}")
    st.close()
    return nc



def build_ffn_program(dbg=False):
    nc = bass.Bass("TRN2", target_bir_lowering=False, debug=False)
    xs = nc.dram_tensor("xs", [S, D], F32, kind="ExternalInput").ap()
    wd = {}
    for nm in ("wfT", "ufT", "wiT", "uiT", "wcT", "ucT", "woT", "uoT"):
        wd[nm] = nc.dram_tensor(nm, [D, D], F32, kind="ExternalInput").ap()
    bfv = nc.dram_tensor("bfv", [1, D], F32, kind="ExternalInput").ap()
    biv = nc.dram_tensor("biv", [1, D], F32, kind="ExternalInput").ap()
    bcv = nc.dram_tensor("bcv", [1, D], F32, kind="ExternalInput").ap()
    bov = nc.dram_tensor("bov", [1, D], F32, kind="ExternalInput").ap()
    incv = nc.dram_tensor("incv", [1, D], F32, kind="ExternalInput").ap()
    ident = nc.dram_tensor("ident", [128, 128], F32, kind="ExternalInput").ap()
    out = nc.dram_tensor("out", [S, D], F32, kind="ExternalOutput").ap()
    xt_scr = nc.dram_tensor("xt_scr", [D, S], BF16,
                            kind="ExternalOutput" if dbg else "Internal").ap()
    dbg_c = (nc.dram_tensor("dbg_c", [128, 4096 + 96], F32,
                            kind="ExternalOutput").ap() if dbg else None)
    dbg_h = (nc.dram_tensor("dbg_h", [128, 32 * D], BF16,
                            kind="ExternalOutput").ap() if dbg else None)

    st = ExitStack()
    sb = lambda name, shape, dt: st.enter_context(nc.sbuf_tensor(name, shape, dt))
    psa = lambda name, shape, dt: st.enter_context(nc.psum_tensor(name, shape, dt))

    w_sb = {}
    for nm in ("wf", "uf", "wi", "ui", "wc", "uc"):
        w_sb[nm] = sb(nm + "_sb", [128, 8 * D], BF16)
    xtT_sl = sb("xtT_sl", [128, 2 * 8 * 512], BF16)
    cT = sb("cT", [128, 8 * 512], F32)
    cT_bf = sb("cT_bf", [128, 8 * 512], BF16)
    gat = sb("gat", [128, 3 * 512], F32)
    xin = sb("xin", [128, 2 * D], F32)
    xln_sb = sb("xln_sb", [128, 2 * D], BF16)
    x1t = sb("x1t", [128, 2 * D], F32)
    sqj = sb("sqj", [128, D], BF16)
    biases = sb("biases", [128, 4 * 8], F32)
    ident_sb = sb("ident_sb", [128, 128], BF16)
    onesf = sb("onesf", [128, 512], F32)
    bo2_sb = sb("bo2_sb", [1, D], BF16)
    onesr2 = sb("onesr2", [1, 128], BF16)
    epst = sb("epst", [128, 1], F32)
    s1 = sb("s1", [128, 2], F32)
    s2 = sb("s2", [128, 2], F32)
    mean = sb("mean", [128, 2], F32)
    var = sb("var", [128, 2], F32)
    msq = sb("msq", [128, 2], F32)
    rstd = sb("rstd", [128, 2], F32)
    hmean = sb("hmean", [128, 32], F32)
    hvar = sb("hvar", [128, 32], F32)
    hrstd = sb("hrstd", [128, 32], F32)
    hms = sb("hms", [128, 32], F32)

    _h_order = ("wf", "uf", "wi", "ui")
    h_view = lambda rt: w_sb[_h_order[rt // 8]][:, (rt % 8) * D:((rt % 8) + 1) * D]
    wo_sl = w_sb["wc"]
    uo_sl = w_sb["uc"]
    xr_sl = gat[:].bitcast(BF16)

    ps_tr = psa("ps_tr", [128, 1024], BF16)
    ps_g = psa("ps_g", [128, 2 * 512], F32)
    ps_h = psa("ps_h", [128, 2 * 512], F32)

    _chans = []
    def c(name, step=1):
        ch = Chan(st, nc, name, step)
        _chans.append(ch)
        return ch
    ch_xd = (c("xd0", 16), c("xd1", 16))
    ch_wd = c("wd", 16)
    ch_cd = c("cd", 16)
    ch_sum = c("sum")
    ch_rst = c("rst")
    ch_xln = c("xln")
    ch_tr = c("tr")
    ch_trev = c("trev")
    ch_trd = (c("trd0", 16), c("trd1", 16))
    ch_xtl = (c("xtl0", 16), c("xtl1", 16))
    ch_xrl = (c("xrl0", 16), c("xrl1", 16))
    ch_g = c("g")
    ch_gev = c("gev")
    ch_cup = c("cup")
    ch_cbf = c("cbf")
    ch_h = c("h")
    ch_hev = c("hev")
    ch_hst = c("hst")
    ch_ho = c("ho")
    ch_sto = (c("sto0", 16), c("sto1", 16))
    ch_vfree = c("vfree")

    P = Plan()

    def g_consts(g):
        ch_cd.inc(g.dma_start(out=ident_sb[:], in_=ident[:]))
        ch_cd.inc(g.dma_start(out=bo2_sb[:], in_=bov[:]))
        for i, bsrc in enumerate((bfv, biv, bcv, incv)):
            ch_cd.inc(g.dma_start(
                out=biases[:, i * 8:(i + 1) * 8],
                in_=bsrc.rearrange("o (mt p) -> p (o mt)", p=128),
                allow_slow_non_contiguous=True))
        for nm in ("wf", "uf", "wi", "ui", "wc", "uc"):
            ch_cd.inc(g.dma_start(
                out=w_sb[nm][:].rearrange("p (ft m) -> p ft m", ft=8),
                in_=wd[nm + "T"].rearrange("(ft p) m -> p ft m", p=128)))
    P("gpsimd", g_consts)
    ch_cd.bump(12)
    CONSTS = ch_cd.n

    NT = S // 128
    tr_hist = []
    trd_hist = []
    for tt in range(NT):
        buf = tt % 2
        xi = xin[:, buf * D:(buf + 1) * D]
        xl = xln_sb[:, buf * D:(buf + 1) * D]
        sl = slice(buf, buf + 1)

        chx = ch_xd[buf]

        def f_load(s, tt=tt, xi=xi, chx=chx, need=ch_vfree.n - 1):
            if tt >= 2:
                s.wait_ge(ch_vfree.sem, need)
            chx.inc(s.dma_start(out=xi, in_=xs[tt * 128:(tt + 1) * 128, :]))
        P("sync", f_load)
        chx.bump()

        def f_pre(v, tt=tt, chx=chx, xd=chx.n):
            v.wait_ge(chx.sem, xd)
            if tt == 0:
                v.memset(epst[:], EPS)
                v.memset(onesf[:], 1.0)
                v.memset(onesr2[:], 1.0)
        P("vector", f_pre)
        ln_stats(P, ch_sum, ch_rst, xi, sqj[:], sl, s1, s2, mean, var, msq,
                 rstd, epst[:], D)

        def f_apply(v, xi=xi, xl=xl, sl=sl, rs=ch_rst.n,
                    trneed=(tr_hist[tt - 2] if tt >= 2 else 0),
                    spneed=(trd_hist[tt - 2] if tt >= 2 else None)):
            v.wait_ge(ch_rst.sem, rs)
            if trneed:
                v.wait_ge(ch_tr.sem, trneed)
            if spneed:
                v.wait_ge(spneed[0].sem, spneed[1])
            ch_xln.inc(v.tensor_scalar(xl, xi, mean[:, sl], rstd[:, sl],
                                       AL.subtract, AL.mult))
            ch_vfree.inc(v.drain())
        P("vector", f_apply)
        ch_xln.bump()
        ch_vfree.bump()

        def f_tr(t, tt=tt, xl=xl, xc=ch_xln.n, tev=ch_trev.n, cd=CONSTS):
            t.wait_ge(ch_xln.sem, xc)
            if tt == 0:
                t.wait_ge(ch_cd.sem, cd)
            else:
                t.wait_ge(ch_trev.sem, tev)
            for ft in range(8):
                inst = t.transpose(ps_tr[:, ft * 128:(ft + 1) * 128],
                                   xl[:, ft * 128:(ft + 1) * 128], ident_sb[:])
            ch_tr.inc(inst)
        P("tensor", f_tr)
        ch_tr.bump()
        tr_hist.append(ch_tr.n)

        def f_trev(v, tt=tt, buf=buf, trc=ch_tr.n):
            v.wait_ge(ch_tr.sem, trc)
            ch_trev.inc(v.tensor_copy(xln_sb[:, buf * D:(buf + 1) * D],
                                      ps_tr[:]))
        P("vector", f_trev)
        ch_trev.bump()

        cht = ch_trd[buf]

        def f_sp(s, tt=tt, buf=buf, cht=cht, tev=ch_trev.n):
            s.wait_ge(ch_trev.sem, tev)
            cht.inc(s.dma_start(
                out=xt_scr[:, tt * 128:(tt + 1) * 128].rearrange(
                    "(ft p) r -> p ft r", p=128),
                in_=xln_sb[:, buf * D:(buf + 1) * D].rearrange(
                    "p (ft r) -> p ft r", ft=8)))
        P("sync", f_sp)
        cht.bump()
        trd_hist.append((cht, cht.n))
    LN3 = tuple((ch.sem, ch.n) for ch in ch_trd)

    def f_cinit(v, cd=CONSTS):
        v.wait_ge(ch_cd.sem, cd)
        v.drain()
        for mt in range(8):
            inst = v.tensor_scalar(cT[:, mt * 512:(mt + 1) * 512], onesf[:],
                                   biases[:, 24 + mt:24 + mt + 1], None, AL.mult)
        ch_cup.inc(inst)
    P("vector", f_cinit)
    ch_cup.bump()

    GATES = (("wf", "uf", 0, AF.Sigmoid), ("wi", "ui", 8, AF.Sigmoid),
             ("wc", "uc", 16, AF.Tanh))
    for t_step in range(8):
        buf = t_step % 2

        chxt = ch_xtl[buf]

        def f_xtl(s, t_step=t_step, buf=buf, chxt=chxt, ln=LN3, need=ch_g.n):
            for sem, v_ in ln:
                s.wait_ge(sem, v_)
            if t_step >= 2:
                s.wait_ge(ch_g.sem, need)
            chxt.inc(s.dma_start(
                out=xtT_sl[:, buf * 4096:(buf + 1) * 4096].rearrange(
                    "p (ft q) -> p ft q", ft=8),
                in_=xt_scr[:, t_step * 512:(t_step + 1) * 512].rearrange(
                    "(ft p) q -> p ft q", p=128)))
        P("sync", f_xtl)
        chxt.bump()
        XTL = (chxt, chxt.n)

        def f_cbf(v, cu=ch_cup.n):
            v.wait_ge(ch_cup.sem, cu)
            ch_cbf.inc(v.tensor_copy(cT_bf[:], cT[:]))
        P("vector", f_cbf)
        ch_cbf.bump()
        CBF = ch_cbf.n

        for mt in range(8):
            for gi, (wn, un, bcol, func) in enumerate(GATES):
                kglob = (t_step * 8 + mt) * 3 + gi
                pg = kglob % 2

                def f_gmm(t, wn=wn, un=un, mt=mt, pg=pg, buf=buf, xtl=XTL,
                          cbf=CBF, gev=(ch_gev.n - 1)):
                    t.wait_ge(xtl[0].sem, xtl[1])
                    t.wait_ge(ch_cbf.sem, cbf)
                    if gev > 0:
                        t.wait_ge(ch_gev.sem, gev)
                    for ft in range(8):
                        t.matmul(ps_g[:, pg * 512:(pg + 1) * 512],
                                 w_sb[wn][:, ft * D + mt * 128: ft * D + mt * 128 + 128],
                                 xtT_sl[:, buf * 4096 + ft * 512:
                                        buf * 4096 + (ft + 1) * 512],
                                 start=(ft == 0), stop=False)
                    for m2 in range(8):
                        inst = t.matmul(
                            ps_g[:, pg * 512:(pg + 1) * 512],
                            w_sb[un][:, m2 * D + mt * 128: m2 * D + mt * 128 + 128],
                            cT_bf[:, m2 * 512:(m2 + 1) * 512],
                            start=False, stop=(m2 == 7))
                    ch_g.inc(inst)
                P("tensor", f_gmm)
                ch_g.bump()

                def f_gev(sc, gi=gi, pg=pg, func=func, bcol=bcol, mt=mt,
                          gc=ch_g.n, cup=(ch_cup.n if gi == 0 else 0)):
                    sc.wait_ge(ch_g.sem, gc)
                    if cup:
                        sc.wait_ge(ch_cup.sem, cup)
                    ch_gev.inc(sc.activation(
                        gat[:, gi * 512:(gi + 1) * 512],
                        ps_g[:, pg * 512:(pg + 1) * 512], func,
                        bias=biases[:, bcol + mt:bcol + mt + 1]))
                P("scalar", f_gev)
                ch_gev.bump()

            def f_cup(v, mt=mt, ge=ch_gev.n):
                v.wait_ge(ch_gev.sem, ge)
                cslice = cT[:, mt * 512:(mt + 1) * 512]
                v.tensor_mul(cslice, cslice, gat[:, 0:512])
                v.tensor_mul(gat[:, 512:1024], gat[:, 512:1024], gat[:, 1024:1536])
                v.drain()
                ch_cup.inc(v.tensor_add(cslice, cslice, gat[:, 512:1024]))
            P("vector", f_cup)
            ch_cup.bump()
    SCAN = ch_cup.n
    SCAN_G = ch_g.n

    def f_cfin(v, sc_=SCAN):
        v.wait_ge(ch_cup.sem, sc_)
        ch_cbf.inc(v.tensor_copy(cT_bf[:], cT[:]))
    P("vector", f_cfin)
    ch_cbf.bump()
    CFIN = ch_cbf.n

    for fc in range(2):
        def f_wol(g, fc=fc, sg=SCAN_G):
            g.wait_ge(ch_g.sem, sg)
            ch_wd.inc(g.dma_start(
                out=wo_sl[:, fc * 4096:(fc + 1) * 4096].rearrange(
                    "p (ft q) -> p ft q", ft=8),
                in_=wd["woT"][:, fc * 512:(fc + 1) * 512].rearrange(
                    "(ft p) q -> p ft q", p=128)))
            ch_wd.inc(g.dma_start(
                out=uo_sl[:, fc * 4096:(fc + 1) * 4096].rearrange(
                    "p (ft q) -> p ft q", ft=8),
                in_=wd["uoT"][:, fc * 512:(fc + 1) * 512].rearrange(
                    "(ft p) q -> p ft q", p=128)))
        P("gpsimd", f_wol)
        ch_wd.bump(2)
    WOL = ch_wd.n

    for rt in range(NT):
        buf = rt % 2
        bsl = rt % 4

        chxr = ch_xrl[buf]

        def f_xrl(g, rt=rt, buf=buf, chxr=chxr, sc_=SCAN, need=(ch_h.n - 1)):
            g.wait_ge(ch_cup.sem, sc_)
            if rt >= 2 and need > 0:
                g.wait_ge(ch_h.sem, need)
            chxr.inc(g.dma_start(
                out=xr_sl[:, buf * 1024:(buf + 1) * 1024].rearrange(
                    "p (ft r) -> p ft r", ft=8),
                in_=xt_scr[:, rt * 128:(rt + 1) * 128].rearrange(
                    "(ft p) r -> p ft r", p=128)))
        P("gpsimd", f_xrl)
        chxr.bump()
        XRL = (chxr, chxr.n)

        for fc in range(2):
            ph = (rt * 2 + fc) % 2

            def f_hmm(t, rt=rt, buf=buf, fc=fc, ph=ph, bsl=bsl, xrl=XRL,
                      cfin=CFIN, wol=WOL, hev=(ch_hev.n - 1)):
                t.wait_ge(xrl[0].sem, xrl[1])
                t.wait_ge(ch_cbf.sem, cfin)
                t.wait_ge(ch_wd.sem, wol)
                if hev > 0:
                    t.wait_ge(ch_hev.sem, hev)
                for ft in range(8):
                    t.matmul(ps_h[:, ph * 512:(ph + 1) * 512],
                             xr_sl[:, buf * 1024 + ft * 128:
                                   buf * 1024 + (ft + 1) * 128],
                             wo_sl[:, fc * 4096 + ft * 512:
                                   fc * 4096 + (ft + 1) * 512],
                             start=(ft == 0), stop=False)
                for m2 in range(8):
                    t.matmul(ps_h[:, ph * 512:(ph + 1) * 512],
                             cT_bf[:, m2 * 512 + bsl * 128:
                                   m2 * 512 + (bsl + 1) * 128],
                             uo_sl[:, fc * 4096 + m2 * 512:
                                   fc * 4096 + (m2 + 1) * 512],
                             start=False, stop=False)
                inst = t.matmul(ps_h[:, ph * 512:(ph + 1) * 512], onesr2[:],
                                bo2_sb[0:1, fc * 512:(fc + 1) * 512],
                                start=False, stop=True)
                ch_h.inc(inst)
            P("tensor", f_hmm)
            ch_h.bump()

            def f_hev(sc, rt=rt, fc=fc, ph=ph, hc=ch_h.n):
                sc.wait_ge(ch_h.sem, hc)
                ch_hev.inc(sc.activation(
                    h_view(rt)[:, fc * 512:(fc + 1) * 512],
                    ps_h[:, ph * 512:(ph + 1) * 512], AF.Tanh))
            P("scalar", f_hev)
            ch_hev.bump()

        def f_hstat(v, rt=rt, he=ch_hev.n):
            v.wait_ge(ch_hev.sem, he)
            v.drain()
            hrow = h_view(rt)
            v.reduce_sum(hmean[:, rt:rt + 1], hrow, axis=AX)
            ch_hst.inc(v.scalar_tensor_tensor(
                sqj[:], hrow, 1.0, hrow, AL.mult, AL.mult,
                accum_out=hvar[:, rt:rt + 1]))
        P("vector", f_hstat)
        ch_hst.bump()
    HSTAT = ch_hst.n

    def f_hpre(v, hs=HSTAT):
        v.wait_ge(ch_hst.sem, hs)
        v.scalar_tensor_tensor(hms[:], hmean[:], 1.0 / (D * D), hmean[:],
                               AL.mult, AL.mult)
        v.tensor_scalar_mul(hvar[:], hvar[:], 1.0 / D)
        v.tensor_scalar_mul(hmean[:], hmean[:], 1.0 / D)
        v.drain()
        ch_hst.inc(v.tensor_sub(hvar[:], hvar[:], hms[:]))
    P("vector", f_hpre)
    ch_hst.bump()

    def f_hrstd(sc, hs=ch_hst.n):
        sc.wait_ge(ch_hst.sem, hs)
        sc.activation(hms[:], hvar[:], AF.Ln, bias=epst[:])
        sc.drain()
        ch_rst.inc(sc.activation(hrstd[:], hms[:], AF.Exp, scale=-0.5))
    P("scalar", f_hrstd)
    ch_rst.bump()
    HR = ch_rst.n

    for rt in range(NT):
        buf = rt % 2
        xi = xin[:, buf * D:(buf + 1) * D]
        ot = x1t[:, buf * D:(buf + 1) * D]

        chx2 = ch_xd[buf]

        def f_xl2(s, rt=rt, xi=xi, chx2=chx2, need=(ch_ho.n - 1)):
            if rt >= 2 and need > 0:
                s.wait_ge(ch_ho.sem, need)
            chx2.inc(s.dma_start(out=xi, in_=xs[rt * 128:(rt + 1) * 128, :]))
        P("sync", f_xl2)
        chx2.bump()
        XD2 = (chx2, chx2.n)

        def f_happ(v, rt=rt, xi=xi, ot=ot, hr=HR, xd=XD2,
                   stneed=(ch_sto[rt % 2], ch_sto[rt % 2].n)):
            v.wait_ge(ch_rst.sem, hr)
            v.wait_ge(xd[0].sem, xd[1])
            if stneed[1] > 0:
                v.wait_ge(stneed[0].sem, stneed[1])
            v.tensor_scalar(ot, h_view(rt), hmean[:, rt:rt + 1],
                            hrstd[:, rt:rt + 1], AL.subtract, AL.mult)
            v.drain()
            ch_ho.inc(v.tensor_add(ot, ot, xi))
        P("vector", f_happ)
        ch_ho.bump()

        def f_sto(s, rt=rt, ot=ot, chs=ch_sto[rt % 2], hc=ch_ho.n):
            s.wait_ge(ch_ho.sem, hc)
            chs.inc(s.dma_start(out=out[rt * 128:(rt + 1) * 128, :], in_=ot))
        P("sync", f_sto)
        ch_sto[rt % 2].bump()

    if dbg:
        ch_sto[0].bump(3)
        def f_dump(s, tots=tuple((ch.sem, ch.n) for ch in ch_sto),
                   chd=ch_sto[0], tot=ch_sto[0].n):
            for sem, v_ in tots[1:]:
                s.wait_ge(sem, v_)
            s.wait_ge(tots[0][0], tots[0][1] - 48)
            chd.inc(s.dma_start(out=dbg_c[:, 0:4096], in_=cT[:]))
            chd.inc(s.dma_start(out=dbg_c[:, 4096:4128], in_=hmean[:]))
            chd.inc(s.dma_start(out=dbg_c[:, 4128:4160], in_=hrstd[:]))
            s.wait_ge(chd.sem, tot)
        P("sync", f_dump)

        ch_sto[1].bump(32)
        def f_dumph(s, chd=ch_sto[1], tot=ch_sto[1].n, base=ch_sto[1].n - 512):
            for rt in range(NT):
                chd.inc(s.dma_start(out=dbg_h[:, rt * D:(rt + 1) * D],
                                    in_=h_view(rt)))
            s.wait_ge(chd.sem, tot)
        P("sync", f_dumph)

    def f_fin(s, tots=tuple((ch.sem, ch.n) for ch in ch_sto)):
        for sem, v_ in tots:
            s.wait_ge(sem, v_)
    P("sync", f_fin)

    with nc.allow_low_precision("bf16 matmul operands / attention weights; "
                                "fp32 accumulation and LN stats throughout"):
        with nc.Block() as block:
            P.emit(block)
    for _ch in _chans:
        assert _ch.emitted == _ch.n, (
            f"channel {_ch.name}: emitted {_ch.emitted} != planned {_ch.n}")
    st.close()
    return nc


def build_fused_program():
    LVL = 3
    nc = bass.Bass("TRN2", target_bir_lowering=False, debug=False)
    x = nc.dram_tensor("x", [B, 512, D], F32, kind="ExternalInput").ap()
    wqT = nc.dram_tensor("wqT", [D, D], F32, kind="ExternalInput").ap()
    wkT = nc.dram_tensor("wkT", [D, D], F32, kind="ExternalInput").ap()
    wvT = nc.dram_tensor("wvT", [D, D], F32, kind="ExternalInput").ap()
    woT = nc.dram_tensor("woT", [D, D], F32, kind="ExternalInput").ap()
    bo = nc.dram_tensor("bo", [1, D], F32, kind="ExternalInput").ap()
    xr = nc.dram_tensor("xr", [B, 512, D], F32, kind="ExternalInput").ap()
    ident = nc.dram_tensor("ident", [128, 128], F32, kind="ExternalInput").ap()
    bsel = nc.dram_tensor("bsel", [8, 128], F32, kind="ExternalInput").ap()
    rowneg = nc.dram_tensor("rowneg", [1, 128], F32, kind="ExternalInput").ap()
    rowpos = nc.dram_tensor("rowpos", [1, 128], F32, kind="ExternalInput").ap()
    maskq = nc.dram_tensor("maskq", [9, 512 * 16], F32, kind="ExternalInput").ap()
    maskk = nc.dram_tensor("maskk", [9, 512 * 16], F32, kind="ExternalInput").ap()
    x1 = nc.dram_tensor("x1", [S, D], F32, kind="Internal").ap()
    out = nc.dram_tensor("out", [S, D], F32, kind="ExternalOutput").ap()
    wd = {}
    for nm in ("wfT", "ufT", "wiT", "uiT", "wcT", "ucT", "woT2", "uoT"):
        wd[nm] = nc.dram_tensor(nm, [D, D], F32, kind="ExternalInput").ap()
    bfv = nc.dram_tensor("bfv", [1, D], F32, kind="ExternalInput").ap()
    biv = nc.dram_tensor("biv", [1, D], F32, kind="ExternalInput").ap()
    bcv = nc.dram_tensor("bcv", [1, D], F32, kind="ExternalInput").ap()
    bov = nc.dram_tensor("bov", [1, D], F32, kind="ExternalInput").ap()
    incv = nc.dram_tensor("incv", [1, D], F32, kind="ExternalInput").ap()
    xt_scr = nc.dram_tensor("xt_scr", [D, S], BF16, kind="Internal").ap()
    v_scr = nc.dram_tensor("v_scr", [2, 512, D], BF16, kind="Internal").ap()
    od2 = nc.dram_tensor("od2", [2, 512 * 16, 64], BF16, kind="Internal").ap()

    st = ExitStack()
    sb = lambda name, shape, dt: st.enter_context(nc.sbuf_tensor(name, shape, dt))
    psa = lambda name, shape, dt: st.enter_context(nc.psum_tensor(name, shape, dt))

    wq_sb = sb("wq_sb", [128, 8 * D], BF16)
    wk_sb = sb("wk_sb", [128, 8 * D], BF16)
    wv_sb = sb("wv_sb", [128, 8 * D], BF16)
    wo_sb = sb("wo_sb", [128, 8 * D], BF16)
    xin = sb("xin", [128, 2 * D], F32)
    xln_sb = sb("xln_sb", [128, 2 * D], BF16)
    xlnT_t = sb("xlnT_t", [128, 2 * 1024], BF16)
    qtok = sb("qtok", [128, 2 * D], BF16)
    ktok = sb("ktok", [128, 2 * D], BF16)
    vtok = sb("vtok", [128, 2 * D], BF16)
    qst = sb("qst", [128, 512 * 16], BF16)
    kst = sb("kst", [128, 512 * 16], BF16)
    vst = sb("vst", [128, 4 * 4 * 65], BF16)
    pt_sb = sb("pt_sb", [128, 2 * 512], BF16)
    otn = sb("otn", [128, 2 * 512], BF16)
    orow = sb("orow", [128, 2 * 256], BF16)
    ones64 = sb("ones64", [128, 64], BF16)
    zin = sb("zin", [128, 2 * D], BF16)
    zlT_t = sb("zlT_t", [128, 2 * 1024], BF16)
    zb = sb("zb", [128, 2 * D], F32)
    x1t = sb("x1t", [128, 2 * D], F32)
    sqj = sb("sqj", [128, D], BF16)
    bo_bc = sb("bo_bc", [128, D], F32)
    bo_sb = sb("bo_sb", [128, D], BF16)
    ident_sb = sb("ident_sb", [128, 128], BF16)
    bsel_sb = sb("bsel_sb", [8, 128], BF16)
    rneg_sb = sb("rneg_sb", [1, 128], BF16)
    rpos_sb = sb("rpos_sb", [1, 128], BF16)
    epst = sb("epst", [128, 1], F32)
    s1 = sb("s1", [128, 2], F32)
    s2 = sb("s2", [128, 2], F32)
    mean = sb("mean", [128, 2], F32)
    var = sb("var", [128, 2], F32)
    msq = sb("msq", [128, 2], F32)
    rlc = sb("rlc", [128, 4], F32)
    rstd = sb("rstd", [128, 2], F32)

    gat = sb("gat", [128, 3 * 512], F32)
    junk2 = gat[:].bitcast(BF16)[:, 0:1024]

    ps_p0 = psa("ps_p0", [128, 512], F32)
    ps_p1 = psa("ps_p1", [128, 512], F32)
    ps_a = psa("ps_a", [128, 2 * 512], F32)
    ps_ot = psa("ps_ot", [128, 512], F32)
    ps_rl = psa("ps_rl", [128, 512], F32)
    ps_tr = psa("ps_tr", [128, 1024], BF16)
    ps_trb = psa("ps_trb", [128, 1024], BF16)

    _chans = []
    def c(name, step=1):
        ch = Chan(st, nc, name, step)
        _chans.append(ch)
        return ch
    ch_xd = (c("xd0", 16), c("xd1", 16))
    ch_cd = c("cd", 16)
    ch_sum = c("sum")
    ch_rst = c("rst")
    ch_xln = c("xln")
    ch_tr = c("tr")
    ch_trev = c("trev")
    ch_trB = c("trB")
    ch_trevB = c("trevB")
    ch_pj = c("pj")
    ch_pjev = c("pjev")
    ch_vsp = (c("vsp0", 16), c("vsp1", 16))
    ch_stk = c("stk")
    ch_am = c("am")
    ch_exp = c("exp")
    ch_vld = tuple(c(f"vld{i}", 16) for i in range(4))
    ch_pv = c("pv")
    ch_rl = c("rl")
    ch_rlb = c("rlb")
    ch_rlbe = c("rlbe")
    ch_nrm = c("nrm")
    ch_orw = c("orw")
    ch_ods = (c("ods0", 16), c("ods1", 16))
    ch_zld = (c("zld0", 16), c("zld1", 16))
    ch_z = c("z")
    ch_zev = c("zev")
    ch_bob = c("bob")
    ch_x1 = c("x1c")
    ch_sto = (c("sto0", 16), c("sto1", 16))
    ch_vfree = c("vfree")

    ps_trx = (ps_tr, ps_trb)
    ch_trx = (ch_tr, ch_trB)
    ch_trevx = (ch_trev, ch_trevB)
    _trs = [0]
    def tr_slot():
        b = _trs[0] % 2
        _trs[0] += 1
        return b

    P = Plan()

    def g_consts(g):
        for out_ap, in_ap in ((ident_sb[:], ident[:]), (bsel_sb[:], bsel[:]),
                              (rneg_sb[:], rowneg[:]), (rpos_sb[:], rowpos[:]),
                              (bo_sb[64:65, :], bo[:])):
            ch_cd.inc(g.dma_start(out=out_ap, in_=in_ap))
        ch_cd.inc(g.dma_start(out=qst[64:73, :], in_=maskq[:]))
        ch_cd.inc(g.dma_start(out=kst[64:73, :], in_=maskk[:]))
        for w_sb_, wdram in ((wq_sb, wqT), (wk_sb, wkT), (wv_sb, wvT),
                             (wo_sb, woT)):
            ch_cd.inc(g.dma_start(
                out=w_sb_[:].rearrange("p (ft m) -> p ft m", ft=8),
                in_=wdram.rearrange("(ft p) m -> p ft m", p=128)))
    P("gpsimd", g_consts)
    ch_cd.bump(11)
    CONSTS = ch_cd.n

    def f_ones_init(v, cd=CONSTS):
        v.wait_ge(ch_cd.sem, cd)
        v.memset(epst[:], EPS)
        v.memset(vst[:].rearrange("p (b e) -> p b e", b=16)[:, :, 64:65], 1.0)
        ch_bob.inc(v.memset(ones64[64:65, :], 1.0))
    P("vector", f_ones_init)
    ch_bob.bump()
    ONES_RDY = ch_bob.n

    for half in (0, 1):
        def f_bob_mm(t, half=half, onr=ONES_RDY, ze=ch_zev.n):
            t.wait_ge(ch_bob.sem, onr)
            if ze:
                t.wait_ge(ch_zev.sem, ze)
            ch_z.inc(t.matmul(ps_rl[0:64, :], ones64[64:65, :],
                              bo_sb[64:65, half * 512:(half + 1) * 512],
                              start=True, stop=True, tile_position=(64, 0)))
        P("tensor", f_bob_mm)
        ch_z.bump()

        def f_bob_ev(sc, half=half, zc=ch_z.n):
            sc.wait_ge(ch_z.sem, zc)
            inst = sc.copy(bo_bc[0:64, half * 512:(half + 1) * 512], ps_rl[0:64, :])
            ch_zev.inc(inst)
        P("scalar", f_bob_ev)
        ch_zev.bump()
    for half in (0, 1):
        def f_bob_mm2(t, half=half, ze=ch_zev.n):
            t.wait_ge(ch_zev.sem, ze)
            ch_z.inc(t.matmul(ps_rl[64:128, :], ones64[64:65, 0:64],
                              bo_sb[64:65, half * 512:(half + 1) * 512],
                              start=True, stop=True, tile_position=(64, 64)))
        P("tensor", f_bob_mm2)
        ch_z.bump()

        def f_bob_ev2(sc, half=half, zc=ch_z.n):
            sc.wait_ge(ch_z.sem, zc)
            inst = sc.copy(bo_bc[64:128, half * 512:(half + 1) * 512],
                           ps_rl[64:128, :])
            ch_zev.inc(inst)
            ch_bob.inc(sc.drain())
        P("scalar", f_bob_ev2)
        ch_zev.bump()
        ch_bob.bump()
    BOB = ch_bob.n
    PS_RL_FREE = ch_zev.n
    BOBZEV = ch_zev.n
    ps_otx = (ps_ot, ps_rl)
    psfree = {0: 0, 1: 0}

    NTIL = 4
    NPK = 64
    exp_hist = []
    nrm_hist = []
    PJ_EV = {}

    def emit_P2(n, rt, ODS_DONE):
        nonlocal PS_RL_FREE
        grt = n * 4 + rt
        zbuf = grt % 2
        zi = zin[:, zbuf * D:(zbuf + 1) * D]
        zt_ = zlT_t[:, zbuf * 1024:(zbuf + 1) * 1024]
        zbt = zb[:, zbuf * D:(zbuf + 1) * D]
        ot_ = x1t[:, zbuf * D:(zbuf + 1) * D]
        sl = slice(zbuf, zbuf + 1)
        chz = ch_zld[zbuf]

        def f_zld(s, n=n, rt=rt, zbuf=zbuf, chz=chz, ods=ODS_DONE,
                  need=(p2_ztr_hist[-2] if len(p2_ztr_hist) >= 2 else None)):
            s.wait_ge(ch_ods[0].sem, ods[0])
            s.wait_ge(ch_ods[1].sem, ods[1])
            if need:
                s.wait_ge(need[0].sem, need[1])
            rowz = zin[:].ap[0][0]
            for h1i in range(4):
                src = bass.AP(
                    tensor=od2.tensor,
                    offset=(n % 2) * 8192 * 64 + rt * 4 * 64 + h1i * 64,
                    ap=[[16384, 32], [1024, 16], [1, 64]])
                dstz = bass.AP(
                    tensor=zin[:].tensor,
                    offset=h1i * 32 * rowz + zbuf * D,
                    ap=[[rowz, 32], [64, 16], [1, 64]])
                chz.inc(s.dma_start(out=dstz, in_=src))
        P("gpsimd", f_zld)
        chz.bump(4)
        ZLD = chz.n

        trb = tr_slot()
        def f_ztr(t, zi=zi, trb=trb, chz=chz, zld=ZLD,
                  tev=ch_trevx[trb].n):
            t.wait_ge(chz.sem, zld)
            t.wait_ge(ch_trevx[trb].sem, tev)
            for ft in range(8):
                inst = t.transpose(
                    ps_trx[trb][:, ft * 128:(ft + 1) * 128],
                    zi[:, ft * 128:(ft + 1) * 128],
                    ident_sb[:])
            ch_trx[trb].inc(inst)
        P("tensor", f_ztr)
        ch_trx[trb].bump()
        p2_ztr_hist.append((ch_trx[trb], ch_trx[trb].n))

        def f_ztrev(v, zt_=zt_, trb=trb, trc=ch_trx[trb].n):
            v.wait_ge(ch_trx[trb].sem, trc)
            ch_trevx[trb].inc(v.tensor_copy(zt_, ps_trx[trb][:]))
        P("vector", f_ztrev)
        ch_trevx[trb].bump()
        ZT_RDY = (ch_trevx[trb], ch_trevx[trb].n)

        for fc in range(2):
            psp = ps_p0 if fc == 0 else ps_p1
            def f_zmm(t, zt_=zt_, fc=fc, psp=psp, zrdy=ZT_RDY,
                      pf=psfree[fc], bob=BOB):
                t.wait_ge(zrdy[0].sem, zrdy[1])
                if pf:
                    t.wait_ge(ch_zev.sem, pf)
                t.wait_ge(ch_bob.sem, bob)
                for ft in range(8):
                    inst = t.matmul(
                        psp[:, :],
                        zt_[:, ft * 128:(ft + 1) * 128],
                        wo_sb[:, ft * D + fc * 512: ft * D + (fc + 1) * 512],
                        start=(ft == 0), stop=(ft == 7))
                ch_z.inc(inst)
            P("tensor", f_zmm)
            ch_z.bump()

            def f_zev(v, zbt=zbt, fc=fc, psp=psp, zc=ch_z.n):
                v.wait_ge(ch_z.sem, zc)
                inst = v.tensor_add(zbt[:, fc * 512:(fc + 1) * 512],
                                    psp[:, :],
                                    bo_bc[:, fc * 512:(fc + 1) * 512])
                ch_zev.inc(inst)
            P("vector", f_zev)
            ch_zev.bump()
            psfree[fc] = ch_zev.n

        ln_stats_sc(P, ch_rst, zbt, sqj[:], junk2, sl, s1, s2, mean,
                    msq, rstd, D, wait=(ch_zev, ch_zev.n))

        def f_xl(s, n=n, rt=rt, zbuf=zbuf, chx=ch_xd[zbuf],
                 need=xin_last[zbuf]):
            if need > 0:
                s.wait_ge(ch_vfree.sem, need)
            chx.inc(s.dma_start(
                out=xin[:, zbuf * D:(zbuf + 1) * D],
                in_=xr[n, rt * 128:(rt + 1) * 128, :]))
        P("gpsimd", f_xl)
        ch_xd[zbuf].bump()
        XL = (ch_xd[zbuf], ch_xd[zbuf].n)

        def f_out(v, zbt=zbt, ot_=ot_, zbuf=zbuf, sl=sl, rs=ch_rst.n,
                  xd=XL, stneed=(ch_sto[zbuf], ch_sto[zbuf].n)):
            v.wait_ge(ch_rst.sem, rs)
            v.wait_ge(xd[0].sem, xd[1])
            if stneed[1] > 0:
                v.wait_ge(stneed[0].sem, stneed[1])
            v.tensor_scalar(zbt, zbt, mean[:, sl], rstd[:, sl],
                            AL.subtract, AL.mult)
            v.drain()
            ch_x1.inc(v.tensor_add(ot_, zbt,
                                   xin[:, zbuf * D:(zbuf + 1) * D]))
            ch_vfree.inc(v.drain())
        P("vector", f_out)
        ch_x1.bump()
        ch_vfree.bump()
        xin_last[zbuf] = ch_vfree.n

        def f_sto(s, n=n, rt=rt, ot_=ot_, chs=ch_sto[zbuf], xc=ch_x1.n):
            s.wait_ge(ch_x1.sem, xc)
            chs.inc(s.dma_start(
                out=x1[n * 512 + rt * 128: n * 512 + (rt + 1) * 128, :],
                in_=ot_))
        P("gpsimd", f_sto)
        ch_sto[zbuf].bump()
        if rt == 3:
            p2_zld_hist[n] = (ch_zld[0].n, ch_zld[1].n)
            sto_hist[n] = ((ch_sto[0], ch_sto[0].n), (ch_sto[1], ch_sto[1].n))


    g_vld_hist = {}
    p2_zld_hist = {}
    p2_ztr_hist = []
    orw_hist = []
    tok_hist = {}
    xin_last = {0: 0, 1: 0}
    stk_vsp = {}
    ODS_hist = {}
    sto_hist = {}
    xln_hist = {0: None, 1: None}
    ch_trd = (c("trd0", 16), c("trd1", 16))

    def emit_LN3(tt):
        buf = tt % 2
        xi = xin[:, buf * D:(buf + 1) * D]
        xl = xln_sb[:, buf * D:(buf + 1) * D]
        sl = slice(buf, buf + 1)
        chx = ch_xd[buf]

        def f_l3(s, tt=tt, xi=xi, chx=chx, need=xin_last[buf],
                 stw=sto_hist[tt // 4]):
            if need:
                s.wait_ge(ch_vfree.sem, need)
            s.wait_ge(stw[0][0].sem, stw[0][1])
            s.wait_ge(stw[1][0].sem, stw[1][1])
            chx.inc(s.dma_start(out=xi, in_=x1[tt * 128:(tt + 1) * 128, :]))
        P("gpsimd", f_l3)
        chx.bump()

        ln_stats_sc(P, ch_rst, xi, sqj[:], sqj[:], sl, s1, s2, mean,
                    msq, rstd, D, wait=(chx, chx.n), dr=True)

        def f_ap3(v, xi=xi, xl=xl, sl=sl, rs=ch_rst.n, xh=xln_hist[buf]):
            v.wait_ge(ch_rst.sem, rs)
            if xh:
                for c_, n_ in xh:
                    v.wait_ge(c_.sem, n_)
            ch_xln.inc(v.tensor_scalar(xl, xi, mean[:, sl], rstd[:, sl],
                                       AL.subtract, AL.mult))
            ch_vfree.inc(v.drain())
        P("vector", f_ap3)
        ch_xln.bump()
        ch_vfree.bump()
        xin_last[buf] = ch_vfree.n

        trb = tr_slot()
        def f_t3(t, xl=xl, trb=trb, xc=ch_xln.n, tev=ch_trevx[trb].n):
            t.wait_ge(ch_xln.sem, xc)
            t.wait_ge(ch_trevx[trb].sem, tev)
            for ft in range(8):
                inst = t.transpose(ps_trx[trb][:, ft * 128:(ft + 1) * 128],
                                   xl[:, ft * 128:(ft + 1) * 128], ident_sb[:])
            ch_trx[trb].inc(inst)
        P("tensor", f_t3)
        ch_trx[trb].bump()

        def f_te3(v, buf=buf, trb=trb, trc=ch_trx[trb].n):
            v.wait_ge(ch_trx[trb].sem, trc)
            ch_trevx[trb].inc(v.tensor_copy(xln_sb[:, buf * D:(buf + 1) * D],
                                            ps_trx[trb][:]))
        P("vector", f_te3)
        ch_trevx[trb].bump()
        TRE = (ch_trevx[trb], ch_trevx[trb].n)

        def f_s3(s, tt=tt, buf=buf, cht=ch_trd[buf], tev=TRE):
            s.wait_ge(tev[0].sem, tev[1])
            cht.inc(s.dma_start(
                out=xt_scr[:, tt * 128:(tt + 1) * 128].rearrange(
                    "(ft p) r -> p ft r", p=128),
                in_=xln_sb[:, buf * D:(buf + 1) * D].rearrange(
                    "p (ft r) -> p ft r", ft=8)))
        P("sync", f_s3)
        ch_trd[buf].bump()
        xln_hist[buf] = ((ch_trd[buf], ch_trd[buf].n),)

    def emit_P1_front(n, tt):
        gti = n * NTIL + tt
        buf = gti % 2
        xi = xin[:, buf * D:(buf + 1) * D]
        xl = xln_sb[:, buf * D:(buf + 1) * D]
        xt = xlnT_t[:, buf * 1024:(buf + 1) * 1024]
        sl = slice(buf, buf + 1)
        chx = ch_xd[buf]

        def f_load(s, n=n, tt=tt, xi=xi, chx=chx,
                   need=xin_last[buf]):
            if need:
                s.wait_ge(ch_vfree.sem, need)
            chx.inc(s.dma_start(
                out=xi, in_=x[n, tt * 128:(tt + 1) * 128, :]))
        P("gpsimd", f_load)
        chx.bump()

        ln_stats_sc(P, ch_rst, xi, sqj[:], junk2, sl, s1, s2, mean,
                    msq, rstd, D, wait=(chx, chx.n))

        def f_apply(v, xi=xi, xl=xl, sl=sl, rs=ch_rst.n,
                    xh=xln_hist[buf]):
            v.wait_ge(ch_rst.sem, rs)
            if xh:
                for c_, n_ in xh:
                    v.wait_ge(c_.sem, n_)
            ch_xln.inc(v.tensor_scalar(xl, xi, mean[:, sl], rstd[:, sl],
                                       AL.subtract, AL.mult))
            ch_vfree.inc(v.drain())
        P("vector", f_apply)
        ch_xln.bump()
        ch_vfree.bump()
        xin_last[buf] = ch_vfree.n

        trb = tr_slot()
        def f_tr(t, xl=xl, trb=trb, xc=ch_xln.n, tev=ch_trevx[trb].n,
                 cd=CONSTS):
            t.wait_ge(ch_xln.sem, xc)
            t.wait_ge(ch_cd.sem, cd)
            if tev:
                t.wait_ge(ch_trevx[trb].sem, tev)
            for ft in range(8):
                inst = t.transpose(ps_trx[trb][:, ft * 128:(ft + 1) * 128],
                                   xl[:, ft * 128:(ft + 1) * 128],
                                   ident_sb[:])
            ch_trx[trb].inc(inst)
        P("tensor", f_tr)
        ch_trx[trb].bump()
        xln_hist[buf] = ((ch_trx[trb], ch_trx[trb].n),)

        def f_trev(v, xt=xt, trb=trb, trc=ch_trx[trb].n):
            v.wait_ge(ch_trx[trb].sem, trc)
            ch_trevx[trb].inc(v.tensor_copy(xt, ps_trx[trb][:]))
        P("vector", f_trev)
        ch_trevx[trb].bump()
        XT_RDY = (ch_trevx[trb], ch_trevx[trb].n)

        for w_sb_, dtok in ((wq_sb, qtok), (wk_sb, ktok), (wv_sb, vtok)):
            for fc in range(2):
                psp = ps_p0 if fc == 0 else ps_p1
                def f_pmm(t, w_sb_=w_sb_, fc=fc, psp=psp, xt=xt,
                          xrdy=XT_RDY, pf=psfree[fc]):
                    t.wait_ge(xrdy[0].sem, xrdy[1])
                    if pf:
                        t.wait_ge(ch_zev.sem, pf)
                    for ft in range(8):
                        inst = t.matmul(
                            psp[:, :],
                            xt[:, ft * 128:(ft + 1) * 128],
                            w_sb_[:, ft * D + fc * 512: ft * D + (fc + 1) * 512],
                            start=(ft == 0), stop=(ft == 7))
                    ch_pj.inc(inst)
                P("tensor", f_pmm)
                ch_pj.bump()

                def f_pev(v, dtok=dtok, buf=buf, fc=fc, psp=psp, pj=ch_pj.n,
                          vspn=(ch_vsp[buf].n if dtok is vtok else 0),
                          chvs=ch_vsp[buf]):
                    v.wait_ge(ch_pj.sem, pj)
                    if vspn:
                        v.wait_ge(chvs.sem, vspn)
                    inst = v.tensor_copy(
                        dtok[:, buf * D + fc * 512: buf * D + (fc + 1) * 512],
                        psp[:, :])
                    ch_zev.inc(inst)
                    ch_pjev.inc(v.drain())
                P("vector", f_pev)
                ch_zev.bump()
                ch_pjev.bump()
                psfree[fc] = ch_zev.n
        TOK_RDY = ch_pjev.n

        def f_vsp(s, n=n, tt=tt, buf=buf, chv=ch_vsp[buf], tok=TOK_RDY,
                  vldw=(g_vld_hist.get(n - 2) if n >= 2 else None)):
            s.wait_ge(ch_pjev.sem, tok)
            if vldw:
                for _i in range(4):
                    s.wait_ge(ch_vld[_i].sem, vldw[_i])
            chv.inc(s.dma_start(
                out=v_scr[n % 2, tt * 128:(tt + 1) * 128, :],
                in_=vtok[:, buf * D:(buf + 1) * D]))
        P("sync", f_vsp)
        ch_vsp[buf].bump()
        tok_hist[(n, tt)] = TOK_RDY

    def emit_P1_qtr(n, tt):
        gti = n * NTIL + tt
        buf = gti % 2
        TOK_RDY = tok_hist[(n, tt)]

        for src_tok, dst_st in ((qtok, qst), (ktok, kst)):
            for hg in range(2):
                trb = tr_slot()
                def f_qtr(t, src_tok=src_tok, buf=buf, hg=hg, trb=trb,
                          tok=TOK_RDY, tev=ch_trevx[trb].n):
                    t.wait_ge(ch_pjev.sem, tok)
                    t.wait_ge(ch_trevx[trb].sem, tev)
                    for hh in range(8):
                        h = hg * 8 + hh
                        inst = t.transpose(
                            ps_trx[trb][0:64, hh * 128:(hh + 1) * 128],
                            src_tok[:, buf * D + h * 64: buf * D + (h + 1) * 64],
                            ident_sb[:])
                    ch_trx[trb].inc(inst)
                P("tensor", f_qtr)
                ch_trx[trb].bump()

                def f_qtrev(v, dst_st=dst_st, tt=tt, hg=hg, trb=trb,
                            trc=ch_trx[trb].n):
                    v.wait_ge(ch_trx[trb].sem, trc)
                    out_ap = dst_st[0:64, :].rearrange(
                        "p (s h) -> p s h", h=16)[:, tt * 128:(tt + 1) * 128,
                                                  hg * 8:(hg + 1) * 8]
                    in_ap = ps_trx[trb][0:64, :].rearrange(
                        "p (hh s) -> p s hh", hh=8)
                    inst = v.tensor_copy(out_ap, in_ap)
                    ch_trevx[trb].inc(inst)
                    ch_stk.inc(v.drain())
                P("vector", f_qtrev)
                ch_trevx[trb].bump()
                ch_stk.bump()

    for tt in range(NTIL):
        emit_P1_front(0, tt)
        emit_P1_qtr(0, tt)
    stk_vsp[0] = (ch_stk.n, (ch_vsp[0].n, ch_vsp[1].n))

    for n in range(B):
        STK_RDY, VSP_DONE = stk_vsp[n]
        for grp in range(NPK // 4):
            ggrp = n * 16 + grp
            gbuf = ggrp % 2
            obuf = ggrp % 2

            vslot = ggrp % 4
            def f_vld(s, n=n, grp=grp, vslot=vslot, chv=ch_vld[vslot],
                      vsp=VSP_DONE, need=(nrm_hist[ggrp - 4] if ggrp >= 4 else 0)):
                s.wait_ge(ch_vsp[0].sem, vsp[0])
                s.wait_ge(ch_vsp[1].sem, vsp[1])
                if need:
                    s.wait_ge(ch_nrm.sem, need)
                row = vst[:].ap[0][0]
                dst = bass.AP(tensor=vst[:].tensor, offset=vslot * 260,
                              ap=[[row, 128], [65, 4], [1, 64]])
                srcb = bass.AP(tensor=v_scr.tensor,
                               offset=(n % 2) * 512 * D + grp * 32 * D,
                               ap=[[64, 128], [8 * D, 4], [1, 64]])
                chv.inc(s.dma_start(out=dst, in_=srcb))
            P("gpsimd", f_vld)
            ch_vld[vslot].bump()
            VLD = ch_vld[vslot].n

            for pk4 in range(4):
                pk = grp * 4 + pk4
                abank = (pk % 8) // 4
                aoff = (pk % 4) * 0
                acol = abank * 512 + pk4 * 128

                def f_amm(t, n=n, pk=pk, acol=acol, stk=STK_RDY,
                          ex=(exp_hist[ggrp - 2] if ggrp >= 2 else 0)):
                    t.wait_ge(ch_stk.sem, stk)
                    if ex:
                        t.wait_ge(ch_exp.sem, ex)
                    inst = t.matmul(ps_a[:, acol:acol + 128],
                                    kst[0:73, pk * 128:(pk + 1) * 128],
                                    qst[0:73, pk * 128:(pk + 1) * 128],
                                    start=True, stop=True)
                    if pk % 4 == 3:
                        ch_am.inc(inst)
                P("tensor", f_amm)
                if pk % 4 == 3:
                    ch_am.bump()

            def f_exp(sc, gbuf=gbuf, abank=((grp % 2)), am=ch_am.n):
                sc.wait_ge(ch_am.sem, am)
                ch_exp.inc(sc.activation(
                    pt_sb[:, gbuf * 512:(gbuf + 1) * 512],
                    ps_a[:, abank * 512:(abank + 1) * 512],
                    AF.Exp, scale=0.125))
            P("scalar", f_exp)
            ch_exp.bump()
            exp_hist.append(ch_exp.n)
            EXPG = ch_exp.n

            psot = ps_otx[obuf]
            for pk4 in range(4):
                def f_pv(t, pk4=pk4, gbuf=gbuf, vslot=vslot, psot=psot,
                         expg=EXPG, vld=VLD,
                         nrm=(nrm_hist[ggrp - 2] if ggrp >= 2 else 0),
                         bz=(BOBZEV if ggrp == 1 else 0)):
                    t.wait_ge(ch_exp.sem, expg)
                    t.wait_ge(ch_vld[vslot].sem, vld)
                    if pk4 == 0 and nrm:
                        t.wait_ge(ch_nrm.sem, nrm)
                    if pk4 == 0 and bz:
                        t.wait_ge(ch_zev.sem, bz)
                    inst = t.matmul(
                        psot[0:65, pk4 * 128:(pk4 + 1) * 128],
                        vst[:, vslot * 260 + pk4 * 65: vslot * 260 + (pk4 + 1) * 65],
                        pt_sb[:, gbuf * 512 + pk4 * 128: gbuf * 512 + (pk4 + 1) * 128],
                        start=True, stop=True)
                    if pk4 == 3:
                        ch_pv.inc(inst)
                P("tensor", f_pv)
                if pk4 == 3:
                    ch_pv.bump()

            def f_oev(v, psot=psot, obuf=obuf, pv=ch_pv.n,
                      orw=(orw_hist[ggrp - 2] if ggrp >= 2 else 0)):
                v.wait_ge(ch_pv.sem, pv)
                if orw:
                    v.wait_ge(ch_orw.sem, orw)
                ch_nrm.inc(v.tensor_copy(otn[0:65, obuf * 512:(obuf + 1) * 512],
                                         psot[0:65, :]))
            P("vector", f_oev)
            ch_nrm.bump()
            nrm_hist.append(ch_nrm.n)

            trb = tr_slot()
            def f_otr(t, trb=trb, obuf=obuf, nm=ch_nrm.n,
                      tev=ch_trevx[trb].n):
                t.wait_ge(ch_nrm.sem, nm)
                t.wait_ge(ch_trevx[trb].sem, tev)
                for pk4 in range(4):
                    inst = t.transpose(
                        ps_trx[trb][:, pk4 * 66:pk4 * 66 + 65],
                        otn[0:65, obuf * 512 + pk4 * 128:
                            obuf * 512 + (pk4 + 1) * 128],
                        ident_sb[0:65, 0:65])
                ch_trx[trb].inc(inst)
            P("tensor", f_otr)
            ch_trx[trb].bump()

            def f_onrm(v, gbuf=gbuf, trb=trb, trc=ch_trx[trb].n,
                       chod=ch_ods[gbuf], ods=ch_ods[gbuf].n):
                v.wait_ge(ch_trx[trb].sem, trc)
                if ods:
                    v.wait_ge(chod.sem, ods)
                pst = ps_trx[trb]
                row = pst[:].ap[0][0]
                lcols = bass.AP(tensor=pst[:].tensor, offset=64,
                                ap=[[row, 128], [66, 4]])
                v.reciprocal(rlc[:], lcols)
                v.drain()
                for pk4 in range(4):
                    inst = v.tensor_scalar(
                        orow[:, gbuf * 256 + pk4 * 64:
                             gbuf * 256 + (pk4 + 1) * 64],
                        pst[:, pk4 * 66:pk4 * 66 + 64],
                        rlc[:, pk4:pk4 + 1], None, AL.mult)
                ch_trevx[trb].inc(inst)
                ch_orw.inc(v.drain())
            P("vector", f_onrm)
            ch_trevx[trb].bump()
            ch_orw.bump()
            orw_hist.append(ch_orw.n)

            def f_osp(s, n=n, grp=grp, gbuf=gbuf, cho=ch_ods[gbuf],
                      orw=ch_orw.n,
                      zldw=(p2_zld_hist.get(n - 2) if n >= 2 else None)):
                s.wait_ge(ch_orw.sem, orw)
                if zldw:
                    s.wait_ge(ch_zld[0].sem, zldw[0])
                    s.wait_ge(ch_zld[1].sem, zldw[1])
                cho.inc(s.dma_start(
                    out=od2[n % 2, grp * 512:(grp + 1) * 512, :].rearrange(
                        "(pk p) e -> p pk e", p=128),
                    in_=orow[:, gbuf * 256:(gbuf + 1) * 256].rearrange(
                        "p (pk e) -> p pk e", pk=4)))
            P("sync", f_osp)
            ch_ods[gbuf].bump()
            if grp % 4 == 1 and n + 1 < B:
                emit_P1_front(n + 1, grp // 4)
            if grp % 4 == 2 and n >= 2:
                emit_LN3(4 * (n - 2) + grp // 4)
            if grp % 4 == 2 and grp >= 6 and n + 1 < B:
                emit_P1_qtr(n + 1, grp // 4 - 1)
            if grp % 4 == 3 and n >= 1:
                emit_P2(n - 1, grp // 4, ODS_hist[n - 1])
        ODS_hist[n] = (ch_ods[0].n, ch_ods[1].n)
        g_vld_hist[n] = tuple(_c.n for _c in ch_vld)
        if n + 1 < B:
            emit_P1_qtr(n + 1, 3)
            stk_vsp[n + 1] = (ch_stk.n, (ch_vsp[0].n, ch_vsp[1].n))

    emit_P2(B - 1, 0, ODS_hist[B - 1])
    emit_LN3(24)
    emit_P2(B - 1, 1, ODS_hist[B - 1])
    emit_LN3(25)
    emit_P2(B - 1, 2, ODS_hist[B - 1])
    emit_LN3(26)
    emit_P2(B - 1, 3, ODS_hist[B - 1])
    for tt in range(27, 32):
        emit_LN3(tt)

    A_PJ = ch_pj.n
    A_AM = ch_am.n
    A_Z = ch_z.n
    A_STO = (ch_sto[0].n, ch_sto[1].n)


    w_sb = {"wf": wq_sb, "uf": wk_sb, "wi": wv_sb, "ui": wo_sb,
            "wc": qst, "uc": kst}
    _h_order = ("wf", "uf", "wi", "ui")
    h_view = lambda rt: w_sb[_h_order[rt // 8]][:, (rt % 8) * D:((rt % 8) + 1) * D]
    wo_sl = qst
    uo_sl = kst
    xtT_sl = sb("xtT_sl", [128, 2 * 8 * 512], BF16)
    cT = sb("cT", [128, 8 * 512], F32)
    cT_bf = zb[:].bitcast(BF16)
    xr_sl = gat[:].bitcast(BF16)
    biases = sb("biases", [128, 4 * 8], F32)
    onesf = bo_bc[:, 0:512]
    bo2_sb = ktok[0:1, 0:D]
    onesr2 = sb("onesr2", [1, 128], BF16)
    hmean = sb("hmean", [128, 32], F32)
    hvar = sb("hvar", [128, 32], F32)
    hrstd = sb("hrstd", [128, 32], F32)
    hms = sb("hms", [128, 32], F32)
    sqj2 = otn[:].bitcast(BF16)
    fjunk2 = qtok[:, 0:D]
    ps_g = (ps_a,)
    ps_h = (ps_p0, ps_p1)
    ch_xtl = (c("xtl0", 16), c("xtl1", 16))
    ch_xrl = (c("xrl0", 16), c("xrl1", 16))
    ch_g = c("g")
    ch_gev = c("gev")
    ch_cup = c("cup")
    ch_cbf = c("cbf")
    ch_h = c("h")
    ch_hev = c("hev")
    ch_hst = c("hst")
    ch_ho = c("ho")

    def g_fc1(g, apj=A_PJ):
        g.wait_ge(ch_pj.sem, apj)
        ch_cd.inc(g.dma_start(out=bo2_sb[:], in_=bov[:]))
        for i, bsrc in enumerate((bfv, biv, bcv, incv)):
            ch_cd.inc(g.dma_start(
                out=biases[:, i * 8:(i + 1) * 8],
                in_=bsrc.rearrange("o (mt p) -> p (o mt)", p=128),
                allow_slow_non_contiguous=True))
        for nm, dn in (("wf", "wfT"), ("uf", "ufT"), ("wi", "wiT")):
            ch_cd.inc(g.dma_start(
                out=w_sb[nm][:].rearrange("p (ft m) -> p ft m", ft=8),
                in_=wd[dn].rearrange("(ft p) m -> p ft m", p=128)))
    P("gpsimd", g_fc1)
    ch_cd.bump(8)

    def g_fc2(g, aam=A_AM):
        g.wait_ge(ch_am.sem, aam)
        for nm, dn in (("wc", "wcT"), ("uc", "ucT")):
            ch_cd.inc(g.dma_start(
                out=w_sb[nm][:].rearrange("p (ft m) -> p ft m", ft=8),
                in_=wd[dn].rearrange("(ft p) m -> p ft m", p=128)))
    P("gpsimd", g_fc2)
    ch_cd.bump(2)

    def g_fc3(g, az=A_Z):
        g.wait_ge(ch_z.sem, az)
        ch_cd.inc(g.dma_start(
            out=w_sb["ui"][:].rearrange("p (ft m) -> p ft m", ft=8),
            in_=wd["uiT"].rearrange("(ft p) m -> p ft m", p=128)))
    P("gpsimd", g_fc3)
    ch_cd.bump(1)
    FCONSTS = ch_cd.n

    NT2 = S // 128
    LN3 = tuple((ch.sem, ch.n) for ch in ch_trd)

    def f_fmemset(v):
        v.drain()
        v.memset(onesf[:], 1.0)
        v.memset(onesr2[:], 1.0)
    P("vector", f_fmemset)

    def f_cinit(v, fc_=FCONSTS):
        v.wait_ge(ch_cd.sem, fc_)
        v.drain()
        for mt in range(8):
            inst = v.tensor_scalar(cT[:, mt * 512:(mt + 1) * 512], onesf[:],
                                   biases[:, 24 + mt:24 + mt + 1], None, AL.mult)
        ch_cup.inc(inst)
    P("vector", f_cinit)
    ch_cup.bump()

    GATES = (("wf", "uf", 0, AF.Sigmoid), ("wi", "ui", 8, AF.Sigmoid),
             ("wc", "uc", 16, AF.Tanh))
    for t_step in range(8):
        buf = t_step % 2

        def f_xtl(s, t_step=t_step, buf=buf, chxt=ch_xtl[buf], ln=LN3,
                  need=ch_g.n):
            for sem_, v_ in ln:
                s.wait_ge(sem_, v_)
            if t_step >= 2:
                s.wait_ge(ch_g.sem, need)
            chxt.inc(s.dma_start(
                out=xtT_sl[:, buf * 4096:(buf + 1) * 4096].rearrange(
                    "p (ft q) -> p ft q", ft=8),
                in_=xt_scr[:, t_step * 512:(t_step + 1) * 512].rearrange(
                    "(ft p) q -> p ft q", p=128)))
        P("sync", f_xtl)
        ch_xtl[buf].bump()
        XTL = (ch_xtl[buf], ch_xtl[buf].n)

        def f_cbf(v, cu=ch_cup.n):
            v.wait_ge(ch_cup.sem, cu)
            ch_cbf.inc(v.tensor_copy(cT_bf[:], cT[:]))
        P("vector", f_cbf)
        ch_cbf.bump()
        CBF = ch_cbf.n

        for mt in range(8):
            for gi, (wn, un, bcol, func) in enumerate(GATES):
                kglob = (t_step * 8 + mt) * 3 + gi
                pg = kglob % 2

                def f_gmm(t, wn=wn, un=un, mt=mt, pg=pg, buf=buf, xtl=XTL,
                          cbf=CBF, gev=(ch_gev.n - 1)):
                    t.wait_ge(xtl[0].sem, xtl[1])
                    t.wait_ge(ch_cbf.sem, cbf)
                    if gev > 0:
                        t.wait_ge(ch_gev.sem, gev)
                    for ft in range(8):
                        t.matmul(ps_a[:, pg * 512:(pg + 1) * 512],
                                 w_sb[wn][:, ft * D + mt * 128: ft * D + mt * 128 + 128],
                                 xtT_sl[:, buf * 4096 + ft * 512:
                                        buf * 4096 + (ft + 1) * 512],
                                 start=(ft == 0), stop=False)
                    for m2 in range(8):
                        inst = t.matmul(
                            ps_a[:, pg * 512:(pg + 1) * 512],
                            w_sb[un][:, m2 * D + mt * 128: m2 * D + mt * 128 + 128],
                            cT_bf[:, m2 * 512:(m2 + 1) * 512],
                            start=False, stop=(m2 == 7))
                    ch_g.inc(inst)
                P("tensor", f_gmm)
                ch_g.bump()

                def f_gev(sc, gi=gi, pg=pg, func=func, bcol=bcol, mt=mt,
                          gc=ch_g.n, cup=(ch_cup.n if gi == 0 else 0)):
                    sc.wait_ge(ch_g.sem, gc)
                    if cup:
                        sc.wait_ge(ch_cup.sem, cup)
                    ch_gev.inc(sc.activation(
                        gat[:, gi * 512:(gi + 1) * 512],
                        ps_a[:, pg * 512:(pg + 1) * 512], func,
                        bias=biases[:, bcol + mt:bcol + mt + 1]))
                P("scalar", f_gev)
                ch_gev.bump()

            def f_cup(v, mt=mt, ge=ch_gev.n):
                v.wait_ge(ch_gev.sem, ge)
                cslice = cT[:, mt * 512:(mt + 1) * 512]
                v.tensor_mul(cslice, cslice, gat[:, 0:512])
                v.tensor_mul(gat[:, 512:1024], gat[:, 512:1024], gat[:, 1024:1536])
                v.drain()
                ch_cup.inc(v.tensor_add(cslice, cslice, gat[:, 512:1024]))
            P("vector", f_cup)
            ch_cup.bump()
    SCAN = ch_cup.n
    SCAN_G = ch_g.n

    def f_cfin(v, sc_=SCAN):
        v.wait_ge(ch_cup.sem, sc_)
        ch_cbf.inc(v.tensor_copy(cT_bf[:], cT[:]))
    P("vector", f_cfin)
    ch_cbf.bump()
    CFIN = ch_cbf.n

    for fc in range(2):
        def f_wol(g, fc=fc, sg=SCAN_G):
            g.wait_ge(ch_g.sem, sg)
            ch_cd.inc(g.dma_start(
                out=wo_sl[:, fc * 4096:(fc + 1) * 4096].rearrange(
                    "p (ft q) -> p ft q", ft=8),
                in_=wd["woT2"][:, fc * 512:(fc + 1) * 512].rearrange(
                    "(ft p) q -> p ft q", p=128)))
            ch_cd.inc(g.dma_start(
                out=uo_sl[:, fc * 4096:(fc + 1) * 4096].rearrange(
                    "p (ft q) -> p ft q", ft=8),
                in_=wd["uoT"][:, fc * 512:(fc + 1) * 512].rearrange(
                    "(ft p) q -> p ft q", p=128)))
        P("gpsimd", f_wol)
        ch_cd.bump(2)
    WOL = ch_cd.n

    for rt in range(NT2):
        buf = rt % 2
        bsl = rt % 4

        def f_xrl(g, rt=rt, buf=buf, chxr=ch_xrl[buf], sc_=SCAN,
                  need=(ch_h.n - 1)):
            g.wait_ge(ch_cup.sem, sc_)
            if rt >= 2 and need > 0:
                g.wait_ge(ch_h.sem, need)
            chxr.inc(g.dma_start(
                out=xr_sl[:, buf * 1024:(buf + 1) * 1024].rearrange(
                    "p (ft r) -> p ft r", ft=8),
                in_=xt_scr[:, rt * 128:(rt + 1) * 128].rearrange(
                    "(ft p) r -> p ft r", p=128)))
        P("gpsimd", f_xrl)
        ch_xrl[buf].bump()
        XRL = (ch_xrl[buf], ch_xrl[buf].n)

        for fc in range(2):
            ph = (rt * 2 + fc) % 2

            def f_hmm(t, rt=rt, buf=buf, fc=fc, ph=ph, bsl=bsl, xrl=XRL,
                      cfin=CFIN, wol=WOL, hev=(ch_hev.n - 1)):
                t.wait_ge(xrl[0].sem, xrl[1])
                t.wait_ge(ch_cbf.sem, cfin)
                t.wait_ge(ch_cd.sem, wol)
                if hev > 0:
                    t.wait_ge(ch_hev.sem, hev)
                psh = ps_h[ph]
                for ft in range(8):
                    t.matmul(psh[:, :],
                             xr_sl[:, buf * 1024 + ft * 128:
                                   buf * 1024 + (ft + 1) * 128],
                             wo_sl[:, fc * 4096 + ft * 512:
                                   fc * 4096 + (ft + 1) * 512],
                             start=(ft == 0), stop=False)
                for m2 in range(8):
                    t.matmul(psh[:, :],
                             cT_bf[:, m2 * 512 + bsl * 128:
                                   m2 * 512 + (bsl + 1) * 128],
                             uo_sl[:, fc * 4096 + m2 * 512:
                                   fc * 4096 + (m2 + 1) * 512],
                             start=False, stop=False)
                inst = t.matmul(psh[:, :], onesr2[:],
                                bo2_sb[0:1, fc * 512:(fc + 1) * 512],
                                start=False, stop=True)
                ch_h.inc(inst)
            P("tensor", f_hmm)
            ch_h.bump()

            def f_hev(sc, rt=rt, fc=fc, ph=ph, hc=ch_h.n):
                sc.wait_ge(ch_h.sem, hc)
                ch_hev.inc(sc.activation(
                    h_view(rt)[:, fc * 512:(fc + 1) * 512],
                    ps_h[ph][:, :], AF.Tanh))
            P("scalar", f_hev)
            ch_hev.bump()

        def f_hstat(v, rt=rt, he=ch_hev.n):
            v.wait_ge(ch_hev.sem, he)
            v.drain()
            hrow = h_view(rt)
            v.reduce_sum(hmean[:, rt:rt + 1], hrow, axis=AX)
            ch_hst.inc(v.scalar_tensor_tensor(
                sqj2[:], hrow, 1.0, hrow, AL.mult, AL.mult,
                accum_out=hvar[:, rt:rt + 1]))
        P("vector", f_hstat)
        ch_hst.bump()
    HSTAT = ch_hst.n

    def f_hpre(v, hs=HSTAT):
        v.wait_ge(ch_hst.sem, hs)
        v.scalar_tensor_tensor(hms[:], hmean[:], 1.0 / (D * D), hmean[:],
                               AL.mult, AL.mult)
        v.tensor_scalar_mul(hvar[:], hvar[:], 1.0 / D)
        v.tensor_scalar_mul(hmean[:], hmean[:], 1.0 / D)
        v.drain()
        ch_hst.inc(v.tensor_sub(hvar[:], hvar[:], hms[:]))
    P("vector", f_hpre)
    ch_hst.bump()

    def f_hrstd(sc, hs=ch_hst.n):
        sc.wait_ge(ch_hst.sem, hs)
        sc.activation(hms[:], hvar[:], AF.Ln, bias=epst[:])
        sc.drain()
        ch_rst.inc(sc.activation(hrstd[:], hms[:], AF.Exp, scale=-0.5))
    P("scalar", f_hrstd)
    ch_rst.bump()
    HR = ch_rst.n

    for rt in range(NT2):
        buf = rt % 2
        xi = xin[:, buf * D:(buf + 1) * D]
        ot = x1t[:, buf * D:(buf + 1) * D]

        def f_xl2(s, rt=rt, xi=xi, chx2=ch_xd[buf], need=(ch_ho.n - 1),
                  xl_=(xin_last[buf] if rt < 2 else 0)):
            if rt >= 2 and need > 0:
                s.wait_ge(ch_ho.sem, need)
            if xl_:
                s.wait_ge(ch_vfree.sem, xl_)
            chx2.inc(s.dma_start(out=xi, in_=x1[rt * 128:(rt + 1) * 128, :]))
        P("gpsimd", f_xl2)
        ch_xd[buf].bump()
        XD2 = (ch_xd[buf], ch_xd[buf].n)

        def f_happ(v, rt=rt, xi=xi, ot=ot, hr=HR, xd=XD2,
                   stneed=(ch_sto[rt % 2], ch_sto[rt % 2].n)):
            v.wait_ge(ch_rst.sem, hr)
            v.wait_ge(xd[0].sem, xd[1])
            if stneed[1] > 0:
                v.wait_ge(stneed[0].sem, stneed[1])
            v.tensor_scalar(ot, h_view(rt), hmean[:, rt:rt + 1],
                            hrstd[:, rt:rt + 1], AL.subtract, AL.mult)
            v.drain()
            ch_ho.inc(v.tensor_add(ot, ot, xi))
        P("vector", f_happ)
        ch_ho.bump()

        def f_sto(s, rt=rt, ot=ot, chs=ch_sto[rt % 2], hc=ch_ho.n):
            s.wait_ge(ch_ho.sem, hc)
            chs.inc(s.dma_start(out=out[rt * 128:(rt + 1) * 128, :], in_=ot))
        P("gpsimd", f_sto)
        ch_sto[rt % 2].bump()

    def f_fin(s, tots=tuple((ch.sem, ch.n) for ch in ch_sto)):
        for sem, v_ in tots:
            s.wait_ge(sem, v_)
    P("sync", f_fin)

    with nc.allow_low_precision("bf16 matmul operands / attention weights; "
                                "fp32 accumulation and LN stats throughout"):
        with nc.Block() as block:
            P.emit(block)
    for _ch in _chans:
        assert _ch.emitted == _ch.n, (
            f"channel {_ch.name}: emitted {_ch.emitted} != planned {_ch.n}")
    st.close()
    return nc





_CACHE = {}


def _programs():
    if "attn" not in _CACHE:
        _CACHE["attn"] = build_attn_program()
        _CACHE["ffn"] = build_ffn_program()
    return _CACHE["attn"], _CACHE["ffn"]


def _attn_consts():
    f32 = np.float32
    ident = np.eye(128, dtype=f32)
    bsel = np.zeros((8, 128), f32)
    for b in range(8):
        bsel[b, b * 16:(b + 1) * 16] = 20.0
    rowneg = np.full((1, 128), -20.0, f32)
    rowpos = np.full((1, 128), 20.0, f32)
    return ident, bsel, rowneg, rowpos


def _rows_for_core(j):
    r = np.arange(512)
    return 256 * (r // 32) + 32 * j + (r % 32)


def kernel(L0_attn_x, L0_attn_Wq, L0_attn_Wk, L0_attn_Wv, L0_attn_Wo,
           L0_attn_bo, L0_ffn_wf, L0_ffn_uf, L0_ffn_bf, L0_ffn_wi, L0_ffn_ui,
           L0_ffn_bi, L0_ffn_wc, L0_ffn_uc, L0_ffn_bc, L0_ffn_wo, L0_ffn_uo,
           L0_ffn_bo, L0_ffn_in_c):
    f32 = np.float32
    TT = lambda a: np.ascontiguousarray(np.asarray(a, f32).T)
    row = lambda a: np.ascontiguousarray(np.asarray(a, f32).reshape(1, -1))
    x = np.ascontiguousarray(np.asarray(L0_attn_x, f32))

    ident, bsel, rowneg, rowpos = _attn_consts()
    maskq = np.zeros((9, 8192), f32)
    maskq[0, :] = 20.0
    maskq[1:9, :] = np.tile(bsel, (1, 64))
    maskk = maskq.copy()
    maskk[0, :] = -20.0
    if "fused" not in _CACHE:
        _CACHE["fused"] = build_fused_program()
    nc = _CACHE["fused"]

    common = dict(wqT=TT(L0_attn_Wq), wkT=TT(L0_attn_Wk), wvT=TT(L0_attn_Wv),
                  woT=TT(L0_attn_Wo), bo=row(L0_attn_bo), ident=ident,
                  bsel=bsel, rowneg=rowneg, rowpos=rowpos,
                  maskq=maskq, maskk=maskk,
                  wfT=TT(L0_ffn_wf), ufT=TT(L0_ffn_uf), wiT=TT(L0_ffn_wi),
                  uiT=TT(L0_ffn_ui), wcT=TT(L0_ffn_wc), ucT=TT(L0_ffn_uc),
                  woT2=TT(L0_ffn_wo), uoT=TT(L0_ffn_uo),
                  bfv=row(L0_ffn_bf), biv=row(L0_ffn_bi), bcv=row(L0_ffn_bc),
                  bov=row(L0_ffn_bo), incv=row(L0_ffn_in_c))
    in_maps = []
    for j in range(NCORES):
        rows = _rows_for_core(j)
        in_maps.append(dict(
            common,
            x=np.ascontiguousarray(x[:, 512 * j:512 * (j + 1), :]),
            xr=np.ascontiguousarray(x[:, rows, :])))
    res = run_bass_kernel_spmd(nc, in_maps, core_ids=list(range(NCORES)))

    out = np.empty((B, S, D), f32)
    for j in range(NCORES):
        rows = _rows_for_core(j)
        out[:, rows, :] = res.results[j]["out"].reshape(B, 512, D)
    return out


# version 24
# speedup vs baseline: 1.0047x; 1.0047x over previous
import numpy as np
import ml_dtypes
import concourse.bass as bass
import concourse.mybir as mybir
from concourse.bass_utils import run_bass_kernel_spmd
from contextlib import ExitStack

F32 = mybir.dt.float32
BF16 = mybir.dt.bfloat16
AF = mybir.ActivationFunctionType
AL = mybir.AluOpType
AX = mybir.AxisListType.X

D = 1024
S = 4096
B = 8
NP = 8
NKT = 32
NCORES = 8
EPS = 1e-5


class Chan:

    def __init__(self, st, nc, name, step=1):
        self.sem = st.enter_context(nc.semaphore(name))
        self.name = name
        self.step = step
        self.n = 0
        self.emitted = 0

    def bump(self, k=1):
        self.n += k * self.step
        return self.n

    def inc(self, inst):
        self.emitted += self.step
        inst.then_inc(self.sem, self.step)
        return inst


class Plan:
    ENGS = ("tensor", "scalar", "vector", "gpsimd", "sync")

    def __init__(self):
        self.ops = {k: [] for k in self.ENGS}

    def __call__(self, eng, fn):
        self.ops[eng].append(fn)

    def emit(self, block):
        for eng in self.ENGS:
            ops = self.ops[eng]
            getattr(block, eng)(lambda e, ops=ops: [fn(e) for fn in ops])


def ln_stats(P, ch_sum, ch_rst, xi, sq_junk, sl, s1, s2, mean, var, msq,
             rstd, epst, width):
    def f_sums(v, xi=xi, sl=sl):
        v.drain()
        v.reduce_sum(s1[:, sl], xi, axis=AX)
        v.scalar_tensor_tensor(sq_junk, xi, 1.0, xi, AL.mult, AL.mult,
                               accum_out=s2[:, sl])
        v.drain()
        v.tensor_scalar_mul(mean[:, sl], s1[:, sl], 1.0 / width)
        v.scalar_tensor_tensor(msq[:, sl], s1[:, sl], 1.0 / (width * width),
                               s1[:, sl], AL.mult, AL.mult)
        v.tensor_scalar_mul(var[:, sl], s2[:, sl], 1.0 / width)
        v.drain()
        ch_sum.inc(v.tensor_sub(var[:, sl], var[:, sl], msq[:, sl]))
    P("vector", f_sums)
    ch_sum.bump()

    def f_rstd(sc, sl=sl, sm=ch_sum.n):
        sc.wait_ge(ch_sum.sem, sm)
        sc.activation(msq[:, sl], var[:, sl], AF.Ln, bias=epst)
        sc.drain()
        ch_rst.inc(sc.activation(rstd[:, sl], msq[:, sl], AF.Exp, scale=-0.5))
    P("scalar", f_rstd)
    ch_rst.bump()


def ln_stats_sc(P, ch_rst, xi, sq_junk, sq_junk2, sl, s1, s2, mean, msq,
                rstd, width, wait=None, dr=False):
    def f_stat(sc, xi=xi, sl=sl, wait=wait, sq_junk2=sq_junk2, dr=dr):
        if wait is not None:
            sc.wait_ge(wait[0].sem, wait[1])
        sc.activation(sq_junk, xi, AF.Square, accum_out=s2[:, sl])
        if dr:
            sc.drain()
        sc.activation(sq_junk2, xi, AF.Copy, accum_out=s1[:, sl])
        sc.drain()
        sc.mul(mean[:, sl], s1[:, sl], 1.0 / width)
        sc.activation(msq[:, sl], s1[:, sl], AF.Square, scale=1.0 / width)
        sc.drain()
        sc.activation(msq[:, sl], msq[:, sl], AF.Copy, scale=-1.0, bias=EPS)
        sc.drain()
        sc.activation(rstd[:, sl], s2[:, sl], AF.Ln, scale=1.0 / width,
                      bias=msq[:, sl])
        sc.drain()
        ch_rst.inc(sc.activation(rstd[:, sl], rstd[:, sl], AF.Exp, scale=-0.5))
    P("scalar", f_stat)
    ch_rst.bump()



def build_attn_program(trunc="full"):
    LVL = {"ln1": 0, "qkst": 1, "attn1": 2, "full": 3}[trunc]
    nc = bass.Bass("TRN2", target_bir_lowering=False, debug=False)
    x = nc.dram_tensor("x", [B, 512, D], F32, kind="ExternalInput").ap()
    wqT = nc.dram_tensor("wqT", [D, D], F32, kind="ExternalInput").ap()
    wkT = nc.dram_tensor("wkT", [D, D], F32, kind="ExternalInput").ap()
    wvT = nc.dram_tensor("wvT", [D, D], F32, kind="ExternalInput").ap()
    woT = nc.dram_tensor("woT", [D, D], F32, kind="ExternalInput").ap()
    bo = nc.dram_tensor("bo", [1, D], F32, kind="ExternalInput").ap()
    xr = nc.dram_tensor("xr", [B, 512, D], F32, kind="ExternalInput").ap()
    ident = nc.dram_tensor("ident", [128, 128], F32, kind="ExternalInput").ap()
    bsel = nc.dram_tensor("bsel", [8, 128], F32, kind="ExternalInput").ap()
    rowneg = nc.dram_tensor("rowneg", [1, 128], F32, kind="ExternalInput").ap()
    rowpos = nc.dram_tensor("rowpos", [1, 128], F32, kind="ExternalInput").ap()
    x1 = nc.dram_tensor("x1", [S, D], F32, kind="ExternalOutput").ap()
    v_scr = nc.dram_tensor("v_scr", [2, 512, D], BF16, kind="Internal").ap()
    od2 = nc.dram_tensor("od2", [2, 512 * 16, 64], BF16, kind="Internal").ap()
    dbg = (nc.dram_tensor("dbg", [128, 49152], BF16, kind="ExternalOutput").ap()
           if LVL < 3 else None)

    st = ExitStack()
    sb = lambda name, shape, dt: st.enter_context(nc.sbuf_tensor(name, shape, dt))
    psa = lambda name, shape, dt: st.enter_context(nc.psum_tensor(name, shape, dt))

    wq_sb = sb("wq_sb", [128, 8 * D], BF16)
    wk_sb = sb("wk_sb", [128, 8 * D], BF16)
    wv_sb = sb("wv_sb", [128, 8 * D], BF16)
    wo_sb = sb("wo_sb", [128, 8 * D], BF16)
    xin = sb("xin", [128, 2 * D], F32)
    xln_sb = sb("xln_sb", [128, 2 * D], BF16)
    xlnT_t = sb("xlnT_t", [128, 2 * 1024], BF16)
    qtok = sb("qtok", [128, 2 * D], BF16)
    ktok = sb("ktok", [128, 2 * D], BF16)
    vtok = sb("vtok", [128, 2 * D], BF16)
    qst = sb("qst", [64, 512 * 16], BF16)
    kst = sb("kst", [64, 512 * 16], BF16)
    vst = sb("vst", [128, 2 * 4 * 65], BF16)
    pt_sb = sb("pt_sb", [128, 2 * 512], BF16)
    otn = sb("otn", [64, 512], BF16)
    orow = sb("orow", [128, 2 * 256], BF16)
    rl_sb = sb("rl_sb", [128, 512], BF16)
    rlb_sb = sb("rlb_sb", [64, 512], F32)
    ones64 = sb("ones64", [128, 64], BF16)
    zin = sb("zin", [128, 2 * D], BF16)
    zlT_t = sb("zlT_t", [128, 2 * 1024], BF16)
    zb = sb("zb", [128, 2 * D], F32)
    x1t = sb("x1t", [128, 2 * D], F32)
    sqj = sb("sqj", [128, D], BF16)
    bo_bc = sb("bo_bc", [128, D], F32)
    bo_sb = sb("bo_sb", [128, D], BF16)
    ident_sb = sb("ident_sb", [128, 128], BF16)
    bsel_sb = sb("bsel_sb", [8, 128], BF16)
    rneg_sb = sb("rneg_sb", [1, 128], BF16)
    rpos_sb = sb("rpos_sb", [1, 128], BF16)
    epst = sb("epst", [128, 1], F32)
    s1 = sb("s1", [128, 2], F32)
    s2 = sb("s2", [128, 2], F32)
    mean = sb("mean", [128, 2], F32)
    var = sb("var", [128, 2], F32)
    msq = sb("msq", [128, 2], F32)
    rstd = sb("rstd", [128, 2], F32)

    ps_p0 = psa("ps_p0", [128, 512], F32)
    ps_p1 = psa("ps_p1", [128, 512], F32)
    ps_a = psa("ps_a", [128, 2 * 512], F32)
    ps_ot = psa("ps_ot", [128, 512], F32)
    ps_rl = psa("ps_rl", [128, 512], F32)
    ps_tr = psa("ps_tr", [128, 1024], BF16)

    _chans = []
    def c(name, step=1):
        ch = Chan(st, nc, name, step)
        _chans.append(ch)
        return ch
    ch_xd = (c("xd0", 16), c("xd1", 16))
    ch_cd = c("cd", 16)
    ch_sum = c("sum")
    ch_rst = c("rst")
    ch_xln = c("xln")
    ch_tr = c("tr")
    ch_trev = c("trev")
    ch_pj = c("pj")
    ch_pjev = c("pjev")
    ch_vsp = (c("vsp0", 16), c("vsp1", 16))
    ch_stk = c("stk")
    ch_am = c("am")
    ch_exp = c("exp")
    ch_vld = (c("vld0", 16), c("vld1", 16))
    ch_pv = c("pv")
    ch_rl = c("rl")
    ch_rlb = c("rlb")
    ch_rlbe = c("rlbe")
    ch_nrm = c("nrm")
    ch_orw = c("orw")
    ch_ods = (c("ods0", 16), c("ods1", 16))
    ch_zld = (c("zld0", 16), c("zld1", 16))
    ch_z = c("z")
    ch_zev = c("zev")
    ch_bob = c("bob")
    ch_x1 = c("x1c")
    ch_sto = (c("sto0", 16), c("sto1", 16))
    ch_vfree = c("vfree")

    P = Plan()

    def g_consts(g):
        for out_ap, in_ap in ((ident_sb[:], ident[:]), (bsel_sb[:], bsel[:]),
                              (rneg_sb[:], rowneg[:]), (rpos_sb[:], rowpos[:]),
                              (bo_sb[64:65, :], bo[:])):
            ch_cd.inc(g.dma_start(out=out_ap, in_=in_ap))
        for w_sb_, wdram in ((wq_sb, wqT), (wk_sb, wkT), (wv_sb, wvT),
                             (wo_sb, woT)):
            ch_cd.inc(g.dma_start(
                out=w_sb_[:].rearrange("p (ft m) -> p ft m", ft=8),
                in_=wdram.rearrange("(ft p) m -> p ft m", p=128)))
    P("gpsimd", g_consts)
    ch_cd.bump(9)
    CONSTS = ch_cd.n

    def f_ones_init(v, cd=CONSTS):
        v.wait_ge(ch_cd.sem, cd)
        v.memset(epst[:], EPS)
        v.memset(vst[:].rearrange("p (b e) -> p b e", b=8)[:, :, 64:65], 1.0)
        ch_bob.inc(v.memset(ones64[64:65, :], 1.0))
    P("vector", f_ones_init)
    ch_bob.bump()
    ONES_RDY = ch_bob.n

    for half in (0, 1):
        def f_bob_mm(t, half=half, onr=ONES_RDY, ze=ch_zev.n):
            t.wait_ge(ch_bob.sem, onr)
            if ze:
                t.wait_ge(ch_zev.sem, ze)
            ch_z.inc(t.matmul(ps_rl[0:64, :], ones64[64:65, :],
                              bo_sb[64:65, half * 512:(half + 1) * 512],
                              start=True, stop=True, tile_position=(64, 0)))
        P("tensor", f_bob_mm)
        ch_z.bump()

        def f_bob_ev(sc, half=half, zc=ch_z.n):
            sc.wait_ge(ch_z.sem, zc)
            inst = sc.copy(bo_bc[0:64, half * 512:(half + 1) * 512], ps_rl[0:64, :])
            ch_zev.inc(inst)
        P("scalar", f_bob_ev)
        ch_zev.bump()
    for half in (0, 1):
        def f_bob_mm2(t, half=half, ze=ch_zev.n):
            t.wait_ge(ch_zev.sem, ze)
            ch_z.inc(t.matmul(ps_rl[64:128, :], ones64[64:65, 0:64],
                              bo_sb[64:65, half * 512:(half + 1) * 512],
                              start=True, stop=True, tile_position=(64, 64)))
        P("tensor", f_bob_mm2)
        ch_z.bump()

        def f_bob_ev2(sc, half=half, zc=ch_z.n):
            sc.wait_ge(ch_z.sem, zc)
            inst = sc.copy(bo_bc[64:128, half * 512:(half + 1) * 512],
                           ps_rl[64:128, :])
            ch_zev.inc(inst)
            ch_bob.inc(sc.drain())
        P("scalar", f_bob_ev2)
        ch_zev.bump()
        ch_bob.bump()
    BOB = ch_bob.n
    PS_RL_FREE = ch_zev.n

    NTIL = 4
    NPK = 64
    tr_hist = []
    exp_hist = []
    nrm_hist = []
    PJ_EV = {}

    for n in range(B):
        for tt in range(NTIL):
            gti = n * NTIL + tt
            buf = gti % 2
            xi = xin[:, buf * D:(buf + 1) * D]
            xl = xln_sb[:, buf * D:(buf + 1) * D]
            xt = xlnT_t[:, buf * 1024:(buf + 1) * 1024]
            sl = slice(buf, buf + 1)
            chx = ch_xd[buf]

            def f_load(s, n=n, tt=tt, gti=gti, xi=xi, chx=chx,
                       need=ch_vfree.n - 1):
                if gti >= 2:
                    s.wait_ge(ch_vfree.sem, need)
                chx.inc(s.dma_start(
                    out=xi, in_=x[n, tt * 128:(tt + 1) * 128, :]))
            P("sync", f_load)
            chx.bump()

            def f_prewait(v, chx=chx, xd=chx.n):
                v.wait_ge(chx.sem, xd)
            P("vector", f_prewait)
            ln_stats(P, ch_sum, ch_rst, xi, sqj[:], sl, s1, s2, mean, var,
                     msq, rstd, epst[:], D)

            def f_apply(v, xi=xi, xl=xl, sl=sl, rs=ch_rst.n,
                        trneed=(tr_hist[gti - 2][0] if gti >= 2 else 0)):
                v.wait_ge(ch_rst.sem, rs)
                if trneed:
                    v.wait_ge(ch_tr.sem, trneed)
                ch_xln.inc(v.tensor_scalar(xl, xi, mean[:, sl], rstd[:, sl],
                                           AL.subtract, AL.mult))
                ch_vfree.inc(v.drain())
            P("vector", f_apply)
            ch_xln.bump()
            ch_vfree.bump()

            def f_tr(t, xl=xl, xc=ch_xln.n, tev=ch_trev.n, cd=CONSTS):
                t.wait_ge(ch_xln.sem, xc)
                t.wait_ge(ch_cd.sem, cd)
                if tev:
                    t.wait_ge(ch_trev.sem, tev)
                for ft in range(8):
                    inst = t.transpose(ps_tr[:, ft * 128:(ft + 1) * 128],
                                       xl[:, ft * 128:(ft + 1) * 128],
                                       ident_sb[:])
                ch_tr.inc(inst)
            P("tensor", f_tr)
            ch_tr.bump()
            tr_hist.append((ch_tr.n, 0))
            XLN_TR = ch_tr.n

            def f_trev(v, xt=xt, trc=ch_tr.n):
                v.wait_ge(ch_tr.sem, trc)
                ch_trev.inc(v.tensor_copy(xt, ps_tr[:]))
            P("vector", f_trev)
            ch_trev.bump()
            XT_RDY = ch_trev.n

            for w_sb_, dtok in ((wq_sb, qtok), (wk_sb, ktok), (wv_sb, vtok)):
                for fc in range(2):
                    psp = ps_p0 if fc == 0 else ps_p1
                    def f_pmm(t, w_sb_=w_sb_, fc=fc, psp=psp, xt=xt,
                              xrdy=XT_RDY, pf=PS_RL_FREE):
                        t.wait_ge(ch_trev.sem, xrdy)
                        t.wait_ge(ch_zev.sem, pf)
                        for ft in range(8):
                            inst = t.matmul(
                                psp[:, :],
                                xt[:, ft * 128:(ft + 1) * 128],
                                w_sb_[:, ft * D + fc * 512: ft * D + (fc + 1) * 512],
                                start=(ft == 0), stop=(ft == 7))
                        ch_pj.inc(inst)
                    P("tensor", f_pmm)
                    ch_pj.bump()

                    def f_pev(v, dtok=dtok, buf=buf, fc=fc, psp=psp, pj=ch_pj.n,
                              vspn=(ch_vsp[buf].n if dtok is vtok else 0),
                              chvs=ch_vsp[buf]):
                        v.wait_ge(ch_pj.sem, pj)
                        if vspn:
                            v.wait_ge(chvs.sem, vspn)
                        inst = v.tensor_copy(
                            dtok[:, buf * D + fc * 512: buf * D + (fc + 1) * 512],
                            psp[:, :])
                        ch_zev.inc(inst)
                        ch_pjev.inc(v.drain())
                    P("vector", f_pev)
                    ch_zev.bump()
                    ch_pjev.bump()
                    PS_RL_FREE = ch_zev.n
            TOK_RDY = ch_pjev.n

            def f_vsp(s, n=n, tt=tt, buf=buf, chv=ch_vsp[buf], tok=TOK_RDY):
                s.wait_ge(ch_pjev.sem, tok)
                chv.inc(s.dma_start(
                    out=v_scr[n % 2, tt * 128:(tt + 1) * 128, :],
                    in_=vtok[:, buf * D:(buf + 1) * D]))
            P("sync", f_vsp)
            ch_vsp[buf].bump()

            for src_tok, dst_st in ((qtok, qst), (ktok, kst)):
                for hg in range(2):
                    def f_qtr(t, src_tok=src_tok, buf=buf, hg=hg, tok=TOK_RDY,
                              tev=ch_trev.n):
                        t.wait_ge(ch_pjev.sem, tok)
                        t.wait_ge(ch_trev.sem, tev)
                        for hh in range(8):
                            h = hg * 8 + hh
                            inst = t.transpose(
                                ps_tr[0:64, hh * 128:(hh + 1) * 128],
                                src_tok[:, buf * D + h * 64: buf * D + (h + 1) * 64],
                                ident_sb[:])
                        ch_tr.inc(inst)
                    P("tensor", f_qtr)
                    ch_tr.bump()

                    def f_qtrev(v, dst_st=dst_st, tt=tt, hg=hg, trc=ch_tr.n):
                        v.wait_ge(ch_tr.sem, trc)
                        out_ap = dst_st[0:64, :].rearrange(
                            "p (s h) -> p s h", h=16)[:, tt * 128:(tt + 1) * 128,
                                                      hg * 8:(hg + 1) * 8]
                        in_ap = ps_tr[0:64, :].rearrange(
                            "p (hh s) -> p s hh", hh=8)
                        inst = v.tensor_copy(out_ap, in_ap)
                        ch_trev.inc(inst)
                        ch_stk.inc(v.drain())
                    P("vector", f_qtrev)
                    ch_trev.bump()
                    ch_stk.bump()
        STK_RDY = ch_stk.n
        VSP_DONE = (ch_vsp[0].n, ch_vsp[1].n)

        for grp in range(NPK // 4):
            ggrp = n * 16 + grp
            gbuf = ggrp % 2

            def f_vld(s, n=n, grp=grp, gbuf=gbuf, chv=ch_vld[gbuf],
                      vsp=VSP_DONE, need=(nrm_hist[ggrp - 2] if ggrp >= 2 else 0)):
                s.wait_ge(ch_vsp[0].sem, vsp[0])
                s.wait_ge(ch_vsp[1].sem, vsp[1])
                if need:
                    s.wait_ge(ch_nrm.sem, need)
                row = vst[:].ap[0][0]
                dst = bass.AP(tensor=vst[:].tensor, offset=gbuf * 260,
                              ap=[[row, 128], [65, 4], [1, 64]])
                srcb = bass.AP(tensor=v_scr.tensor,
                               offset=(n % 2) * 512 * D + grp * 32 * D,
                               ap=[[64, 128], [8 * D, 4], [1, 64]])
                chv.inc(s.dma_start(out=dst, in_=srcb))
            P("sync", f_vld)
            ch_vld[gbuf].bump()
            VLD = ch_vld[gbuf].n

            for pk4 in range(4):
                pk = grp * 4 + pk4
                abank = (pk % 8) // 4
                aoff = (pk % 4) * 0
                acol = abank * 512 + pk4 * 128

                def f_amm(t, n=n, pk=pk, acol=acol, stk=STK_RDY,
                          ex=(exp_hist[ggrp - 2] if ggrp >= 2 else 0),
                          onr=ONES_RDY):
                    t.wait_ge(ch_stk.sem, stk)
                    t.wait_ge(ch_bob.sem, onr)
                    if ex:
                        t.wait_ge(ch_exp.sem, ex)
                    t.matmul(ps_a[:, acol:acol + 128], rneg_sb[:], rpos_sb[:],
                             start=True, stop=False)
                    t.matmul(ps_a[:, acol:acol + 128], bsel_sb[:], bsel_sb[:],
                             start=False, stop=False)
                    inst = t.matmul(ps_a[:, acol:acol + 128],
                                    kst[:, pk * 128:(pk + 1) * 128],
                                    qst[:, pk * 128:(pk + 1) * 128],
                                    start=False, stop=True)
                    if pk % 4 == 3:
                        ch_am.inc(inst)
                P("tensor", f_amm)
                if pk % 4 == 3:
                    ch_am.bump()

            def f_exp(sc, gbuf=gbuf, abank=((grp % 2)), am=ch_am.n):
                sc.wait_ge(ch_am.sem, am)
                ch_exp.inc(sc.activation(
                    pt_sb[:, gbuf * 512:(gbuf + 1) * 512],
                    ps_a[:, abank * 512:(abank + 1) * 512],
                    AF.Exp, scale=0.125))
            P("scalar", f_exp)
            ch_exp.bump()
            exp_hist.append(ch_exp.n)
            EXPG = ch_exp.n

            for pk4 in range(4):
                def f_pv(t, pk4=pk4, gbuf=gbuf, expg=EXPG, vld=VLD,
                         nrm=(nrm_hist[ggrp - 1] if ggrp >= 1 else 0)):
                    t.wait_ge(ch_exp.sem, expg)
                    t.wait_ge(ch_vld[gbuf].sem, vld)
                    if pk4 == 0 and nrm:
                        t.wait_ge(ch_nrm.sem, nrm)
                    inst = t.matmul(
                        ps_ot[0:65, pk4 * 128:(pk4 + 1) * 128],
                        vst[:, gbuf * 260 + pk4 * 65: gbuf * 260 + (pk4 + 1) * 65],
                        pt_sb[:, gbuf * 512 + pk4 * 128: gbuf * 512 + (pk4 + 1) * 128],
                        start=True, stop=True)
                    if pk4 == 3:
                        ch_pv.inc(inst)
                P("tensor", f_pv)
                if pk4 == 3:
                    ch_pv.bump()

            def f_recip(v, pv=ch_pv.n):
                v.wait_ge(ch_pv.sem, pv)
                ch_rl.inc(v.reciprocal(rl_sb[64:65, :], ps_ot[64:65, :]))
            P("vector", f_recip)
            ch_rl.bump()

            def f_rlb(t, rlc=ch_rl.n, pf=PS_RL_FREE):
                t.wait_ge(ch_rl.sem, rlc)
                t.wait_ge(ch_zev.sem, pf)
                ch_rlb.inc(t.matmul(ps_rl[0:64, :], ones64[64:65, :],
                                    rl_sb[64:65, :], start=True, stop=True,
                                    tile_position=(64, 0)))
            P("tensor", f_rlb)
            ch_rlb.bump()

            def f_rlbe(sc, rb=ch_rlb.n):
                sc.wait_ge(ch_rlb.sem, rb)
                inst = sc.copy(rlb_sb[:], ps_rl[0:64, :])
                ch_zev.inc(inst)
                ch_rlbe.inc(sc.drain())
            P("scalar", f_rlbe)
            ch_zev.bump()
            ch_rlbe.bump()
            PS_RL_FREE = ch_zev.n

            def f_norm(v, rbe=ch_rlbe.n, orw=ch_orw.n):
                v.wait_ge(ch_rlbe.sem, rbe)
                if orw:
                    v.wait_ge(ch_orw.sem, orw)
                ch_nrm.inc(v.tensor_mul(otn[0:64, :], ps_ot[0:64, :], rlb_sb[:]))
            P("vector", f_norm)
            ch_nrm.bump()
            nrm_hist.append(ch_nrm.n)

            def f_otr(t, nm=ch_nrm.n, tev=ch_trev.n):
                t.wait_ge(ch_nrm.sem, nm)
                t.wait_ge(ch_trev.sem, tev)
                for pk4 in range(4):
                    inst = t.transpose(
                        ps_tr[:, pk4 * 64:(pk4 + 1) * 64],
                        otn[0:64, pk4 * 128:(pk4 + 1) * 128],
                        ident_sb[0:64, 0:64])
                ch_tr.inc(inst)
            P("tensor", f_otr)
            ch_tr.bump()

            def f_orev(v, gbuf=gbuf, trc=ch_tr.n, chod=ch_ods[gbuf],
                       ods=ch_ods[gbuf].n):
                v.wait_ge(ch_tr.sem, trc)
                if ods:
                    v.wait_ge(chod.sem, ods)
                inst = v.tensor_copy(orow[:, gbuf * 256:(gbuf + 1) * 256],
                                     ps_tr[:, 0:256])
                ch_trev.inc(inst)
                ch_orw.inc(v.drain())
            P("vector", f_orev)
            ch_trev.bump()
            ch_orw.bump()

            def f_osp(s, n=n, grp=grp, gbuf=gbuf, cho=ch_ods[gbuf],
                      orw=ch_orw.n):
                s.wait_ge(ch_orw.sem, orw)
                cho.inc(s.dma_start(
                    out=od2[n % 2, grp * 512:(grp + 1) * 512, :].rearrange(
                        "(pk p) e -> p pk e", p=128),
                    in_=orow[:, gbuf * 256:(gbuf + 1) * 256].rearrange(
                        "p (pk e) -> p pk e", pk=4)))
            P("sync", f_osp)
            ch_ods[gbuf].bump()
        ODS_DONE = (ch_ods[0].n, ch_ods[1].n)

        if LVL >= 3:
            for rt in range(4):
                grt = n * 4 + rt
                zbuf = grt % 2
                zi = zin[:, zbuf * D:(zbuf + 1) * D]
                zt_ = zlT_t[:, zbuf * 1024:(zbuf + 1) * 1024]
                zbt = zb[:, zbuf * D:(zbuf + 1) * D]
                ot_ = x1t[:, zbuf * D:(zbuf + 1) * D]
                sl = slice(zbuf, zbuf + 1)
                chz = ch_zld[zbuf]

                def f_zld(s, n=n, rt=rt, zbuf=zbuf, chz=chz, ods=ODS_DONE,
                          need=ch_z.n):
                    s.wait_ge(ch_ods[0].sem, ods[0])
                    s.wait_ge(ch_ods[1].sem, ods[1])
                    if need:
                        s.wait_ge(ch_z.sem, need)
                    rowz = zin[:].ap[0][0]
                    for h1i in range(4):
                        src = bass.AP(
                            tensor=od2.tensor,
                            offset=(n % 2) * 8192 * 64 + rt * 4 * 64 + h1i * 64,
                            ap=[[16384, 32], [1024, 16], [1, 64]])
                        dstz = bass.AP(
                            tensor=zin[:].tensor,
                            offset=h1i * 32 * rowz + zbuf * D,
                            ap=[[rowz, 32], [64, 16], [1, 64]])
                        chz.inc(s.dma_start(out=dstz, in_=src))
                P("sync", f_zld)
                chz.bump(4)
                ZLD = chz.n

                def f_ztr(t, zi=zi, chz=chz, zld=ZLD, tev=ch_trev.n):
                    t.wait_ge(chz.sem, zld)
                    t.wait_ge(ch_trev.sem, tev)
                    for ft in range(8):
                        inst = t.transpose(ps_tr[:, ft * 128:(ft + 1) * 128],
                                           zi[:, ft * 128:(ft + 1) * 128],
                                           ident_sb[:])
                    ch_tr.inc(inst)
                P("tensor", f_ztr)
                ch_tr.bump()

                def f_ztrev(v, zt_=zt_, trc=ch_tr.n):
                    v.wait_ge(ch_tr.sem, trc)
                    ch_trev.inc(v.tensor_copy(zt_, ps_tr[:]))
                P("vector", f_ztrev)
                ch_trev.bump()
                ZT_RDY = ch_trev.n

                for fc in range(2):
                    psp = ps_p0 if fc == 0 else ps_p1
                    def f_zmm(t, zt_=zt_, fc=fc, psp=psp, zrdy=ZT_RDY,
                              pf=PS_RL_FREE, bob=BOB):
                        t.wait_ge(ch_trev.sem, zrdy)
                        t.wait_ge(ch_zev.sem, pf)
                        t.wait_ge(ch_bob.sem, bob)
                        for ft in range(8):
                            inst = t.matmul(
                                psp[:, :],
                                zt_[:, ft * 128:(ft + 1) * 128],
                                wo_sb[:, ft * D + fc * 512: ft * D + (fc + 1) * 512],
                                start=(ft == 0), stop=(ft == 7))
                        ch_z.inc(inst)
                    P("tensor", f_zmm)
                    ch_z.bump()

                    def f_zev(v, zbt=zbt, fc=fc, psp=psp, zc=ch_z.n):
                        v.wait_ge(ch_z.sem, zc)
                        inst = v.tensor_add(zbt[:, fc * 512:(fc + 1) * 512],
                                            psp[:, :],
                                            bo_bc[:, fc * 512:(fc + 1) * 512])
                        ch_zev.inc(inst)
                    P("vector", f_zev)
                    ch_zev.bump()
                    PS_RL_FREE = ch_zev.n

                ln_stats(P, ch_sum, ch_rst, zbt, sqj[:], sl, s1, s2, mean,
                         var, msq, rstd, epst[:], D)

                def f_xl(s, n=n, rt=rt, zbuf=zbuf, chx=ch_xd[zbuf],
                         need=(ch_vfree.n - 1)):
                    if need > 0:
                        s.wait_ge(ch_vfree.sem, need)
                    chx.inc(s.dma_start(
                        out=xin[:, zbuf * D:(zbuf + 1) * D],
                        in_=xr[n, rt * 128:(rt + 1) * 128, :]))
                P("sync", f_xl)
                ch_xd[zbuf].bump()
                XL = (ch_xd[zbuf], ch_xd[zbuf].n)

                def f_out(v, zbt=zbt, ot_=ot_, zbuf=zbuf, sl=sl, rs=ch_rst.n,
                          xd=XL, stneed=(ch_sto[zbuf], ch_sto[zbuf].n)):
                    v.wait_ge(ch_rst.sem, rs)
                    v.wait_ge(xd[0].sem, xd[1])
                    if stneed[1] > 0:
                        v.wait_ge(stneed[0].sem, stneed[1])
                    v.tensor_scalar(zbt, zbt, mean[:, sl], rstd[:, sl],
                                    AL.subtract, AL.mult)
                    v.drain()
                    ch_x1.inc(v.tensor_add(ot_, zbt,
                                           xin[:, zbuf * D:(zbuf + 1) * D]))
                    ch_vfree.inc(v.drain())
                P("vector", f_out)
                ch_x1.bump()
                ch_vfree.bump()

                def f_sto(s, n=n, rt=rt, ot_=ot_, chs=ch_sto[zbuf], xc=ch_x1.n):
                    s.wait_ge(ch_x1.sem, xc)
                    chs.inc(s.dma_start(
                        out=x1[n * 512 + rt * 128: n * 512 + (rt + 1) * 128, :],
                        in_=ot_))
                P("sync", f_sto)
                ch_sto[zbuf].bump()

    if LVL < 3:
        chd = ch_sto[0]
        nd = 3 if LVL >= 1 else 1
        chd.bump(nd)
        _dma_tots = [(ch.sem, ch.n) for ch in
                     (ch_vsp[0], ch_vsp[1], ch_vld[0], ch_vld[1],
                      ch_ods[0], ch_ods[1], ch_xd[0], ch_xd[1])]
        def f_dbg(s, lvl=LVL, chd=chd, tot=chd.n, stk=ch_stk.n,
                  tots=tuple(_dma_tots)):
            s.wait_ge(ch_stk.sem, stk)
            for sem_, v_ in tots:
                if v_:
                    s.wait_ge(sem_, v_)
            chd.inc(s.dma_start(out=dbg[0:64, 0:8192], in_=qst[:]))
            if lvl >= 1:
                chd.inc(s.dma_start(out=dbg[0:64, 8192:16384], in_=kst[:]))
                chd.inc(s.dma_start(
                    out=dbg[:, 16384:16640].rearrange("p (g e) -> p g e", g=4),
                    in_=od2[1, 0:512, :].rearrange("(g p) e -> p g e", p=128)))
            s.wait_ge(chd.sem, tot)
        P("sync", f_dbg)
        with nc.allow_low_precision("debug"):
            with nc.Block() as block:
                P.emit(block)
        for _ch in _chans:
            assert _ch.emitted == _ch.n, (
                f"channel {_ch.name}: emitted {_ch.emitted} != planned {_ch.n}")
        st.close()
        return nc

    def f_fin(s, tots=tuple((ch.sem, ch.n) for ch in ch_sto)):
        for sem, v_ in tots:
            s.wait_ge(sem, v_)
    P("sync", f_fin)

    with nc.allow_low_precision("bf16 matmul operands / attention weights; "
                                "fp32 accumulation and LN stats throughout"):
        with nc.Block() as block:
            P.emit(block)
    for _ch in _chans:
        assert _ch.emitted == _ch.n, (
            f"channel {_ch.name}: emitted {_ch.emitted} != planned {_ch.n}")
    st.close()
    return nc



def build_ffn_program(dbg=False):
    nc = bass.Bass("TRN2", target_bir_lowering=False, debug=False)
    xs = nc.dram_tensor("xs", [S, D], F32, kind="ExternalInput").ap()
    wd = {}
    for nm in ("wfT", "ufT", "wiT", "uiT", "wcT", "ucT", "woT", "uoT"):
        wd[nm] = nc.dram_tensor(nm, [D, D], F32, kind="ExternalInput").ap()
    bfv = nc.dram_tensor("bfv", [1, D], F32, kind="ExternalInput").ap()
    biv = nc.dram_tensor("biv", [1, D], F32, kind="ExternalInput").ap()
    bcv = nc.dram_tensor("bcv", [1, D], F32, kind="ExternalInput").ap()
    bov = nc.dram_tensor("bov", [1, D], F32, kind="ExternalInput").ap()
    incv = nc.dram_tensor("incv", [1, D], F32, kind="ExternalInput").ap()
    ident = nc.dram_tensor("ident", [128, 128], F32, kind="ExternalInput").ap()
    out = nc.dram_tensor("out", [S, D], F32, kind="ExternalOutput").ap()
    xt_scr = nc.dram_tensor("xt_scr", [D, S], BF16,
                            kind="ExternalOutput" if dbg else "Internal").ap()
    dbg_c = (nc.dram_tensor("dbg_c", [128, 4096 + 96], F32,
                            kind="ExternalOutput").ap() if dbg else None)
    dbg_h = (nc.dram_tensor("dbg_h", [128, 32 * D], BF16,
                            kind="ExternalOutput").ap() if dbg else None)

    st = ExitStack()
    sb = lambda name, shape, dt: st.enter_context(nc.sbuf_tensor(name, shape, dt))
    psa = lambda name, shape, dt: st.enter_context(nc.psum_tensor(name, shape, dt))

    w_sb = {}
    for nm in ("wf", "uf", "wi", "ui", "wc", "uc"):
        w_sb[nm] = sb(nm + "_sb", [128, 8 * D], BF16)
    xtT_sl = sb("xtT_sl", [128, 2 * 8 * 512], BF16)
    cT = sb("cT", [128, 8 * 512], F32)
    cT_bf = sb("cT_bf", [128, 8 * 512], BF16)
    gat = sb("gat", [128, 3 * 512], F32)
    xin = sb("xin", [128, 2 * D], F32)
    xln_sb = sb("xln_sb", [128, 2 * D], BF16)
    x1t = sb("x1t", [128, 2 * D], F32)
    sqj = sb("sqj", [128, D], BF16)
    biases = sb("biases", [128, 4 * 8], F32)
    ident_sb = sb("ident_sb", [128, 128], BF16)
    onesf = sb("onesf", [128, 512], F32)
    bo2_sb = sb("bo2_sb", [1, D], BF16)
    onesr2 = sb("onesr2", [1, 128], BF16)
    epst = sb("epst", [128, 1], F32)
    s1 = sb("s1", [128, 2], F32)
    s2 = sb("s2", [128, 2], F32)
    mean = sb("mean", [128, 2], F32)
    var = sb("var", [128, 2], F32)
    msq = sb("msq", [128, 2], F32)
    rstd = sb("rstd", [128, 2], F32)
    hmean = sb("hmean", [128, 32], F32)
    hvar = sb("hvar", [128, 32], F32)
    hrstd = sb("hrstd", [128, 32], F32)
    hms = sb("hms", [128, 32], F32)

    _h_order = ("wf", "uf", "wi", "ui")
    h_view = lambda rt: w_sb[_h_order[rt // 8]][:, (rt % 8) * D:((rt % 8) + 1) * D]
    wo_sl = w_sb["wc"]
    uo_sl = w_sb["uc"]
    xr_sl = gat[:].bitcast(BF16)

    ps_tr = psa("ps_tr", [128, 1024], BF16)
    ps_g = psa("ps_g", [128, 2 * 512], F32)
    ps_h = psa("ps_h", [128, 2 * 512], F32)

    _chans = []
    def c(name, step=1):
        ch = Chan(st, nc, name, step)
        _chans.append(ch)
        return ch
    ch_xd = (c("xd0", 16), c("xd1", 16))
    ch_wd = c("wd", 16)
    ch_cd = c("cd", 16)
    ch_sum = c("sum")
    ch_rst = c("rst")
    ch_xln = c("xln")
    ch_tr = c("tr")
    ch_trev = c("trev")
    ch_trd = (c("trd0", 16), c("trd1", 16))
    ch_xtl = (c("xtl0", 16), c("xtl1", 16))
    ch_xrl = (c("xrl0", 16), c("xrl1", 16))
    ch_g = c("g")
    ch_gev = c("gev")
    ch_cup = c("cup")
    ch_cbf = c("cbf")
    ch_h = c("h")
    ch_hev = c("hev")
    ch_hst = c("hst")
    ch_ho = c("ho")
    ch_sto = (c("sto0", 16), c("sto1", 16))
    ch_vfree = c("vfree")

    P = Plan()

    def g_consts(g):
        ch_cd.inc(g.dma_start(out=ident_sb[:], in_=ident[:]))
        ch_cd.inc(g.dma_start(out=bo2_sb[:], in_=bov[:]))
        for i, bsrc in enumerate((bfv, biv, bcv, incv)):
            ch_cd.inc(g.dma_start(
                out=biases[:, i * 8:(i + 1) * 8],
                in_=bsrc.rearrange("o (mt p) -> p (o mt)", p=128),
                allow_slow_non_contiguous=True))
        for nm in ("wf", "uf", "wi", "ui", "wc", "uc"):
            ch_cd.inc(g.dma_start(
                out=w_sb[nm][:].rearrange("p (ft m) -> p ft m", ft=8),
                in_=wd[nm + "T"].rearrange("(ft p) m -> p ft m", p=128)))
    P("gpsimd", g_consts)
    ch_cd.bump(12)
    CONSTS = ch_cd.n

    NT = S // 128
    tr_hist = []
    trd_hist = []
    for tt in range(NT):
        buf = tt % 2
        xi = xin[:, buf * D:(buf + 1) * D]
        xl = xln_sb[:, buf * D:(buf + 1) * D]
        sl = slice(buf, buf + 1)

        chx = ch_xd[buf]

        def f_load(s, tt=tt, xi=xi, chx=chx, need=ch_vfree.n - 1):
            if tt >= 2:
                s.wait_ge(ch_vfree.sem, need)
            chx.inc(s.dma_start(out=xi, in_=xs[tt * 128:(tt + 1) * 128, :]))
        P("sync", f_load)
        chx.bump()

        def f_pre(v, tt=tt, chx=chx, xd=chx.n):
            v.wait_ge(chx.sem, xd)
            if tt == 0:
                v.memset(epst[:], EPS)
                v.memset(onesf[:], 1.0)
                v.memset(onesr2[:], 1.0)
        P("vector", f_pre)
        ln_stats(P, ch_sum, ch_rst, xi, sqj[:], sl, s1, s2, mean, var, msq,
                 rstd, epst[:], D)

        def f_apply(v, xi=xi, xl=xl, sl=sl, rs=ch_rst.n,
                    trneed=(tr_hist[tt - 2] if tt >= 2 else 0),
                    spneed=(trd_hist[tt - 2] if tt >= 2 else None)):
            v.wait_ge(ch_rst.sem, rs)
            if trneed:
                v.wait_ge(ch_tr.sem, trneed)
            if spneed:
                v.wait_ge(spneed[0].sem, spneed[1])
            ch_xln.inc(v.tensor_scalar(xl, xi, mean[:, sl], rstd[:, sl],
                                       AL.subtract, AL.mult))
            ch_vfree.inc(v.drain())
        P("vector", f_apply)
        ch_xln.bump()
        ch_vfree.bump()

        def f_tr(t, tt=tt, xl=xl, xc=ch_xln.n, tev=ch_trev.n, cd=CONSTS):
            t.wait_ge(ch_xln.sem, xc)
            if tt == 0:
                t.wait_ge(ch_cd.sem, cd)
            else:
                t.wait_ge(ch_trev.sem, tev)
            for ft in range(8):
                inst = t.transpose(ps_tr[:, ft * 128:(ft + 1) * 128],
                                   xl[:, ft * 128:(ft + 1) * 128], ident_sb[:])
            ch_tr.inc(inst)
        P("tensor", f_tr)
        ch_tr.bump()
        tr_hist.append(ch_tr.n)

        def f_trev(v, tt=tt, buf=buf, trc=ch_tr.n):
            v.wait_ge(ch_tr.sem, trc)
            ch_trev.inc(v.tensor_copy(xln_sb[:, buf * D:(buf + 1) * D],
                                      ps_tr[:]))
        P("vector", f_trev)
        ch_trev.bump()

        cht = ch_trd[buf]

        def f_sp(s, tt=tt, buf=buf, cht=cht, tev=ch_trev.n):
            s.wait_ge(ch_trev.sem, tev)
            cht.inc(s.dma_start(
                out=xt_scr[:, tt * 128:(tt + 1) * 128].rearrange(
                    "(ft p) r -> p ft r", p=128),
                in_=xln_sb[:, buf * D:(buf + 1) * D].rearrange(
                    "p (ft r) -> p ft r", ft=8)))
        P("sync", f_sp)
        cht.bump()
        trd_hist.append((cht, cht.n))
    LN3 = tuple((ch.sem, ch.n) for ch in ch_trd)

    def f_cinit(v, cd=CONSTS):
        v.wait_ge(ch_cd.sem, cd)
        v.drain()
        for mt in range(8):
            inst = v.tensor_scalar(cT[:, mt * 512:(mt + 1) * 512], onesf[:],
                                   biases[:, 24 + mt:24 + mt + 1], None, AL.mult)
        ch_cup.inc(inst)
    P("vector", f_cinit)
    ch_cup.bump()

    GATES = (("wf", "uf", 0, AF.Sigmoid), ("wi", "ui", 8, AF.Sigmoid),
             ("wc", "uc", 16, AF.Tanh))
    for t_step in range(8):
        buf = t_step % 2

        chxt = ch_xtl[buf]

        def f_xtl(s, t_step=t_step, buf=buf, chxt=chxt, ln=LN3, need=ch_g.n):
            for sem, v_ in ln:
                s.wait_ge(sem, v_)
            if t_step >= 2:
                s.wait_ge(ch_g.sem, need)
            chxt.inc(s.dma_start(
                out=xtT_sl[:, buf * 4096:(buf + 1) * 4096].rearrange(
                    "p (ft q) -> p ft q", ft=8),
                in_=xt_scr[:, t_step * 512:(t_step + 1) * 512].rearrange(
                    "(ft p) q -> p ft q", p=128)))
        P("sync", f_xtl)
        chxt.bump()
        XTL = (chxt, chxt.n)

        def f_cbf(v, cu=ch_cup.n):
            v.wait_ge(ch_cup.sem, cu)
            ch_cbf.inc(v.tensor_copy(cT_bf[:], cT[:]))
        P("vector", f_cbf)
        ch_cbf.bump()
        CBF = ch_cbf.n

        for mt in range(8):
            for gi, (wn, un, bcol, func) in enumerate(GATES):
                kglob = (t_step * 8 + mt) * 3 + gi
                pg = kglob % 2

                def f_gmm(t, wn=wn, un=un, mt=mt, pg=pg, buf=buf, xtl=XTL,
                          cbf=CBF, gev=(ch_gev.n - 1)):
                    t.wait_ge(xtl[0].sem, xtl[1])
                    t.wait_ge(ch_cbf.sem, cbf)
                    if gev > 0:
                        t.wait_ge(ch_gev.sem, gev)
                    for ft in range(8):
                        t.matmul(ps_g[:, pg * 512:(pg + 1) * 512],
                                 w_sb[wn][:, ft * D + mt * 128: ft * D + mt * 128 + 128],
                                 xtT_sl[:, buf * 4096 + ft * 512:
                                        buf * 4096 + (ft + 1) * 512],
                                 start=(ft == 0), stop=False)
                    for m2 in range(8):
                        inst = t.matmul(
                            ps_g[:, pg * 512:(pg + 1) * 512],
                            w_sb[un][:, m2 * D + mt * 128: m2 * D + mt * 128 + 128],
                            cT_bf[:, m2 * 512:(m2 + 1) * 512],
                            start=False, stop=(m2 == 7))
                    ch_g.inc(inst)
                P("tensor", f_gmm)
                ch_g.bump()

                def f_gev(sc, gi=gi, pg=pg, func=func, bcol=bcol, mt=mt,
                          gc=ch_g.n, cup=(ch_cup.n if gi == 0 else 0)):
                    sc.wait_ge(ch_g.sem, gc)
                    if cup:
                        sc.wait_ge(ch_cup.sem, cup)
                    ch_gev.inc(sc.activation(
                        gat[:, gi * 512:(gi + 1) * 512],
                        ps_g[:, pg * 512:(pg + 1) * 512], func,
                        bias=biases[:, bcol + mt:bcol + mt + 1]))
                P("scalar", f_gev)
                ch_gev.bump()

            def f_cup(v, mt=mt, ge=ch_gev.n):
                v.wait_ge(ch_gev.sem, ge)
                cslice = cT[:, mt * 512:(mt + 1) * 512]
                v.tensor_mul(cslice, cslice, gat[:, 0:512])
                v.tensor_mul(gat[:, 512:1024], gat[:, 512:1024], gat[:, 1024:1536])
                v.drain()
                ch_cup.inc(v.tensor_add(cslice, cslice, gat[:, 512:1024]))
            P("vector", f_cup)
            ch_cup.bump()
    SCAN = ch_cup.n
    SCAN_G = ch_g.n

    def f_cfin(v, sc_=SCAN):
        v.wait_ge(ch_cup.sem, sc_)
        ch_cbf.inc(v.tensor_copy(cT_bf[:], cT[:]))
    P("vector", f_cfin)
    ch_cbf.bump()
    CFIN = ch_cbf.n

    for fc in range(2):
        def f_wol(g, fc=fc, sg=SCAN_G):
            g.wait_ge(ch_g.sem, sg)
            ch_wd.inc(g.dma_start(
                out=wo_sl[:, fc * 4096:(fc + 1) * 4096].rearrange(
                    "p (ft q) -> p ft q", ft=8),
                in_=wd["woT"][:, fc * 512:(fc + 1) * 512].rearrange(
                    "(ft p) q -> p ft q", p=128)))
            ch_wd.inc(g.dma_start(
                out=uo_sl[:, fc * 4096:(fc + 1) * 4096].rearrange(
                    "p (ft q) -> p ft q", ft=8),
                in_=wd["uoT"][:, fc * 512:(fc + 1) * 512].rearrange(
                    "(ft p) q -> p ft q", p=128)))
        P("gpsimd", f_wol)
        ch_wd.bump(2)
    WOL = ch_wd.n

    for rt in range(NT):
        buf = rt % 2
        bsl = rt % 4

        chxr = ch_xrl[buf]

        def f_xrl(g, rt=rt, buf=buf, chxr=chxr, sc_=SCAN, need=(ch_h.n - 1)):
            g.wait_ge(ch_cup.sem, sc_)
            if rt >= 2 and need > 0:
                g.wait_ge(ch_h.sem, need)
            chxr.inc(g.dma_start(
                out=xr_sl[:, buf * 1024:(buf + 1) * 1024].rearrange(
                    "p (ft r) -> p ft r", ft=8),
                in_=xt_scr[:, rt * 128:(rt + 1) * 128].rearrange(
                    "(ft p) r -> p ft r", p=128)))
        P("gpsimd", f_xrl)
        chxr.bump()
        XRL = (chxr, chxr.n)

        for fc in range(2):
            ph = (rt * 2 + fc) % 2

            def f_hmm(t, rt=rt, buf=buf, fc=fc, ph=ph, bsl=bsl, xrl=XRL,
                      cfin=CFIN, wol=WOL, hev=(ch_hev.n - 1)):
                t.wait_ge(xrl[0].sem, xrl[1])
                t.wait_ge(ch_cbf.sem, cfin)
                t.wait_ge(ch_wd.sem, wol)
                if hev > 0:
                    t.wait_ge(ch_hev.sem, hev)
                for ft in range(8):
                    t.matmul(ps_h[:, ph * 512:(ph + 1) * 512],
                             xr_sl[:, buf * 1024 + ft * 128:
                                   buf * 1024 + (ft + 1) * 128],
                             wo_sl[:, fc * 4096 + ft * 512:
                                   fc * 4096 + (ft + 1) * 512],
                             start=(ft == 0), stop=False)
                for m2 in range(8):
                    t.matmul(ps_h[:, ph * 512:(ph + 1) * 512],
                             cT_bf[:, m2 * 512 + bsl * 128:
                                   m2 * 512 + (bsl + 1) * 128],
                             uo_sl[:, fc * 4096 + m2 * 512:
                                   fc * 4096 + (m2 + 1) * 512],
                             start=False, stop=False)
                inst = t.matmul(ps_h[:, ph * 512:(ph + 1) * 512], onesr2[:],
                                bo2_sb[0:1, fc * 512:(fc + 1) * 512],
                                start=False, stop=True)
                ch_h.inc(inst)
            P("tensor", f_hmm)
            ch_h.bump()

            def f_hev(sc, rt=rt, fc=fc, ph=ph, hc=ch_h.n):
                sc.wait_ge(ch_h.sem, hc)
                ch_hev.inc(sc.activation(
                    h_view(rt)[:, fc * 512:(fc + 1) * 512],
                    ps_h[:, ph * 512:(ph + 1) * 512], AF.Tanh))
            P("scalar", f_hev)
            ch_hev.bump()

        def f_hstat(v, rt=rt, he=ch_hev.n):
            v.wait_ge(ch_hev.sem, he)
            v.drain()
            hrow = h_view(rt)
            v.reduce_sum(hmean[:, rt:rt + 1], hrow, axis=AX)
            ch_hst.inc(v.scalar_tensor_tensor(
                sqj[:], hrow, 1.0, hrow, AL.mult, AL.mult,
                accum_out=hvar[:, rt:rt + 1]))
        P("vector", f_hstat)
        ch_hst.bump()
    HSTAT = ch_hst.n

    def f_hpre(v, hs=HSTAT):
        v.wait_ge(ch_hst.sem, hs)
        v.scalar_tensor_tensor(hms[:], hmean[:], 1.0 / (D * D), hmean[:],
                               AL.mult, AL.mult)
        v.tensor_scalar_mul(hvar[:], hvar[:], 1.0 / D)
        v.tensor_scalar_mul(hmean[:], hmean[:], 1.0 / D)
        v.drain()
        ch_hst.inc(v.tensor_sub(hvar[:], hvar[:], hms[:]))
    P("vector", f_hpre)
    ch_hst.bump()

    def f_hrstd(sc, hs=ch_hst.n):
        sc.wait_ge(ch_hst.sem, hs)
        sc.activation(hms[:], hvar[:], AF.Ln, bias=epst[:])
        sc.drain()
        ch_rst.inc(sc.activation(hrstd[:], hms[:], AF.Exp, scale=-0.5))
    P("scalar", f_hrstd)
    ch_rst.bump()
    HR = ch_rst.n

    for rt in range(NT):
        buf = rt % 2
        xi = xin[:, buf * D:(buf + 1) * D]
        ot = x1t[:, buf * D:(buf + 1) * D]

        chx2 = ch_xd[buf]

        def f_xl2(s, rt=rt, xi=xi, chx2=chx2, need=(ch_ho.n - 1)):
            if rt >= 2 and need > 0:
                s.wait_ge(ch_ho.sem, need)
            chx2.inc(s.dma_start(out=xi, in_=xs[rt * 128:(rt + 1) * 128, :]))
        P("sync", f_xl2)
        chx2.bump()
        XD2 = (chx2, chx2.n)

        def f_happ(v, rt=rt, xi=xi, ot=ot, hr=HR, xd=XD2,
                   stneed=(ch_sto[rt % 2], ch_sto[rt % 2].n)):
            v.wait_ge(ch_rst.sem, hr)
            v.wait_ge(xd[0].sem, xd[1])
            if stneed[1] > 0:
                v.wait_ge(stneed[0].sem, stneed[1])
            v.tensor_scalar(ot, h_view(rt), hmean[:, rt:rt + 1],
                            hrstd[:, rt:rt + 1], AL.subtract, AL.mult)
            v.drain()
            ch_ho.inc(v.tensor_add(ot, ot, xi))
        P("vector", f_happ)
        ch_ho.bump()

        def f_sto(s, rt=rt, ot=ot, chs=ch_sto[rt % 2], hc=ch_ho.n):
            s.wait_ge(ch_ho.sem, hc)
            chs.inc(s.dma_start(out=out[rt * 128:(rt + 1) * 128, :], in_=ot))
        P("sync", f_sto)
        ch_sto[rt % 2].bump()

    if dbg:
        ch_sto[0].bump(3)
        def f_dump(s, tots=tuple((ch.sem, ch.n) for ch in ch_sto),
                   chd=ch_sto[0], tot=ch_sto[0].n):
            for sem, v_ in tots[1:]:
                s.wait_ge(sem, v_)
            s.wait_ge(tots[0][0], tots[0][1] - 48)
            chd.inc(s.dma_start(out=dbg_c[:, 0:4096], in_=cT[:]))
            chd.inc(s.dma_start(out=dbg_c[:, 4096:4128], in_=hmean[:]))
            chd.inc(s.dma_start(out=dbg_c[:, 4128:4160], in_=hrstd[:]))
            s.wait_ge(chd.sem, tot)
        P("sync", f_dump)

        ch_sto[1].bump(32)
        def f_dumph(s, chd=ch_sto[1], tot=ch_sto[1].n, base=ch_sto[1].n - 512):
            for rt in range(NT):
                chd.inc(s.dma_start(out=dbg_h[:, rt * D:(rt + 1) * D],
                                    in_=h_view(rt)))
            s.wait_ge(chd.sem, tot)
        P("sync", f_dumph)

    def f_fin(s, tots=tuple((ch.sem, ch.n) for ch in ch_sto)):
        for sem, v_ in tots:
            s.wait_ge(sem, v_)
    P("sync", f_fin)

    with nc.allow_low_precision("bf16 matmul operands / attention weights; "
                                "fp32 accumulation and LN stats throughout"):
        with nc.Block() as block:
            P.emit(block)
    for _ch in _chans:
        assert _ch.emitted == _ch.n, (
            f"channel {_ch.name}: emitted {_ch.emitted} != planned {_ch.n}")
    st.close()
    return nc


def build_fused_program():
    LVL = 3
    nc = bass.Bass("TRN2", target_bir_lowering=False, debug=False)
    x = nc.dram_tensor("x", [B, 512, D], F32, kind="ExternalInput").ap()
    wqT = nc.dram_tensor("wqT", [D, D], F32, kind="ExternalInput").ap()
    wkT = nc.dram_tensor("wkT", [D, D], F32, kind="ExternalInput").ap()
    wvT = nc.dram_tensor("wvT", [D, D], F32, kind="ExternalInput").ap()
    woT = nc.dram_tensor("woT", [D, D], F32, kind="ExternalInput").ap()
    bo = nc.dram_tensor("bo", [1, D], F32, kind="ExternalInput").ap()
    xr = nc.dram_tensor("xr", [B, 512, D], F32, kind="ExternalInput").ap()
    ident = nc.dram_tensor("ident", [128, 128], F32, kind="ExternalInput").ap()
    bsel = nc.dram_tensor("bsel", [8, 128], F32, kind="ExternalInput").ap()
    rowneg = nc.dram_tensor("rowneg", [1, 128], F32, kind="ExternalInput").ap()
    rowpos = nc.dram_tensor("rowpos", [1, 128], F32, kind="ExternalInput").ap()
    maskq = nc.dram_tensor("maskq", [9, 512 * 16], F32, kind="ExternalInput").ap()
    maskk = nc.dram_tensor("maskk", [9, 512 * 16], F32, kind="ExternalInput").ap()
    x1 = nc.dram_tensor("x1", [S, D], F32, kind="Internal").ap()
    out = nc.dram_tensor("out", [S, D], F32, kind="ExternalOutput").ap()
    wd = {}
    for nm in ("wfT", "ufT", "wiT", "uiT", "wcT", "ucT", "woT2", "uoT"):
        wd[nm] = nc.dram_tensor(nm, [D, D], F32, kind="ExternalInput").ap()
    bfv = nc.dram_tensor("bfv", [1, D], F32, kind="ExternalInput").ap()
    biv = nc.dram_tensor("biv", [1, D], F32, kind="ExternalInput").ap()
    bcv = nc.dram_tensor("bcv", [1, D], F32, kind="ExternalInput").ap()
    bov = nc.dram_tensor("bov", [1, D], F32, kind="ExternalInput").ap()
    incv = nc.dram_tensor("incv", [1, D], F32, kind="ExternalInput").ap()
    xt_scr = nc.dram_tensor("xt_scr", [D, S], BF16, kind="Internal").ap()
    v_scr = nc.dram_tensor("v_scr", [2, 512, D], BF16, kind="Internal").ap()
    od2 = nc.dram_tensor("od2", [2, 512 * 16, 64], BF16, kind="Internal").ap()

    st = ExitStack()
    sb = lambda name, shape, dt: st.enter_context(nc.sbuf_tensor(name, shape, dt))
    psa = lambda name, shape, dt: st.enter_context(nc.psum_tensor(name, shape, dt))

    wq_sb = sb("wq_sb", [128, 8 * D], BF16)
    wk_sb = sb("wk_sb", [128, 8 * D], BF16)
    wv_sb = sb("wv_sb", [128, 8 * D], BF16)
    wo_sb = sb("wo_sb", [128, 8 * D], BF16)
    xin = sb("xin", [128, 2 * D], F32)
    xln_sb = sb("xln_sb", [128, 2 * D], BF16)
    xlnT_t = sb("xlnT_t", [128, 2 * 1024], BF16)
    qtok = sb("qtok", [128, 2 * D], BF16)
    ktok = sb("ktok", [128, 2 * D], BF16)
    vtok = sb("vtok", [128, 2 * D], BF16)
    qst = sb("qst", [128, 512 * 16], BF16)
    kst = sb("kst", [128, 512 * 16], BF16)
    vst = sb("vst", [128, 4 * 4 * 65], BF16)
    pt_sb = sb("pt_sb", [128, 2 * 512], BF16)
    otn = sb("otn", [128, 2 * 512], BF16)
    orow = sb("orow", [128, 2 * 256], BF16)
    ones64 = sb("ones64", [128, 64], BF16)
    zin = sb("zin", [128, 2 * D], BF16)
    zlT_t = sb("zlT_t", [128, 2 * 1024], BF16)
    zb = sb("zb", [128, 2 * D], F32)
    x1t = sb("x1t", [128, 2 * D], F32)
    sqj = sb("sqj", [128, D], BF16)
    bo_bc = sb("bo_bc", [128, D], F32)
    bo_sb = sb("bo_sb", [128, D], BF16)
    ident_sb = sb("ident_sb", [128, 128], BF16)
    bsel_sb = sb("bsel_sb", [8, 128], BF16)
    rneg_sb = sb("rneg_sb", [1, 128], BF16)
    rpos_sb = sb("rpos_sb", [1, 128], BF16)
    epst = sb("epst", [128, 1], F32)
    s1 = sb("s1", [128, 2], F32)
    s2 = sb("s2", [128, 2], F32)
    mean = sb("mean", [128, 2], F32)
    var = sb("var", [128, 2], F32)
    msq = sb("msq", [128, 2], F32)
    rlc = sb("rlc", [128, 4], F32)
    rstd = sb("rstd", [128, 2], F32)

    gat = sb("gat", [128, 3 * 512], F32)
    junk2 = gat[:].bitcast(BF16)[:, 0:1024]

    ps_p0 = psa("ps_p0", [128, 512], F32)
    ps_p1 = psa("ps_p1", [128, 512], F32)
    ps_a = psa("ps_a", [128, 2 * 512], F32)
    ps_ot = psa("ps_ot", [128, 512], F32)
    ps_rl = psa("ps_rl", [128, 512], F32)
    ps_tr = psa("ps_tr", [128, 1024], BF16)
    ps_trb = psa("ps_trb", [128, 1024], BF16)

    _chans = []
    def c(name, step=1):
        ch = Chan(st, nc, name, step)
        _chans.append(ch)
        return ch
    ch_xd = (c("xd0", 16), c("xd1", 16))
    ch_cd = c("cd", 16)
    ch_sum = c("sum")
    ch_rst = c("rst")
    ch_xln = c("xln")
    ch_tr = c("tr")
    ch_trev = c("trev")
    ch_trB = c("trB")
    ch_trevB = c("trevB")
    ch_pj = c("pj")
    ch_pjev = c("pjev")
    ch_vsp = (c("vsp0", 16), c("vsp1", 16))
    ch_stk = c("stk")
    ch_am = c("am")
    ch_exp = c("exp")
    ch_vld = tuple(c(f"vld{i}", 16) for i in range(4))
    ch_pv = c("pv")
    ch_rl = c("rl")
    ch_rlb = c("rlb")
    ch_rlbe = c("rlbe")
    ch_nrm = c("nrm")
    ch_orw = c("orw")
    ch_ods = (c("ods0", 16), c("ods1", 16))
    ch_zld = (c("zld0", 16), c("zld1", 16))
    ch_z = c("z")
    ch_zev = c("zev")
    ch_bob = c("bob")
    ch_x1 = c("x1c")
    ch_sto = (c("sto0", 16), c("sto1", 16))
    ch_vfree = c("vfree")

    ps_trx = (ps_tr, ps_trb)
    ch_trx = (ch_tr, ch_trB)
    ch_trevx = (ch_trev, ch_trevB)
    _trs = [0]
    def tr_slot():
        b = _trs[0] % 2
        _trs[0] += 1
        return b

    P = Plan()

    def g_consts(g):
        for out_ap, in_ap in ((ident_sb[:], ident[:]), (bsel_sb[:], bsel[:]),
                              (rneg_sb[:], rowneg[:]), (rpos_sb[:], rowpos[:]),
                              (bo_sb[64:65, :], bo[:])):
            ch_cd.inc(g.dma_start(out=out_ap, in_=in_ap))
        ch_cd.inc(g.dma_start(out=qst[64:73, :], in_=maskq[:]))
        ch_cd.inc(g.dma_start(out=kst[64:73, :], in_=maskk[:]))
        for w_sb_, wdram in ((wq_sb, wqT), (wk_sb, wkT), (wv_sb, wvT),
                             (wo_sb, woT)):
            ch_cd.inc(g.dma_start(
                out=w_sb_[:].rearrange("p (ft m) -> p ft m", ft=8),
                in_=wdram.rearrange("(ft p) m -> p ft m", p=128)))
    P("gpsimd", g_consts)
    ch_cd.bump(11)
    CONSTS = ch_cd.n

    def f_ones_init(v, cd=CONSTS):
        v.wait_ge(ch_cd.sem, cd)
        v.memset(epst[:], EPS)
        v.memset(vst[:].rearrange("p (b e) -> p b e", b=16)[:, :, 64:65], 1.0)
        ch_bob.inc(v.memset(ones64[64:65, :], 1.0))
    P("vector", f_ones_init)
    ch_bob.bump()
    ONES_RDY = ch_bob.n

    for half in (0, 1):
        def f_bob_mm(t, half=half, onr=ONES_RDY, ze=ch_zev.n):
            t.wait_ge(ch_bob.sem, onr)
            if ze:
                t.wait_ge(ch_zev.sem, ze)
            ch_z.inc(t.matmul(ps_rl[0:64, :], ones64[64:65, :],
                              bo_sb[64:65, half * 512:(half + 1) * 512],
                              start=True, stop=True, tile_position=(64, 0)))
        P("tensor", f_bob_mm)
        ch_z.bump()

        def f_bob_ev(sc, half=half, zc=ch_z.n):
            sc.wait_ge(ch_z.sem, zc)
            inst = sc.copy(bo_bc[0:64, half * 512:(half + 1) * 512], ps_rl[0:64, :])
            ch_zev.inc(inst)
        P("scalar", f_bob_ev)
        ch_zev.bump()
    for half in (0, 1):
        def f_bob_mm2(t, half=half, ze=ch_zev.n):
            t.wait_ge(ch_zev.sem, ze)
            ch_z.inc(t.matmul(ps_rl[64:128, :], ones64[64:65, 0:64],
                              bo_sb[64:65, half * 512:(half + 1) * 512],
                              start=True, stop=True, tile_position=(64, 64)))
        P("tensor", f_bob_mm2)
        ch_z.bump()

        def f_bob_ev2(sc, half=half, zc=ch_z.n):
            sc.wait_ge(ch_z.sem, zc)
            inst = sc.copy(bo_bc[64:128, half * 512:(half + 1) * 512],
                           ps_rl[64:128, :])
            ch_zev.inc(inst)
            ch_bob.inc(sc.drain())
        P("scalar", f_bob_ev2)
        ch_zev.bump()
        ch_bob.bump()
    BOB = ch_bob.n
    PS_RL_FREE = ch_zev.n
    BOBZEV = ch_zev.n
    ps_otx = (ps_ot, ps_rl)
    psfree = {0: 0, 1: 0}

    NTIL = 4
    NPK = 64
    exp_hist = []
    nrm_hist = []
    PJ_EV = {}

    def emit_P2(n, rt, ODS_DONE):
        nonlocal PS_RL_FREE
        grt = n * 4 + rt
        zbuf = grt % 2
        zi = zin[:, zbuf * D:(zbuf + 1) * D]
        zt_ = zlT_t[:, zbuf * 1024:(zbuf + 1) * 1024]
        zbt = zb[:, zbuf * D:(zbuf + 1) * D]
        ot_ = x1t[:, zbuf * D:(zbuf + 1) * D]
        sl = slice(zbuf, zbuf + 1)
        chz = ch_zld[zbuf]

        def f_zld(s, n=n, rt=rt, zbuf=zbuf, chz=chz, ods=ODS_DONE,
                  need=(p2_ztr_hist[-2] if len(p2_ztr_hist) >= 2 else None)):
            s.wait_ge(ch_ods[0].sem, ods[0])
            s.wait_ge(ch_ods[1].sem, ods[1])
            if need:
                s.wait_ge(need[0].sem, need[1])
            rowz = zin[:].ap[0][0]
            for h1i in range(4):
                src = bass.AP(
                    tensor=od2.tensor,
                    offset=(n % 2) * 8192 * 64 + rt * 4 * 64 + h1i * 64,
                    ap=[[16384, 32], [1024, 16], [1, 64]])
                dstz = bass.AP(
                    tensor=zin[:].tensor,
                    offset=h1i * 32 * rowz + zbuf * D,
                    ap=[[rowz, 32], [64, 16], [1, 64]])
                chz.inc(s.dma_start(out=dstz, in_=src))
        P("gpsimd", f_zld)
        chz.bump(4)
        ZLD = chz.n

        trb = tr_slot()
        def f_ztr(t, zi=zi, trb=trb, chz=chz, zld=ZLD,
                  tev=ch_trevx[trb].n):
            t.wait_ge(chz.sem, zld)
            t.wait_ge(ch_trevx[trb].sem, tev)
            for ft in range(8):
                inst = t.transpose(
                    ps_trx[trb][:, ft * 128:(ft + 1) * 128],
                    zi[:, ft * 128:(ft + 1) * 128],
                    ident_sb[:])
            ch_trx[trb].inc(inst)
        P("tensor", f_ztr)
        ch_trx[trb].bump()
        p2_ztr_hist.append((ch_trx[trb], ch_trx[trb].n))

        def f_ztrev(v, zt_=zt_, trb=trb, trc=ch_trx[trb].n):
            v.wait_ge(ch_trx[trb].sem, trc)
            ch_trevx[trb].inc(v.tensor_copy(zt_, ps_trx[trb][:]))
        P("vector", f_ztrev)
        ch_trevx[trb].bump()
        ZT_RDY = (ch_trevx[trb], ch_trevx[trb].n)

        for fc in range(2):
            psp = ps_p0 if fc == 0 else ps_p1
            def f_zmm(t, zt_=zt_, fc=fc, psp=psp, zrdy=ZT_RDY,
                      pf=psfree[fc], bob=BOB):
                t.wait_ge(zrdy[0].sem, zrdy[1])
                if pf:
                    t.wait_ge(ch_zev.sem, pf)
                t.wait_ge(ch_bob.sem, bob)
                for ft in range(8):
                    inst = t.matmul(
                        psp[:, :],
                        zt_[:, ft * 128:(ft + 1) * 128],
                        wo_sb[:, ft * D + fc * 512: ft * D + (fc + 1) * 512],
                        start=(ft == 0), stop=(ft == 7))
                ch_z.inc(inst)
            P("tensor", f_zmm)
            ch_z.bump()

            def f_zev(v, zbt=zbt, fc=fc, psp=psp, zc=ch_z.n):
                v.wait_ge(ch_z.sem, zc)
                inst = v.tensor_add(zbt[:, fc * 512:(fc + 1) * 512],
                                    psp[:, :],
                                    bo_bc[:, fc * 512:(fc + 1) * 512])
                ch_zev.inc(inst)
            P("vector", f_zev)
            ch_zev.bump()
            psfree[fc] = ch_zev.n

        ln_stats_sc(P, ch_rst, zbt, sqj[:], junk2, sl, s1, s2, mean,
                    msq, rstd, D, wait=(ch_zev, ch_zev.n))

        def f_xl(s, n=n, rt=rt, zbuf=zbuf, chx=ch_xd[zbuf],
                 need=xin_last[zbuf]):
            if need > 0:
                s.wait_ge(ch_vfree.sem, need)
            chx.inc(s.dma_start(
                out=xin[:, zbuf * D:(zbuf + 1) * D],
                in_=xr[n, rt * 128:(rt + 1) * 128, :]))
        P("gpsimd", f_xl)
        ch_xd[zbuf].bump()
        XL = (ch_xd[zbuf], ch_xd[zbuf].n)

        def f_out(v, zbt=zbt, ot_=ot_, zbuf=zbuf, sl=sl, rs=ch_rst.n,
                  xd=XL, stneed=(ch_sto[zbuf], ch_sto[zbuf].n)):
            v.wait_ge(ch_rst.sem, rs)
            v.wait_ge(xd[0].sem, xd[1])
            if stneed[1] > 0:
                v.wait_ge(stneed[0].sem, stneed[1])
            v.tensor_scalar(zbt, zbt, mean[:, sl], rstd[:, sl],
                            AL.subtract, AL.mult)
            v.drain()
            ch_x1.inc(v.tensor_add(ot_, zbt,
                                   xin[:, zbuf * D:(zbuf + 1) * D]))
            ch_vfree.inc(v.drain())
        P("vector", f_out)
        ch_x1.bump()
        ch_vfree.bump()
        xin_last[zbuf] = ch_vfree.n

        def f_sto(s, n=n, rt=rt, ot_=ot_, chs=ch_sto[zbuf], xc=ch_x1.n):
            s.wait_ge(ch_x1.sem, xc)
            chs.inc(s.dma_start(
                out=x1[n * 512 + rt * 128: n * 512 + (rt + 1) * 128, :],
                in_=ot_))
        P("gpsimd", f_sto)
        ch_sto[zbuf].bump()
        if rt == 3:
            p2_zld_hist[n] = (ch_zld[0].n, ch_zld[1].n)
            sto_hist[n] = ((ch_sto[0], ch_sto[0].n), (ch_sto[1], ch_sto[1].n))


    g_vld_hist = {}
    p2_zld_hist = {}
    p2_ztr_hist = []
    orw_hist = []
    tok_hist = {}
    xin_last = {0: 0, 1: 0}
    stk_vsp = {}
    ODS_hist = {}
    sto_hist = {}
    xln_hist = {0: None, 1: None}
    ch_trd = (c("trd0", 16), c("trd1", 16))

    def emit_LN3(tt):
        buf = tt % 2
        xi = xin[:, buf * D:(buf + 1) * D]
        xl = xln_sb[:, buf * D:(buf + 1) * D]
        sl = slice(buf, buf + 1)
        chx = ch_xd[buf]

        def f_l3(s, tt=tt, xi=xi, chx=chx, need=xin_last[buf],
                 stw=sto_hist[tt // 4]):
            if need:
                s.wait_ge(ch_vfree.sem, need)
            s.wait_ge(stw[0][0].sem, stw[0][1])
            s.wait_ge(stw[1][0].sem, stw[1][1])
            chx.inc(s.dma_start(out=xi, in_=x1[tt * 128:(tt + 1) * 128, :]))
        P("gpsimd", f_l3)
        chx.bump()

        ln_stats_sc(P, ch_rst, xi, sqj[:], sqj[:], sl, s1, s2, mean,
                    msq, rstd, D, wait=(chx, chx.n), dr=True)

        def f_ap3(v, xi=xi, xl=xl, sl=sl, rs=ch_rst.n, xh=xln_hist[buf]):
            v.wait_ge(ch_rst.sem, rs)
            if xh:
                for c_, n_ in xh:
                    v.wait_ge(c_.sem, n_)
            ch_xln.inc(v.tensor_scalar(xl, xi, mean[:, sl], rstd[:, sl],
                                       AL.subtract, AL.mult))
            ch_vfree.inc(v.drain())
        P("vector", f_ap3)
        ch_xln.bump()
        ch_vfree.bump()
        xin_last[buf] = ch_vfree.n

        trb = tr_slot()
        def f_t3(t, xl=xl, trb=trb, xc=ch_xln.n, tev=ch_trevx[trb].n):
            t.wait_ge(ch_xln.sem, xc)
            t.wait_ge(ch_trevx[trb].sem, tev)
            for ft in range(8):
                inst = t.transpose(ps_trx[trb][:, ft * 128:(ft + 1) * 128],
                                   xl[:, ft * 128:(ft + 1) * 128], ident_sb[:])
            ch_trx[trb].inc(inst)
        P("tensor", f_t3)
        ch_trx[trb].bump()

        def f_te3(v, buf=buf, trb=trb, trc=ch_trx[trb].n):
            v.wait_ge(ch_trx[trb].sem, trc)
            ch_trevx[trb].inc(v.tensor_copy(xln_sb[:, buf * D:(buf + 1) * D],
                                            ps_trx[trb][:]))
        P("vector", f_te3)
        ch_trevx[trb].bump()
        TRE = (ch_trevx[trb], ch_trevx[trb].n)

        def f_s3(s, tt=tt, buf=buf, cht=ch_trd[buf], tev=TRE):
            s.wait_ge(tev[0].sem, tev[1])
            cht.inc(s.dma_start(
                out=xt_scr[:, tt * 128:(tt + 1) * 128].rearrange(
                    "(ft p) r -> p ft r", p=128),
                in_=xln_sb[:, buf * D:(buf + 1) * D].rearrange(
                    "p (ft r) -> p ft r", ft=8)))
        P("sync", f_s3)
        ch_trd[buf].bump()
        xln_hist[buf] = ((ch_trd[buf], ch_trd[buf].n),)

    def emit_P1_front(n, tt):
        gti = n * NTIL + tt
        buf = gti % 2
        xi = xin[:, buf * D:(buf + 1) * D]
        xl = xln_sb[:, buf * D:(buf + 1) * D]
        xt = xlnT_t[:, buf * 1024:(buf + 1) * 1024]
        sl = slice(buf, buf + 1)
        chx = ch_xd[buf]

        def f_load(s, n=n, tt=tt, xi=xi, chx=chx,
                   need=xin_last[buf]):
            if need:
                s.wait_ge(ch_vfree.sem, need)
            chx.inc(s.dma_start(
                out=xi, in_=x[n, tt * 128:(tt + 1) * 128, :]))
        P("gpsimd", f_load)
        chx.bump()

        ln_stats_sc(P, ch_rst, xi, sqj[:], junk2, sl, s1, s2, mean,
                    msq, rstd, D, wait=(chx, chx.n))

        def f_apply(v, xi=xi, xl=xl, sl=sl, rs=ch_rst.n,
                    xh=xln_hist[buf]):
            v.wait_ge(ch_rst.sem, rs)
            if xh:
                for c_, n_ in xh:
                    v.wait_ge(c_.sem, n_)
            ch_xln.inc(v.tensor_scalar(xl, xi, mean[:, sl], rstd[:, sl],
                                       AL.subtract, AL.mult))
            ch_vfree.inc(v.drain())
        P("vector", f_apply)
        ch_xln.bump()
        ch_vfree.bump()
        xin_last[buf] = ch_vfree.n

        trb = tr_slot()
        def f_tr(t, xl=xl, trb=trb, xc=ch_xln.n, tev=ch_trevx[trb].n,
                 cd=CONSTS):
            t.wait_ge(ch_xln.sem, xc)
            t.wait_ge(ch_cd.sem, cd)
            if tev:
                t.wait_ge(ch_trevx[trb].sem, tev)
            for ft in range(8):
                inst = t.transpose(ps_trx[trb][:, ft * 128:(ft + 1) * 128],
                                   xl[:, ft * 128:(ft + 1) * 128],
                                   ident_sb[:])
            ch_trx[trb].inc(inst)
        P("tensor", f_tr)
        ch_trx[trb].bump()
        xln_hist[buf] = ((ch_trx[trb], ch_trx[trb].n),)

        def f_trev(v, xt=xt, trb=trb, trc=ch_trx[trb].n):
            v.wait_ge(ch_trx[trb].sem, trc)
            ch_trevx[trb].inc(v.tensor_copy(xt, ps_trx[trb][:]))
        P("vector", f_trev)
        ch_trevx[trb].bump()
        XT_RDY = (ch_trevx[trb], ch_trevx[trb].n)

        for w_sb_, dtok in ((wq_sb, qtok), (wk_sb, ktok), (wv_sb, vtok)):
            for fc in range(2):
                psp = ps_p0 if fc == 0 else ps_p1
                def f_pmm(t, w_sb_=w_sb_, fc=fc, psp=psp, xt=xt,
                          xrdy=XT_RDY, pf=psfree[fc]):
                    t.wait_ge(xrdy[0].sem, xrdy[1])
                    if pf:
                        t.wait_ge(ch_zev.sem, pf)
                    for ft in range(8):
                        inst = t.matmul(
                            psp[:, :],
                            xt[:, ft * 128:(ft + 1) * 128],
                            w_sb_[:, ft * D + fc * 512: ft * D + (fc + 1) * 512],
                            start=(ft == 0), stop=(ft == 7))
                    ch_pj.inc(inst)
                P("tensor", f_pmm)
                ch_pj.bump()

                def f_pev(v, dtok=dtok, buf=buf, fc=fc, psp=psp, pj=ch_pj.n,
                          vspn=(ch_vsp[buf].n if dtok is vtok else 0),
                          chvs=ch_vsp[buf]):
                    v.wait_ge(ch_pj.sem, pj)
                    if vspn:
                        v.wait_ge(chvs.sem, vspn)
                    inst = v.tensor_copy(
                        dtok[:, buf * D + fc * 512: buf * D + (fc + 1) * 512],
                        psp[:, :])
                    ch_zev.inc(inst)
                    ch_pjev.inc(v.drain())
                P("vector", f_pev)
                ch_zev.bump()
                ch_pjev.bump()
                psfree[fc] = ch_zev.n
        TOK_RDY = ch_pjev.n

        def f_vsp(s, n=n, tt=tt, buf=buf, chv=ch_vsp[buf], tok=TOK_RDY,
                  vldw=(g_vld_hist.get(n - 2) if n >= 2 else None)):
            s.wait_ge(ch_pjev.sem, tok)
            if vldw:
                for _i in range(4):
                    s.wait_ge(ch_vld[_i].sem, vldw[_i])
            chv.inc(s.dma_start(
                out=v_scr[n % 2, tt * 128:(tt + 1) * 128, :],
                in_=vtok[:, buf * D:(buf + 1) * D]))
        P("sync", f_vsp)
        ch_vsp[buf].bump()
        tok_hist[(n, tt)] = TOK_RDY

    def emit_P1_qtr(n, tt):
        gti = n * NTIL + tt
        buf = gti % 2
        TOK_RDY = tok_hist[(n, tt)]

        for src_tok, dst_st in ((qtok, qst), (ktok, kst)):
            for hg in range(2):
                trb = tr_slot()
                def f_qtr(t, src_tok=src_tok, buf=buf, hg=hg, trb=trb,
                          tok=TOK_RDY, tev=ch_trevx[trb].n):
                    t.wait_ge(ch_pjev.sem, tok)
                    t.wait_ge(ch_trevx[trb].sem, tev)
                    for hh in range(8):
                        h = hg * 8 + hh
                        inst = t.transpose(
                            ps_trx[trb][0:64, hh * 128:(hh + 1) * 128],
                            src_tok[:, buf * D + h * 64: buf * D + (h + 1) * 64],
                            ident_sb[:])
                    ch_trx[trb].inc(inst)
                P("tensor", f_qtr)
                ch_trx[trb].bump()

                def f_qtrev(v, dst_st=dst_st, tt=tt, hg=hg, trb=trb,
                            trc=ch_trx[trb].n):
                    v.wait_ge(ch_trx[trb].sem, trc)
                    out_ap = dst_st[0:64, :].rearrange(
                        "p (s h) -> p s h", h=16)[:, tt * 128:(tt + 1) * 128,
                                                  hg * 8:(hg + 1) * 8]
                    in_ap = ps_trx[trb][0:64, :].rearrange(
                        "p (hh s) -> p s hh", hh=8)
                    inst = v.tensor_copy(out_ap, in_ap)
                    ch_trevx[trb].inc(inst)
                    ch_stk.inc(v.drain())
                P("vector", f_qtrev)
                ch_trevx[trb].bump()
                ch_stk.bump()

    emit_P1_front(0, 0)
    emit_P1_front(0, 1)
    emit_P1_qtr(0, 0)
    emit_P1_front(0, 2)
    emit_P1_qtr(0, 1)
    emit_P1_front(0, 3)
    emit_P1_qtr(0, 2)
    emit_P1_qtr(0, 3)
    stk_vsp[0] = (ch_stk.n, (ch_vsp[0].n, ch_vsp[1].n))

    for n in range(B):
        STK_RDY, VSP_DONE = stk_vsp[n]
        for grp in range(NPK // 4):
            ggrp = n * 16 + grp
            gbuf = ggrp % 2
            obuf = ggrp % 2

            vslot = ggrp % 4
            def f_vld(s, n=n, grp=grp, vslot=vslot, chv=ch_vld[vslot],
                      vsp=VSP_DONE, need=(nrm_hist[ggrp - 4] if ggrp >= 4 else 0)):
                s.wait_ge(ch_vsp[0].sem, vsp[0])
                s.wait_ge(ch_vsp[1].sem, vsp[1])
                if need:
                    s.wait_ge(ch_nrm.sem, need)
                row = vst[:].ap[0][0]
                dst = bass.AP(tensor=vst[:].tensor, offset=vslot * 260,
                              ap=[[row, 128], [65, 4], [1, 64]])
                srcb = bass.AP(tensor=v_scr.tensor,
                               offset=(n % 2) * 512 * D + grp * 32 * D,
                               ap=[[64, 128], [8 * D, 4], [1, 64]])
                chv.inc(s.dma_start(out=dst, in_=srcb))
            P("gpsimd", f_vld)
            ch_vld[vslot].bump()
            VLD = ch_vld[vslot].n

            for pk4 in range(4):
                pk = grp * 4 + pk4
                abank = (pk % 8) // 4
                aoff = (pk % 4) * 0
                acol = abank * 512 + pk4 * 128

                def f_amm(t, n=n, pk=pk, acol=acol, stk=STK_RDY,
                          ex=(exp_hist[ggrp - 2] if ggrp >= 2 else 0)):
                    t.wait_ge(ch_stk.sem, stk)
                    if ex:
                        t.wait_ge(ch_exp.sem, ex)
                    inst = t.matmul(ps_a[:, acol:acol + 128],
                                    kst[0:73, pk * 128:(pk + 1) * 128],
                                    qst[0:73, pk * 128:(pk + 1) * 128],
                                    start=True, stop=True)
                    if pk % 4 == 3:
                        ch_am.inc(inst)
                P("tensor", f_amm)
                if pk % 4 == 3:
                    ch_am.bump()

            def f_exp(sc, gbuf=gbuf, abank=((grp % 2)), am=ch_am.n):
                sc.wait_ge(ch_am.sem, am)
                ch_exp.inc(sc.activation(
                    pt_sb[:, gbuf * 512:(gbuf + 1) * 512],
                    ps_a[:, abank * 512:(abank + 1) * 512],
                    AF.Exp, scale=0.125))
            P("scalar", f_exp)
            ch_exp.bump()
            exp_hist.append(ch_exp.n)
            EXPG = ch_exp.n

            psot = ps_otx[obuf]
            for pk4 in range(4):
                def f_pv(t, pk4=pk4, gbuf=gbuf, vslot=vslot, psot=psot,
                         expg=EXPG, vld=VLD,
                         nrm=(nrm_hist[ggrp - 2] if ggrp >= 2 else 0),
                         bz=(BOBZEV if ggrp == 1 else 0)):
                    t.wait_ge(ch_exp.sem, expg)
                    t.wait_ge(ch_vld[vslot].sem, vld)
                    if pk4 == 0 and nrm:
                        t.wait_ge(ch_nrm.sem, nrm)
                    if pk4 == 0 and bz:
                        t.wait_ge(ch_zev.sem, bz)
                    inst = t.matmul(
                        psot[0:65, pk4 * 128:(pk4 + 1) * 128],
                        vst[:, vslot * 260 + pk4 * 65: vslot * 260 + (pk4 + 1) * 65],
                        pt_sb[:, gbuf * 512 + pk4 * 128: gbuf * 512 + (pk4 + 1) * 128],
                        start=True, stop=True)
                    if pk4 == 3:
                        ch_pv.inc(inst)
                P("tensor", f_pv)
                if pk4 == 3:
                    ch_pv.bump()

            def f_oev(v, psot=psot, obuf=obuf, pv=ch_pv.n,
                      orw=(orw_hist[ggrp - 2] if ggrp >= 2 else 0)):
                v.wait_ge(ch_pv.sem, pv)
                if orw:
                    v.wait_ge(ch_orw.sem, orw)
                ch_nrm.inc(v.tensor_copy(otn[0:65, obuf * 512:(obuf + 1) * 512],
                                         psot[0:65, :]))
            P("vector", f_oev)
            ch_nrm.bump()
            nrm_hist.append(ch_nrm.n)

            trb = tr_slot()
            def f_otr(t, trb=trb, obuf=obuf, nm=ch_nrm.n,
                      tev=ch_trevx[trb].n):
                t.wait_ge(ch_nrm.sem, nm)
                t.wait_ge(ch_trevx[trb].sem, tev)
                for pk4 in range(4):
                    inst = t.transpose(
                        ps_trx[trb][:, pk4 * 66:pk4 * 66 + 65],
                        otn[0:65, obuf * 512 + pk4 * 128:
                            obuf * 512 + (pk4 + 1) * 128],
                        ident_sb[0:65, 0:65])
                ch_trx[trb].inc(inst)
            P("tensor", f_otr)
            ch_trx[trb].bump()

            def f_onrm(v, gbuf=gbuf, trb=trb, trc=ch_trx[trb].n,
                       chod=ch_ods[gbuf], ods=ch_ods[gbuf].n):
                v.wait_ge(ch_trx[trb].sem, trc)
                if ods:
                    v.wait_ge(chod.sem, ods)
                pst = ps_trx[trb]
                row = pst[:].ap[0][0]
                lcols = bass.AP(tensor=pst[:].tensor, offset=64,
                                ap=[[row, 128], [66, 4]])
                v.reciprocal(rlc[:], lcols)
                v.drain()
                for pk4 in range(4):
                    inst = v.tensor_scalar(
                        orow[:, gbuf * 256 + pk4 * 64:
                             gbuf * 256 + (pk4 + 1) * 64],
                        pst[:, pk4 * 66:pk4 * 66 + 64],
                        rlc[:, pk4:pk4 + 1], None, AL.mult)
                ch_trevx[trb].inc(inst)
                ch_orw.inc(v.drain())
            P("vector", f_onrm)
            ch_trevx[trb].bump()
            ch_orw.bump()
            orw_hist.append(ch_orw.n)

            def f_osp(s, n=n, grp=grp, gbuf=gbuf, cho=ch_ods[gbuf],
                      orw=ch_orw.n,
                      zldw=(p2_zld_hist.get(n - 2) if n >= 2 else None)):
                s.wait_ge(ch_orw.sem, orw)
                if zldw:
                    s.wait_ge(ch_zld[0].sem, zldw[0])
                    s.wait_ge(ch_zld[1].sem, zldw[1])
                cho.inc(s.dma_start(
                    out=od2[n % 2, grp * 512:(grp + 1) * 512, :].rearrange(
                        "(pk p) e -> p pk e", p=128),
                    in_=orow[:, gbuf * 256:(gbuf + 1) * 256].rearrange(
                        "p (pk e) -> p pk e", pk=4)))
            P("sync", f_osp)
            ch_ods[gbuf].bump()
            if grp % 4 == 1 and n + 1 < B:
                emit_P1_front(n + 1, grp // 4)
            if grp % 4 == 2 and n >= 2:
                emit_LN3(4 * (n - 2) + grp // 4)
            if grp % 4 == 2 and grp >= 6 and n + 1 < B:
                emit_P1_qtr(n + 1, grp // 4 - 1)
            if grp % 4 == 3 and n >= 1:
                emit_P2(n - 1, grp // 4, ODS_hist[n - 1])
        ODS_hist[n] = (ch_ods[0].n, ch_ods[1].n)
        g_vld_hist[n] = tuple(_c.n for _c in ch_vld)
        if n + 1 < B:
            emit_P1_qtr(n + 1, 3)
            stk_vsp[n + 1] = (ch_stk.n, (ch_vsp[0].n, ch_vsp[1].n))

    A_PJ = ch_pj.n
    A_AM = ch_am.n
    w_sb = {"wf": wq_sb, "uf": wk_sb, "wi": wv_sb, "ui": wo_sb,
            "wc": qst, "uc": kst}
    _h_order = ("wf", "uf", "wi", "ui")
    h_view = lambda rt: w_sb[_h_order[rt // 8]][:, (rt % 8) * D:((rt % 8) + 1) * D]
    wo_sl = qst
    uo_sl = kst
    xtT_sl = sb("xtT_sl", [128, 2 * 8 * 512], BF16)
    cT = sb("cT", [128, 8 * 512], F32)
    cT_bf = zb[:].bitcast(BF16)
    xr_sl = gat[:].bitcast(BF16)
    biases = sb("biases", [128, 4 * 8], F32)
    onesf = bo_bc[:, 0:512]
    bo2_sb = ktok[0:1, 0:D]
    onesr2 = sb("onesr2", [1, 128], BF16)
    hmean = sb("hmean", [128, 32], F32)
    hvar = sb("hvar", [128, 32], F32)
    hrstd = sb("hrstd", [128, 32], F32)
    hms = sb("hms", [128, 32], F32)
    sqj2 = otn[:].bitcast(BF16)
    fjunk2 = qtok[:, 0:D]
    ps_g = (ps_a,)
    ps_h = (ps_p0, ps_p1)
    ch_xtl = (c("xtl0", 16), c("xtl1", 16))
    ch_xrl = (c("xrl0", 16), c("xrl1", 16))
    ch_g = c("g")
    ch_gev = c("gev")
    ch_cup = c("cup")
    ch_cbf = c("cbf")
    ch_h = c("h")
    ch_hev = c("hev")
    ch_hst = c("hst")
    ch_ho = c("ho")

    def g_fc1(g, apj=A_PJ):
        g.wait_ge(ch_pj.sem, apj)
        ch_cd.inc(g.dma_start(out=bo2_sb[:], in_=bov[:]))
        for i, bsrc in enumerate((bfv, biv, bcv, incv)):
            ch_cd.inc(g.dma_start(
                out=biases[:, i * 8:(i + 1) * 8],
                in_=bsrc.rearrange("o (mt p) -> p (o mt)", p=128),
                allow_slow_non_contiguous=True))
        for nm, dn in (("wf", "wfT"), ("uf", "ufT"), ("wi", "wiT")):
            ch_cd.inc(g.dma_start(
                out=w_sb[nm][:].rearrange("p (ft m) -> p ft m", ft=8),
                in_=wd[dn].rearrange("(ft p) m -> p ft m", p=128)))
    P("gpsimd", g_fc1)
    ch_cd.bump(8)

    def g_fc2(g, aam=A_AM):
        g.wait_ge(ch_am.sem, aam)
        for nm, dn in (("wc", "wcT"), ("uc", "ucT")):
            ch_cd.inc(g.dma_start(
                out=w_sb[nm][:].rearrange("p (ft m) -> p ft m", ft=8),
                in_=wd[dn].rearrange("(ft p) m -> p ft m", p=128)))
    P("gpsimd", g_fc2)
    ch_cd.bump(2)

    emit_P2(B - 1, 0, ODS_hist[B - 1])
    emit_LN3(24)
    emit_P2(B - 1, 1, ODS_hist[B - 1])
    emit_LN3(25)
    emit_P2(B - 1, 2, ODS_hist[B - 1])
    emit_LN3(26)
    emit_P2(B - 1, 3, ODS_hist[B - 1])
    for tt in range(27, 32):
        emit_LN3(tt)

    A_PJ = ch_pj.n
    A_AM = ch_am.n
    A_Z = ch_z.n
    A_STO = (ch_sto[0].n, ch_sto[1].n)


    def g_fc3(g, az=A_Z):
        g.wait_ge(ch_z.sem, az)
        ch_cd.inc(g.dma_start(
            out=w_sb["ui"][:].rearrange("p (ft m) -> p ft m", ft=8),
            in_=wd["uiT"].rearrange("(ft p) m -> p ft m", p=128)))
    P("gpsimd", g_fc3)
    ch_cd.bump(1)
    FCONSTS = ch_cd.n

    NT2 = S // 128
    LN3 = tuple((ch.sem, ch.n) for ch in ch_trd)

    def f_fmemset(v):
        v.drain()
        v.memset(onesf[:], 1.0)
        v.memset(onesr2[:], 1.0)
    P("vector", f_fmemset)

    def f_cinit(v, fc_=FCONSTS):
        v.wait_ge(ch_cd.sem, fc_)
        v.drain()
        for mt in range(8):
            inst = v.tensor_scalar(cT[:, mt * 512:(mt + 1) * 512], onesf[:],
                                   biases[:, 24 + mt:24 + mt + 1], None, AL.mult)
        ch_cup.inc(inst)
    P("vector", f_cinit)
    ch_cup.bump()

    GATES = (("wf", "uf", 0, AF.Sigmoid), ("wi", "ui", 8, AF.Sigmoid),
             ("wc", "uc", 16, AF.Tanh))
    for t_step in range(8):
        buf = t_step % 2

        def f_xtl(s, t_step=t_step, buf=buf, chxt=ch_xtl[buf], ln=LN3,
                  need=ch_g.n):
            for sem_, v_ in ln:
                s.wait_ge(sem_, v_)
            if t_step >= 2:
                s.wait_ge(ch_g.sem, need)
            chxt.inc(s.dma_start(
                out=xtT_sl[:, buf * 4096:(buf + 1) * 4096].rearrange(
                    "p (ft q) -> p ft q", ft=8),
                in_=xt_scr[:, t_step * 512:(t_step + 1) * 512].rearrange(
                    "(ft p) q -> p ft q", p=128)))
        P("sync", f_xtl)
        ch_xtl[buf].bump()
        XTL = (ch_xtl[buf], ch_xtl[buf].n)

        def f_cbf(v, cu=ch_cup.n):
            v.wait_ge(ch_cup.sem, cu)
            ch_cbf.inc(v.tensor_copy(cT_bf[:], cT[:]))
        P("vector", f_cbf)
        ch_cbf.bump()
        CBF = ch_cbf.n

        for mt in range(8):
            for gi, (wn, un, bcol, func) in enumerate(GATES):
                kglob = (t_step * 8 + mt) * 3 + gi
                pg = kglob % 2

                def f_gmm(t, wn=wn, un=un, mt=mt, pg=pg, buf=buf, xtl=XTL,
                          cbf=CBF, gev=(ch_gev.n - 1)):
                    t.wait_ge(xtl[0].sem, xtl[1])
                    t.wait_ge(ch_cbf.sem, cbf)
                    if gev > 0:
                        t.wait_ge(ch_gev.sem, gev)
                    for ft in range(8):
                        t.matmul(ps_a[:, pg * 512:(pg + 1) * 512],
                                 w_sb[wn][:, ft * D + mt * 128: ft * D + mt * 128 + 128],
                                 xtT_sl[:, buf * 4096 + ft * 512:
                                        buf * 4096 + (ft + 1) * 512],
                                 start=(ft == 0), stop=False)
                    for m2 in range(8):
                        inst = t.matmul(
                            ps_a[:, pg * 512:(pg + 1) * 512],
                            w_sb[un][:, m2 * D + mt * 128: m2 * D + mt * 128 + 128],
                            cT_bf[:, m2 * 512:(m2 + 1) * 512],
                            start=False, stop=(m2 == 7))
                    ch_g.inc(inst)
                P("tensor", f_gmm)
                ch_g.bump()

                def f_gev(sc, gi=gi, pg=pg, func=func, bcol=bcol, mt=mt,
                          gc=ch_g.n, cup=(ch_cup.n if gi == 0 else 0)):
                    sc.wait_ge(ch_g.sem, gc)
                    if cup:
                        sc.wait_ge(ch_cup.sem, cup)
                    ch_gev.inc(sc.activation(
                        gat[:, gi * 512:(gi + 1) * 512],
                        ps_a[:, pg * 512:(pg + 1) * 512], func,
                        bias=biases[:, bcol + mt:bcol + mt + 1]))
                P("scalar", f_gev)
                ch_gev.bump()

            def f_cup(v, mt=mt, ge=ch_gev.n):
                v.wait_ge(ch_gev.sem, ge)
                cslice = cT[:, mt * 512:(mt + 1) * 512]
                v.tensor_mul(cslice, cslice, gat[:, 0:512])
                v.tensor_mul(gat[:, 512:1024], gat[:, 512:1024], gat[:, 1024:1536])
                v.drain()
                ch_cup.inc(v.tensor_add(cslice, cslice, gat[:, 512:1024]))
            P("vector", f_cup)
            ch_cup.bump()
    SCAN = ch_cup.n
    SCAN_G = ch_g.n

    def f_cfin(v, sc_=SCAN):
        v.wait_ge(ch_cup.sem, sc_)
        ch_cbf.inc(v.tensor_copy(cT_bf[:], cT[:]))
    P("vector", f_cfin)
    ch_cbf.bump()
    CFIN = ch_cbf.n

    for fc in range(2):
        def f_wol(g, fc=fc, sg=SCAN_G):
            g.wait_ge(ch_g.sem, sg)
            ch_cd.inc(g.dma_start(
                out=wo_sl[:, fc * 4096:(fc + 1) * 4096].rearrange(
                    "p (ft q) -> p ft q", ft=8),
                in_=wd["woT2"][:, fc * 512:(fc + 1) * 512].rearrange(
                    "(ft p) q -> p ft q", p=128)))
            ch_cd.inc(g.dma_start(
                out=uo_sl[:, fc * 4096:(fc + 1) * 4096].rearrange(
                    "p (ft q) -> p ft q", ft=8),
                in_=wd["uoT"][:, fc * 512:(fc + 1) * 512].rearrange(
                    "(ft p) q -> p ft q", p=128)))
        P("gpsimd", f_wol)
        ch_cd.bump(2)
    WOL = ch_cd.n

    for rt in range(NT2):
        buf = rt % 2
        bsl = rt % 4

        def f_xrl(g, rt=rt, buf=buf, chxr=ch_xrl[buf], sc_=SCAN,
                  need=(ch_h.n - 1)):
            g.wait_ge(ch_cup.sem, sc_)
            if rt >= 2 and need > 0:
                g.wait_ge(ch_h.sem, need)
            chxr.inc(g.dma_start(
                out=xr_sl[:, buf * 1024:(buf + 1) * 1024].rearrange(
                    "p (ft r) -> p ft r", ft=8),
                in_=xt_scr[:, rt * 128:(rt + 1) * 128].rearrange(
                    "(ft p) r -> p ft r", p=128)))
        P("gpsimd", f_xrl)
        ch_xrl[buf].bump()
        XRL = (ch_xrl[buf], ch_xrl[buf].n)

        for fc in range(2):
            ph = (rt * 2 + fc) % 2

            def f_hmm(t, rt=rt, buf=buf, fc=fc, ph=ph, bsl=bsl, xrl=XRL,
                      cfin=CFIN, wol=WOL, hev=(ch_hev.n - 1)):
                t.wait_ge(xrl[0].sem, xrl[1])
                t.wait_ge(ch_cbf.sem, cfin)
                t.wait_ge(ch_cd.sem, wol)
                if hev > 0:
                    t.wait_ge(ch_hev.sem, hev)
                psh = ps_h[ph]
                for ft in range(8):
                    t.matmul(psh[:, :],
                             xr_sl[:, buf * 1024 + ft * 128:
                                   buf * 1024 + (ft + 1) * 128],
                             wo_sl[:, fc * 4096 + ft * 512:
                                   fc * 4096 + (ft + 1) * 512],
                             start=(ft == 0), stop=False)
                for m2 in range(8):
                    t.matmul(psh[:, :],
                             cT_bf[:, m2 * 512 + bsl * 128:
                                   m2 * 512 + (bsl + 1) * 128],
                             uo_sl[:, fc * 4096 + m2 * 512:
                                   fc * 4096 + (m2 + 1) * 512],
                             start=False, stop=False)
                inst = t.matmul(psh[:, :], onesr2[:],
                                bo2_sb[0:1, fc * 512:(fc + 1) * 512],
                                start=False, stop=True)
                ch_h.inc(inst)
            P("tensor", f_hmm)
            ch_h.bump()

            def f_hev(sc, rt=rt, fc=fc, ph=ph, hc=ch_h.n):
                sc.wait_ge(ch_h.sem, hc)
                ch_hev.inc(sc.activation(
                    h_view(rt)[:, fc * 512:(fc + 1) * 512],
                    ps_h[ph][:, :], AF.Tanh))
            P("scalar", f_hev)
            ch_hev.bump()

        def f_hstat(v, rt=rt, he=ch_hev.n):
            v.wait_ge(ch_hev.sem, he)
            v.drain()
            hrow = h_view(rt)
            v.reduce_sum(hmean[:, rt:rt + 1], hrow, axis=AX)
            ch_hst.inc(v.scalar_tensor_tensor(
                sqj2[:], hrow, 1.0, hrow, AL.mult, AL.mult,
                accum_out=hvar[:, rt:rt + 1]))
        P("vector", f_hstat)
        ch_hst.bump()
    HSTAT = ch_hst.n

    def f_hpre(v, hs=HSTAT):
        v.wait_ge(ch_hst.sem, hs)
        v.scalar_tensor_tensor(hms[:], hmean[:], 1.0 / (D * D), hmean[:],
                               AL.mult, AL.mult)
        v.tensor_scalar_mul(hvar[:], hvar[:], 1.0 / D)
        v.tensor_scalar_mul(hmean[:], hmean[:], 1.0 / D)
        v.drain()
        ch_hst.inc(v.tensor_sub(hvar[:], hvar[:], hms[:]))
    P("vector", f_hpre)
    ch_hst.bump()

    def f_hrstd(sc, hs=ch_hst.n):
        sc.wait_ge(ch_hst.sem, hs)
        sc.activation(hms[:], hvar[:], AF.Ln, bias=epst[:])
        sc.drain()
        ch_rst.inc(sc.activation(hrstd[:], hms[:], AF.Exp, scale=-0.5))
    P("scalar", f_hrstd)
    ch_rst.bump()
    HR = ch_rst.n

    for rt in range(NT2):
        buf = rt % 2
        xi = xin[:, buf * D:(buf + 1) * D]
        ot = x1t[:, buf * D:(buf + 1) * D]

        def f_xl2(s, rt=rt, xi=xi, chx2=ch_xd[buf], need=(ch_ho.n - 1),
                  xl_=(xin_last[buf] if rt < 2 else 0)):
            if rt >= 2 and need > 0:
                s.wait_ge(ch_ho.sem, need)
            if xl_:
                s.wait_ge(ch_vfree.sem, xl_)
            chx2.inc(s.dma_start(out=xi, in_=x1[rt * 128:(rt + 1) * 128, :]))
        P("gpsimd", f_xl2)
        ch_xd[buf].bump()
        XD2 = (ch_xd[buf], ch_xd[buf].n)

        def f_happ(v, rt=rt, xi=xi, ot=ot, hr=HR, xd=XD2,
                   stneed=(ch_sto[rt % 2], ch_sto[rt % 2].n)):
            v.wait_ge(ch_rst.sem, hr)
            v.wait_ge(xd[0].sem, xd[1])
            if stneed[1] > 0:
                v.wait_ge(stneed[0].sem, stneed[1])
            v.tensor_scalar(ot, h_view(rt), hmean[:, rt:rt + 1],
                            hrstd[:, rt:rt + 1], AL.subtract, AL.mult)
            v.drain()
            ch_ho.inc(v.tensor_add(ot, ot, xi))
        P("vector", f_happ)
        ch_ho.bump()

        def f_sto(s, rt=rt, ot=ot, chs=ch_sto[rt % 2], hc=ch_ho.n):
            s.wait_ge(ch_ho.sem, hc)
            chs.inc(s.dma_start(out=out[rt * 128:(rt + 1) * 128, :], in_=ot))
        P("gpsimd", f_sto)
        ch_sto[rt % 2].bump()

    def f_fin(s, tots=tuple((ch.sem, ch.n) for ch in ch_sto)):
        for sem, v_ in tots:
            s.wait_ge(sem, v_)
    P("sync", f_fin)

    with nc.allow_low_precision("bf16 matmul operands / attention weights; "
                                "fp32 accumulation and LN stats throughout"):
        with nc.Block() as block:
            P.emit(block)
    for _ch in _chans:
        assert _ch.emitted == _ch.n, (
            f"channel {_ch.name}: emitted {_ch.emitted} != planned {_ch.n}")
    st.close()
    return nc





_CACHE = {}


def _programs():
    if "attn" not in _CACHE:
        _CACHE["attn"] = build_attn_program()
        _CACHE["ffn"] = build_ffn_program()
    return _CACHE["attn"], _CACHE["ffn"]


def _attn_consts():
    f32 = np.float32
    ident = np.eye(128, dtype=f32)
    bsel = np.zeros((8, 128), f32)
    for b in range(8):
        bsel[b, b * 16:(b + 1) * 16] = 20.0
    rowneg = np.full((1, 128), -20.0, f32)
    rowpos = np.full((1, 128), 20.0, f32)
    return ident, bsel, rowneg, rowpos


def _rows_for_core(j):
    r = np.arange(512)
    return 256 * (r // 32) + 32 * j + (r % 32)


def kernel(L0_attn_x, L0_attn_Wq, L0_attn_Wk, L0_attn_Wv, L0_attn_Wo,
           L0_attn_bo, L0_ffn_wf, L0_ffn_uf, L0_ffn_bf, L0_ffn_wi, L0_ffn_ui,
           L0_ffn_bi, L0_ffn_wc, L0_ffn_uc, L0_ffn_bc, L0_ffn_wo, L0_ffn_uo,
           L0_ffn_bo, L0_ffn_in_c):
    f32 = np.float32
    TT = lambda a: np.ascontiguousarray(np.asarray(a, f32).T)
    row = lambda a: np.ascontiguousarray(np.asarray(a, f32).reshape(1, -1))
    x = np.ascontiguousarray(np.asarray(L0_attn_x, f32))

    ident, bsel, rowneg, rowpos = _attn_consts()
    maskq = np.zeros((9, 8192), f32)
    maskq[0, :] = 20.0
    maskq[1:9, :] = np.tile(bsel, (1, 64))
    maskk = maskq.copy()
    maskk[0, :] = -20.0
    if "fused" not in _CACHE:
        _CACHE["fused"] = build_fused_program()
    nc = _CACHE["fused"]

    common = dict(wqT=TT(L0_attn_Wq), wkT=TT(L0_attn_Wk), wvT=TT(L0_attn_Wv),
                  woT=TT(L0_attn_Wo), bo=row(L0_attn_bo), ident=ident,
                  bsel=bsel, rowneg=rowneg, rowpos=rowpos,
                  maskq=maskq, maskk=maskk,
                  wfT=TT(L0_ffn_wf), ufT=TT(L0_ffn_uf), wiT=TT(L0_ffn_wi),
                  uiT=TT(L0_ffn_ui), wcT=TT(L0_ffn_wc), ucT=TT(L0_ffn_uc),
                  woT2=TT(L0_ffn_wo), uoT=TT(L0_ffn_uo),
                  bfv=row(L0_ffn_bf), biv=row(L0_ffn_bi), bcv=row(L0_ffn_bc),
                  bov=row(L0_ffn_bo), incv=row(L0_ffn_in_c))
    in_maps = []
    for j in range(NCORES):
        rows = _rows_for_core(j)
        in_maps.append(dict(
            common,
            x=np.ascontiguousarray(x[:, 512 * j:512 * (j + 1), :]),
            xr=np.ascontiguousarray(x[:, rows, :])))
    res = run_bass_kernel_spmd(nc, in_maps, core_ids=list(range(NCORES)))

    out = np.empty((B, S, D), f32)
    for j in range(NCORES):
        rows = _rows_for_core(j)
        out[:, rows, :] = res.results[j]["out"].reshape(B, 512, D)
    return out


# version 26
# speedup vs baseline: 1.0152x; 1.0105x over previous
import numpy as np
import ml_dtypes
import concourse.bass as bass
import concourse.mybir as mybir
from concourse.bass_utils import run_bass_kernel_spmd
from contextlib import ExitStack

F32 = mybir.dt.float32
BF16 = mybir.dt.bfloat16
AF = mybir.ActivationFunctionType
AL = mybir.AluOpType
AX = mybir.AxisListType.X

D = 1024
S = 4096
B = 8
NP = 8
NKT = 32
NCORES = 8
EPS = 1e-5


class Chan:

    def __init__(self, st, nc, name, step=1):
        self.sem = st.enter_context(nc.semaphore(name))
        self.name = name
        self.step = step
        self.n = 0
        self.emitted = 0

    def bump(self, k=1):
        self.n += k * self.step
        return self.n

    def inc(self, inst):
        self.emitted += self.step
        inst.then_inc(self.sem, self.step)
        return inst


class Plan:
    ENGS = ("tensor", "scalar", "vector", "gpsimd", "sync")

    def __init__(self):
        self.ops = {k: [] for k in self.ENGS}

    def __call__(self, eng, fn):
        self.ops[eng].append(fn)

    def emit(self, block):
        for eng in self.ENGS:
            ops = self.ops[eng]
            getattr(block, eng)(lambda e, ops=ops: [fn(e) for fn in ops])


def ln_stats(P, ch_sum, ch_rst, xi, sq_junk, sl, s1, s2, mean, var, msq,
             rstd, epst, width):
    def f_sums(v, xi=xi, sl=sl):
        v.drain()
        v.reduce_sum(s1[:, sl], xi, axis=AX)
        v.scalar_tensor_tensor(sq_junk, xi, 1.0, xi, AL.mult, AL.mult,
                               accum_out=s2[:, sl])
        v.drain()
        v.tensor_scalar_mul(mean[:, sl], s1[:, sl], 1.0 / width)
        v.scalar_tensor_tensor(msq[:, sl], s1[:, sl], 1.0 / (width * width),
                               s1[:, sl], AL.mult, AL.mult)
        v.tensor_scalar_mul(var[:, sl], s2[:, sl], 1.0 / width)
        v.drain()
        ch_sum.inc(v.tensor_sub(var[:, sl], var[:, sl], msq[:, sl]))
    P("vector", f_sums)
    ch_sum.bump()

    def f_rstd(sc, sl=sl, sm=ch_sum.n):
        sc.wait_ge(ch_sum.sem, sm)
        sc.activation(msq[:, sl], var[:, sl], AF.Ln, bias=epst)
        sc.drain()
        ch_rst.inc(sc.activation(rstd[:, sl], msq[:, sl], AF.Exp, scale=-0.5))
    P("scalar", f_rstd)
    ch_rst.bump()


def ln_stats_sc(P, ch_rst, xi, sq_junk, sq_junk2, sl, s1, s2, mean, msq,
                rstd, width, wait=None, dr=False):
    def f_stat(sc, xi=xi, sl=sl, wait=wait, sq_junk2=sq_junk2, dr=dr):
        if wait is not None:
            sc.wait_ge(wait[0].sem, wait[1])
        sc.activation(sq_junk, xi, AF.Square, accum_out=s2[:, sl])
        if dr:
            sc.drain()
        sc.activation(sq_junk2, xi, AF.Copy, accum_out=s1[:, sl])
        sc.drain()
        sc.mul(mean[:, sl], s1[:, sl], 1.0 / width)
        sc.activation(msq[:, sl], s1[:, sl], AF.Square, scale=1.0 / width)
        sc.drain()
        sc.activation(msq[:, sl], msq[:, sl], AF.Copy, scale=-1.0, bias=EPS)
        sc.drain()
        sc.activation(rstd[:, sl], s2[:, sl], AF.Ln, scale=1.0 / width,
                      bias=msq[:, sl])
        sc.drain()
        ch_rst.inc(sc.activation(rstd[:, sl], rstd[:, sl], AF.Exp, scale=-0.5))
    P("scalar", f_stat)
    ch_rst.bump()



def build_attn_program(trunc="full"):
    LVL = {"ln1": 0, "qkst": 1, "attn1": 2, "full": 3}[trunc]
    nc = bass.Bass("TRN2", target_bir_lowering=False, debug=False)
    x = nc.dram_tensor("x", [B, 512, D], F32, kind="ExternalInput").ap()
    wqT = nc.dram_tensor("wqT", [D, D], F32, kind="ExternalInput").ap()
    wkT = nc.dram_tensor("wkT", [D, D], F32, kind="ExternalInput").ap()
    wvT = nc.dram_tensor("wvT", [D, D], F32, kind="ExternalInput").ap()
    woT = nc.dram_tensor("woT", [D, D], F32, kind="ExternalInput").ap()
    bo = nc.dram_tensor("bo", [1, D], F32, kind="ExternalInput").ap()
    xr = nc.dram_tensor("xr", [B, 512, D], F32, kind="ExternalInput").ap()
    ident = nc.dram_tensor("ident", [128, 128], F32, kind="ExternalInput").ap()
    bsel = nc.dram_tensor("bsel", [8, 128], F32, kind="ExternalInput").ap()
    rowneg = nc.dram_tensor("rowneg", [1, 128], F32, kind="ExternalInput").ap()
    rowpos = nc.dram_tensor("rowpos", [1, 128], F32, kind="ExternalInput").ap()
    x1 = nc.dram_tensor("x1", [S, D], F32, kind="ExternalOutput").ap()
    v_scr = nc.dram_tensor("v_scr", [2, 512, D], BF16, kind="Internal").ap()
    od2 = nc.dram_tensor("od2", [2, 512 * 16, 64], BF16, kind="Internal").ap()
    dbg = (nc.dram_tensor("dbg", [128, 49152], BF16, kind="ExternalOutput").ap()
           if LVL < 3 else None)

    st = ExitStack()
    sb = lambda name, shape, dt: st.enter_context(nc.sbuf_tensor(name, shape, dt))
    psa = lambda name, shape, dt: st.enter_context(nc.psum_tensor(name, shape, dt))

    wq_sb = sb("wq_sb", [128, 8 * D], BF16)
    wk_sb = sb("wk_sb", [128, 8 * D], BF16)
    wv_sb = sb("wv_sb", [128, 8 * D], BF16)
    wo_sb = sb("wo_sb", [128, 8 * D], BF16)
    xin = sb("xin", [128, 2 * D], F32)
    xln_sb = sb("xln_sb", [128, 2 * D], BF16)
    xlnT_t = sb("xlnT_t", [128, 2 * 1024], BF16)
    qtok = sb("qtok", [128, 2 * D], BF16)
    ktok = sb("ktok", [128, 2 * D], BF16)
    vtok = sb("vtok", [128, 2 * D], BF16)
    qst = sb("qst", [64, 512 * 16], BF16)
    kst = sb("kst", [64, 512 * 16], BF16)
    vst = sb("vst", [128, 2 * 4 * 65], BF16)
    pt_sb = sb("pt_sb", [128, 2 * 512], BF16)
    otn = sb("otn", [64, 512], BF16)
    orow = sb("orow", [128, 2 * 256], BF16)
    rl_sb = sb("rl_sb", [128, 512], BF16)
    rlb_sb = sb("rlb_sb", [64, 512], F32)
    ones64 = sb("ones64", [128, 64], BF16)
    zin = sb("zin", [128, 2 * D], BF16)
    zlT_t = sb("zlT_t", [128, 2 * 1024], BF16)
    zb = sb("zb", [128, 2 * D], F32)
    x1t = sb("x1t", [128, 2 * D], F32)
    sqj = sb("sqj", [128, D], BF16)
    bo_bc = sb("bo_bc", [128, D], F32)
    bo_sb = sb("bo_sb", [128, D], BF16)
    ident_sb = sb("ident_sb", [128, 128], BF16)
    bsel_sb = sb("bsel_sb", [8, 128], BF16)
    rneg_sb = sb("rneg_sb", [1, 128], BF16)
    rpos_sb = sb("rpos_sb", [1, 128], BF16)
    epst = sb("epst", [128, 1], F32)
    s1 = sb("s1", [128, 2], F32)
    s2 = sb("s2", [128, 2], F32)
    mean = sb("mean", [128, 2], F32)
    var = sb("var", [128, 2], F32)
    msq = sb("msq", [128, 2], F32)
    rstd = sb("rstd", [128, 2], F32)

    ps_p0 = psa("ps_p0", [128, 512], F32)
    ps_p1 = psa("ps_p1", [128, 512], F32)
    ps_a = psa("ps_a", [128, 2 * 512], F32)
    ps_ot = psa("ps_ot", [128, 512], F32)
    ps_rl = psa("ps_rl", [128, 512], F32)
    ps_tr = psa("ps_tr", [128, 1024], BF16)

    _chans = []
    def c(name, step=1):
        ch = Chan(st, nc, name, step)
        _chans.append(ch)
        return ch
    ch_xd = (c("xd0", 16), c("xd1", 16))
    ch_cd = c("cd", 16)
    ch_sum = c("sum")
    ch_rst = c("rst")
    ch_xln = c("xln")
    ch_tr = c("tr")
    ch_trev = c("trev")
    ch_pj = c("pj")
    ch_pjev = c("pjev")
    ch_vsp = (c("vsp0", 16), c("vsp1", 16))
    ch_stk = c("stk")
    ch_am = c("am")
    ch_exp = c("exp")
    ch_vld = (c("vld0", 16), c("vld1", 16))
    ch_pv = c("pv")
    ch_rl = c("rl")
    ch_rlb = c("rlb")
    ch_rlbe = c("rlbe")
    ch_nrm = c("nrm")
    ch_orw = c("orw")
    ch_ods = (c("ods0", 16), c("ods1", 16))
    ch_zld = (c("zld0", 16), c("zld1", 16))
    ch_z = c("z")
    ch_zev = c("zev")
    ch_bob = c("bob")
    ch_x1 = c("x1c")
    ch_sto = (c("sto0", 16), c("sto1", 16))
    ch_vfree = c("vfree")

    P = Plan()

    def g_consts(g):
        for out_ap, in_ap in ((ident_sb[:], ident[:]), (bsel_sb[:], bsel[:]),
                              (rneg_sb[:], rowneg[:]), (rpos_sb[:], rowpos[:]),
                              (bo_sb[64:65, :], bo[:])):
            ch_cd.inc(g.dma_start(out=out_ap, in_=in_ap))
        for w_sb_, wdram in ((wq_sb, wqT), (wk_sb, wkT), (wv_sb, wvT),
                             (wo_sb, woT)):
            ch_cd.inc(g.dma_start(
                out=w_sb_[:].rearrange("p (ft m) -> p ft m", ft=8),
                in_=wdram.rearrange("(ft p) m -> p ft m", p=128)))
    P("gpsimd", g_consts)
    ch_cd.bump(9)
    CONSTS = ch_cd.n

    def f_ones_init(v, cd=CONSTS):
        v.wait_ge(ch_cd.sem, cd)
        v.memset(epst[:], EPS)
        v.memset(vst[:].rearrange("p (b e) -> p b e", b=8)[:, :, 64:65], 1.0)
        ch_bob.inc(v.memset(ones64[64:65, :], 1.0))
    P("vector", f_ones_init)
    ch_bob.bump()
    ONES_RDY = ch_bob.n

    for half in (0, 1):
        def f_bob_mm(t, half=half, onr=ONES_RDY, ze=ch_zev.n):
            t.wait_ge(ch_bob.sem, onr)
            if ze:
                t.wait_ge(ch_zev.sem, ze)
            ch_z.inc(t.matmul(ps_rl[0:64, :], ones64[64:65, :],
                              bo_sb[64:65, half * 512:(half + 1) * 512],
                              start=True, stop=True, tile_position=(64, 0)))
        P("tensor", f_bob_mm)
        ch_z.bump()

        def f_bob_ev(sc, half=half, zc=ch_z.n):
            sc.wait_ge(ch_z.sem, zc)
            inst = sc.copy(bo_bc[0:64, half * 512:(half + 1) * 512], ps_rl[0:64, :])
            ch_zev.inc(inst)
        P("scalar", f_bob_ev)
        ch_zev.bump()
    for half in (0, 1):
        def f_bob_mm2(t, half=half, ze=ch_zev.n):
            t.wait_ge(ch_zev.sem, ze)
            ch_z.inc(t.matmul(ps_rl[64:128, :], ones64[64:65, 0:64],
                              bo_sb[64:65, half * 512:(half + 1) * 512],
                              start=True, stop=True, tile_position=(64, 64)))
        P("tensor", f_bob_mm2)
        ch_z.bump()

        def f_bob_ev2(sc, half=half, zc=ch_z.n):
            sc.wait_ge(ch_z.sem, zc)
            inst = sc.copy(bo_bc[64:128, half * 512:(half + 1) * 512],
                           ps_rl[64:128, :])
            ch_zev.inc(inst)
            ch_bob.inc(sc.drain())
        P("scalar", f_bob_ev2)
        ch_zev.bump()
        ch_bob.bump()
    BOB = ch_bob.n
    PS_RL_FREE = ch_zev.n

    NTIL = 4
    NPK = 64
    tr_hist = []
    exp_hist = []
    nrm_hist = []
    PJ_EV = {}

    for n in range(B):
        for tt in range(NTIL):
            gti = n * NTIL + tt
            buf = gti % 2
            xi = xin[:, buf * D:(buf + 1) * D]
            xl = xln_sb[:, buf * D:(buf + 1) * D]
            xt = xlnT_t[:, buf * 1024:(buf + 1) * 1024]
            sl = slice(buf, buf + 1)
            chx = ch_xd[buf]

            def f_load(s, n=n, tt=tt, gti=gti, xi=xi, chx=chx,
                       need=ch_vfree.n - 1):
                if gti >= 2:
                    s.wait_ge(ch_vfree.sem, need)
                chx.inc(s.dma_start(
                    out=xi, in_=x[n, tt * 128:(tt + 1) * 128, :]))
            P("sync", f_load)
            chx.bump()

            def f_prewait(v, chx=chx, xd=chx.n):
                v.wait_ge(chx.sem, xd)
            P("vector", f_prewait)
            ln_stats(P, ch_sum, ch_rst, xi, sqj[:], sl, s1, s2, mean, var,
                     msq, rstd, epst[:], D)

            def f_apply(v, xi=xi, xl=xl, sl=sl, rs=ch_rst.n,
                        trneed=(tr_hist[gti - 2][0] if gti >= 2 else 0)):
                v.wait_ge(ch_rst.sem, rs)
                if trneed:
                    v.wait_ge(ch_tr.sem, trneed)
                ch_xln.inc(v.tensor_scalar(xl, xi, mean[:, sl], rstd[:, sl],
                                           AL.subtract, AL.mult))
                ch_vfree.inc(v.drain())
            P("vector", f_apply)
            ch_xln.bump()
            ch_vfree.bump()

            def f_tr(t, xl=xl, xc=ch_xln.n, tev=ch_trev.n, cd=CONSTS):
                t.wait_ge(ch_xln.sem, xc)
                t.wait_ge(ch_cd.sem, cd)
                if tev:
                    t.wait_ge(ch_trev.sem, tev)
                for ft in range(8):
                    inst = t.transpose(ps_tr[:, ft * 128:(ft + 1) * 128],
                                       xl[:, ft * 128:(ft + 1) * 128],
                                       ident_sb[:])
                ch_tr.inc(inst)
            P("tensor", f_tr)
            ch_tr.bump()
            tr_hist.append((ch_tr.n, 0))
            XLN_TR = ch_tr.n

            def f_trev(v, xt=xt, trc=ch_tr.n):
                v.wait_ge(ch_tr.sem, trc)
                ch_trev.inc(v.tensor_copy(xt, ps_tr[:]))
            P("vector", f_trev)
            ch_trev.bump()
            XT_RDY = ch_trev.n

            for w_sb_, dtok in ((wq_sb, qtok), (wk_sb, ktok), (wv_sb, vtok)):
                for fc in range(2):
                    psp = ps_p0 if fc == 0 else ps_p1
                    def f_pmm(t, w_sb_=w_sb_, fc=fc, psp=psp, xt=xt,
                              xrdy=XT_RDY, pf=PS_RL_FREE):
                        t.wait_ge(ch_trev.sem, xrdy)
                        t.wait_ge(ch_zev.sem, pf)
                        for ft in range(8):
                            inst = t.matmul(
                                psp[:, :],
                                xt[:, ft * 128:(ft + 1) * 128],
                                w_sb_[:, ft * D + fc * 512: ft * D + (fc + 1) * 512],
                                start=(ft == 0), stop=(ft == 7))
                        ch_pj.inc(inst)
                    P("tensor", f_pmm)
                    ch_pj.bump()

                    def f_pev(v, dtok=dtok, buf=buf, fc=fc, psp=psp, pj=ch_pj.n,
                              vspn=(ch_vsp[buf].n if dtok is vtok else 0),
                              chvs=ch_vsp[buf]):
                        v.wait_ge(ch_pj.sem, pj)
                        if vspn:
                            v.wait_ge(chvs.sem, vspn)
                        inst = v.tensor_copy(
                            dtok[:, buf * D + fc * 512: buf * D + (fc + 1) * 512],
                            psp[:, :])
                        ch_zev.inc(inst)
                        ch_pjev.inc(v.drain())
                    P("vector", f_pev)
                    ch_zev.bump()
                    ch_pjev.bump()
                    PS_RL_FREE = ch_zev.n
            TOK_RDY = ch_pjev.n

            def f_vsp(s, n=n, tt=tt, buf=buf, chv=ch_vsp[buf], tok=TOK_RDY):
                s.wait_ge(ch_pjev.sem, tok)
                chv.inc(s.dma_start(
                    out=v_scr[n % 2, tt * 128:(tt + 1) * 128, :],
                    in_=vtok[:, buf * D:(buf + 1) * D]))
            P("sync", f_vsp)
            ch_vsp[buf].bump()

            for src_tok, dst_st in ((qtok, qst), (ktok, kst)):
                for hg in range(2):
                    def f_qtr(t, src_tok=src_tok, buf=buf, hg=hg, tok=TOK_RDY,
                              tev=ch_trev.n):
                        t.wait_ge(ch_pjev.sem, tok)
                        t.wait_ge(ch_trev.sem, tev)
                        for hh in range(8):
                            h = hg * 8 + hh
                            inst = t.transpose(
                                ps_tr[0:64, hh * 128:(hh + 1) * 128],
                                src_tok[:, buf * D + h * 64: buf * D + (h + 1) * 64],
                                ident_sb[:])
                        ch_tr.inc(inst)
                    P("tensor", f_qtr)
                    ch_tr.bump()

                    def f_qtrev(v, dst_st=dst_st, tt=tt, hg=hg, trc=ch_tr.n):
                        v.wait_ge(ch_tr.sem, trc)
                        out_ap = dst_st[0:64, :].rearrange(
                            "p (s h) -> p s h", h=16)[:, tt * 128:(tt + 1) * 128,
                                                      hg * 8:(hg + 1) * 8]
                        in_ap = ps_tr[0:64, :].rearrange(
                            "p (hh s) -> p s hh", hh=8)
                        inst = v.tensor_copy(out_ap, in_ap)
                        ch_trev.inc(inst)
                        ch_stk.inc(v.drain())
                    P("vector", f_qtrev)
                    ch_trev.bump()
                    ch_stk.bump()
        STK_RDY = ch_stk.n
        VSP_DONE = (ch_vsp[0].n, ch_vsp[1].n)

        for grp in range(NPK // 4):
            ggrp = n * 16 + grp
            gbuf = ggrp % 2

            def f_vld(s, n=n, grp=grp, gbuf=gbuf, chv=ch_vld[gbuf],
                      vsp=VSP_DONE, need=(nrm_hist[ggrp - 2] if ggrp >= 2 else 0)):
                s.wait_ge(ch_vsp[0].sem, vsp[0])
                s.wait_ge(ch_vsp[1].sem, vsp[1])
                if need:
                    s.wait_ge(ch_nrm.sem, need)
                row = vst[:].ap[0][0]
                dst = bass.AP(tensor=vst[:].tensor, offset=gbuf * 260,
                              ap=[[row, 128], [65, 4], [1, 64]])
                srcb = bass.AP(tensor=v_scr.tensor,
                               offset=(n % 2) * 512 * D + grp * 32 * D,
                               ap=[[64, 128], [8 * D, 4], [1, 64]])
                chv.inc(s.dma_start(out=dst, in_=srcb))
            P("sync", f_vld)
            ch_vld[gbuf].bump()
            VLD = ch_vld[gbuf].n

            for pk4 in range(4):
                pk = grp * 4 + pk4
                abank = (pk % 8) // 4
                aoff = (pk % 4) * 0
                acol = abank * 512 + pk4 * 128

                def f_amm(t, n=n, pk=pk, acol=acol, stk=STK_RDY,
                          ex=(exp_hist[ggrp - 2] if ggrp >= 2 else 0),
                          onr=ONES_RDY):
                    t.wait_ge(ch_stk.sem, stk)
                    t.wait_ge(ch_bob.sem, onr)
                    if ex:
                        t.wait_ge(ch_exp.sem, ex)
                    t.matmul(ps_a[:, acol:acol + 128], rneg_sb[:], rpos_sb[:],
                             start=True, stop=False)
                    t.matmul(ps_a[:, acol:acol + 128], bsel_sb[:], bsel_sb[:],
                             start=False, stop=False)
                    inst = t.matmul(ps_a[:, acol:acol + 128],
                                    kst[:, pk * 128:(pk + 1) * 128],
                                    qst[:, pk * 128:(pk + 1) * 128],
                                    start=False, stop=True)
                    if pk % 4 == 3:
                        ch_am.inc(inst)
                P("tensor", f_amm)
                if pk % 4 == 3:
                    ch_am.bump()

            def f_exp(sc, gbuf=gbuf, abank=((grp % 2)), am=ch_am.n):
                sc.wait_ge(ch_am.sem, am)
                ch_exp.inc(sc.activation(
                    pt_sb[:, gbuf * 512:(gbuf + 1) * 512],
                    ps_a[:, abank * 512:(abank + 1) * 512],
                    AF.Exp, scale=0.125))
            P("scalar", f_exp)
            ch_exp.bump()
            exp_hist.append(ch_exp.n)
            EXPG = ch_exp.n

            for pk4 in range(4):
                def f_pv(t, pk4=pk4, gbuf=gbuf, expg=EXPG, vld=VLD,
                         nrm=(nrm_hist[ggrp - 1] if ggrp >= 1 else 0)):
                    t.wait_ge(ch_exp.sem, expg)
                    t.wait_ge(ch_vld[gbuf].sem, vld)
                    if pk4 == 0 and nrm:
                        t.wait_ge(ch_nrm.sem, nrm)
                    inst = t.matmul(
                        ps_ot[0:65, pk4 * 128:(pk4 + 1) * 128],
                        vst[:, gbuf * 260 + pk4 * 65: gbuf * 260 + (pk4 + 1) * 65],
                        pt_sb[:, gbuf * 512 + pk4 * 128: gbuf * 512 + (pk4 + 1) * 128],
                        start=True, stop=True)
                    if pk4 == 3:
                        ch_pv.inc(inst)
                P("tensor", f_pv)
                if pk4 == 3:
                    ch_pv.bump()

            def f_recip(v, pv=ch_pv.n):
                v.wait_ge(ch_pv.sem, pv)
                ch_rl.inc(v.reciprocal(rl_sb[64:65, :], ps_ot[64:65, :]))
            P("vector", f_recip)
            ch_rl.bump()

            def f_rlb(t, rlc=ch_rl.n, pf=PS_RL_FREE):
                t.wait_ge(ch_rl.sem, rlc)
                t.wait_ge(ch_zev.sem, pf)
                ch_rlb.inc(t.matmul(ps_rl[0:64, :], ones64[64:65, :],
                                    rl_sb[64:65, :], start=True, stop=True,
                                    tile_position=(64, 0)))
            P("tensor", f_rlb)
            ch_rlb.bump()

            def f_rlbe(sc, rb=ch_rlb.n):
                sc.wait_ge(ch_rlb.sem, rb)
                inst = sc.copy(rlb_sb[:], ps_rl[0:64, :])
                ch_zev.inc(inst)
                ch_rlbe.inc(sc.drain())
            P("scalar", f_rlbe)
            ch_zev.bump()
            ch_rlbe.bump()
            PS_RL_FREE = ch_zev.n

            def f_norm(v, rbe=ch_rlbe.n, orw=ch_orw.n):
                v.wait_ge(ch_rlbe.sem, rbe)
                if orw:
                    v.wait_ge(ch_orw.sem, orw)
                ch_nrm.inc(v.tensor_mul(otn[0:64, :], ps_ot[0:64, :], rlb_sb[:]))
            P("vector", f_norm)
            ch_nrm.bump()
            nrm_hist.append(ch_nrm.n)

            def f_otr(t, nm=ch_nrm.n, tev=ch_trev.n):
                t.wait_ge(ch_nrm.sem, nm)
                t.wait_ge(ch_trev.sem, tev)
                for pk4 in range(4):
                    inst = t.transpose(
                        ps_tr[:, pk4 * 64:(pk4 + 1) * 64],
                        otn[0:64, pk4 * 128:(pk4 + 1) * 128],
                        ident_sb[0:64, 0:64])
                ch_tr.inc(inst)
            P("tensor", f_otr)
            ch_tr.bump()

            def f_orev(v, gbuf=gbuf, trc=ch_tr.n, chod=ch_ods[gbuf],
                       ods=ch_ods[gbuf].n):
                v.wait_ge(ch_tr.sem, trc)
                if ods:
                    v.wait_ge(chod.sem, ods)
                inst = v.tensor_copy(orow[:, gbuf * 256:(gbuf + 1) * 256],
                                     ps_tr[:, 0:256])
                ch_trev.inc(inst)
                ch_orw.inc(v.drain())
            P("vector", f_orev)
            ch_trev.bump()
            ch_orw.bump()

            def f_osp(s, n=n, grp=grp, gbuf=gbuf, cho=ch_ods[gbuf],
                      orw=ch_orw.n):
                s.wait_ge(ch_orw.sem, orw)
                cho.inc(s.dma_start(
                    out=od2[n % 2, grp * 512:(grp + 1) * 512, :].rearrange(
                        "(pk p) e -> p pk e", p=128),
                    in_=orow[:, gbuf * 256:(gbuf + 1) * 256].rearrange(
                        "p (pk e) -> p pk e", pk=4)))
            P("sync", f_osp)
            ch_ods[gbuf].bump()
        ODS_DONE = (ch_ods[0].n, ch_ods[1].n)

        if LVL >= 3:
            for rt in range(4):
                grt = n * 4 + rt
                zbuf = grt % 2
                zi = zin[:, zbuf * D:(zbuf + 1) * D]
                zt_ = zlT_t[:, zbuf * 1024:(zbuf + 1) * 1024]
                zbt = zb[:, zbuf * D:(zbuf + 1) * D]
                ot_ = x1t[:, zbuf * D:(zbuf + 1) * D]
                sl = slice(zbuf, zbuf + 1)
                chz = ch_zld[zbuf]

                def f_zld(s, n=n, rt=rt, zbuf=zbuf, chz=chz, ods=ODS_DONE,
                          need=ch_z.n):
                    s.wait_ge(ch_ods[0].sem, ods[0])
                    s.wait_ge(ch_ods[1].sem, ods[1])
                    if need:
                        s.wait_ge(ch_z.sem, need)
                    rowz = zin[:].ap[0][0]
                    for h1i in range(4):
                        src = bass.AP(
                            tensor=od2.tensor,
                            offset=(n % 2) * 8192 * 64 + rt * 4 * 64 + h1i * 64,
                            ap=[[16384, 32], [1024, 16], [1, 64]])
                        dstz = bass.AP(
                            tensor=zin[:].tensor,
                            offset=h1i * 32 * rowz + zbuf * D,
                            ap=[[rowz, 32], [64, 16], [1, 64]])
                        chz.inc(s.dma_start(out=dstz, in_=src))
                P("sync", f_zld)
                chz.bump(4)
                ZLD = chz.n

                def f_ztr(t, zi=zi, chz=chz, zld=ZLD, tev=ch_trev.n):
                    t.wait_ge(chz.sem, zld)
                    t.wait_ge(ch_trev.sem, tev)
                    for ft in range(8):
                        inst = t.transpose(ps_tr[:, ft * 128:(ft + 1) * 128],
                                           zi[:, ft * 128:(ft + 1) * 128],
                                           ident_sb[:])
                    ch_tr.inc(inst)
                P("tensor", f_ztr)
                ch_tr.bump()

                def f_ztrev(v, zt_=zt_, trc=ch_tr.n):
                    v.wait_ge(ch_tr.sem, trc)
                    ch_trev.inc(v.tensor_copy(zt_, ps_tr[:]))
                P("vector", f_ztrev)
                ch_trev.bump()
                ZT_RDY = ch_trev.n

                for fc in range(2):
                    psp = ps_p0 if fc == 0 else ps_p1
                    def f_zmm(t, zt_=zt_, fc=fc, psp=psp, zrdy=ZT_RDY,
                              pf=PS_RL_FREE, bob=BOB):
                        t.wait_ge(ch_trev.sem, zrdy)
                        t.wait_ge(ch_zev.sem, pf)
                        t.wait_ge(ch_bob.sem, bob)
                        for ft in range(8):
                            inst = t.matmul(
                                psp[:, :],
                                zt_[:, ft * 128:(ft + 1) * 128],
                                wo_sb[:, ft * D + fc * 512: ft * D + (fc + 1) * 512],
                                start=(ft == 0), stop=(ft == 7))
                        ch_z.inc(inst)
                    P("tensor", f_zmm)
                    ch_z.bump()

                    def f_zev(v, zbt=zbt, fc=fc, psp=psp, zc=ch_z.n):
                        v.wait_ge(ch_z.sem, zc)
                        inst = v.tensor_add(zbt[:, fc * 512:(fc + 1) * 512],
                                            psp[:, :],
                                            bo_bc[:, fc * 512:(fc + 1) * 512])
                        ch_zev.inc(inst)
                    P("vector", f_zev)
                    ch_zev.bump()
                    PS_RL_FREE = ch_zev.n

                ln_stats(P, ch_sum, ch_rst, zbt, sqj[:], sl, s1, s2, mean,
                         var, msq, rstd, epst[:], D)

                def f_xl(s, n=n, rt=rt, zbuf=zbuf, chx=ch_xd[zbuf],
                         need=(ch_vfree.n - 1)):
                    if need > 0:
                        s.wait_ge(ch_vfree.sem, need)
                    chx.inc(s.dma_start(
                        out=xin[:, zbuf * D:(zbuf + 1) * D],
                        in_=xr[n, rt * 128:(rt + 1) * 128, :]))
                P("sync", f_xl)
                ch_xd[zbuf].bump()
                XL = (ch_xd[zbuf], ch_xd[zbuf].n)

                def f_out(v, zbt=zbt, ot_=ot_, zbuf=zbuf, sl=sl, rs=ch_rst.n,
                          xd=XL, stneed=(ch_sto[zbuf], ch_sto[zbuf].n)):
                    v.wait_ge(ch_rst.sem, rs)
                    v.wait_ge(xd[0].sem, xd[1])
                    if stneed[1] > 0:
                        v.wait_ge(stneed[0].sem, stneed[1])
                    v.tensor_scalar(zbt, zbt, mean[:, sl], rstd[:, sl],
                                    AL.subtract, AL.mult)
                    v.drain()
                    ch_x1.inc(v.tensor_add(ot_, zbt,
                                           xin[:, zbuf * D:(zbuf + 1) * D]))
                    ch_vfree.inc(v.drain())
                P("vector", f_out)
                ch_x1.bump()
                ch_vfree.bump()

                def f_sto(s, n=n, rt=rt, ot_=ot_, chs=ch_sto[zbuf], xc=ch_x1.n):
                    s.wait_ge(ch_x1.sem, xc)
                    chs.inc(s.dma_start(
                        out=x1[n * 512 + rt * 128: n * 512 + (rt + 1) * 128, :],
                        in_=ot_))
                P("sync", f_sto)
                ch_sto[zbuf].bump()

    if LVL < 3:
        chd = ch_sto[0]
        nd = 3 if LVL >= 1 else 1
        chd.bump(nd)
        _dma_tots = [(ch.sem, ch.n) for ch in
                     (ch_vsp[0], ch_vsp[1], ch_vld[0], ch_vld[1],
                      ch_ods[0], ch_ods[1], ch_xd[0], ch_xd[1])]
        def f_dbg(s, lvl=LVL, chd=chd, tot=chd.n, stk=ch_stk.n,
                  tots=tuple(_dma_tots)):
            s.wait_ge(ch_stk.sem, stk)
            for sem_, v_ in tots:
                if v_:
                    s.wait_ge(sem_, v_)
            chd.inc(s.dma_start(out=dbg[0:64, 0:8192], in_=qst[:]))
            if lvl >= 1:
                chd.inc(s.dma_start(out=dbg[0:64, 8192:16384], in_=kst[:]))
                chd.inc(s.dma_start(
                    out=dbg[:, 16384:16640].rearrange("p (g e) -> p g e", g=4),
                    in_=od2[1, 0:512, :].rearrange("(g p) e -> p g e", p=128)))
            s.wait_ge(chd.sem, tot)
        P("sync", f_dbg)
        with nc.allow_low_precision("debug"):
            with nc.Block() as block:
                P.emit(block)
        for _ch in _chans:
            assert _ch.emitted == _ch.n, (
                f"channel {_ch.name}: emitted {_ch.emitted} != planned {_ch.n}")
        st.close()
        return nc

    def f_fin(s, tots=tuple((ch.sem, ch.n) for ch in ch_sto)):
        for sem, v_ in tots:
            s.wait_ge(sem, v_)
    P("sync", f_fin)

    with nc.allow_low_precision("bf16 matmul operands / attention weights; "
                                "fp32 accumulation and LN stats throughout"):
        with nc.Block() as block:
            P.emit(block)
    for _ch in _chans:
        assert _ch.emitted == _ch.n, (
            f"channel {_ch.name}: emitted {_ch.emitted} != planned {_ch.n}")
    st.close()
    return nc



def build_ffn_program(dbg=False):
    nc = bass.Bass("TRN2", target_bir_lowering=False, debug=False)
    xs = nc.dram_tensor("xs", [S, D], F32, kind="ExternalInput").ap()
    wd = {}
    for nm in ("wfT", "ufT", "wiT", "uiT", "wcT", "ucT", "woT", "uoT"):
        wd[nm] = nc.dram_tensor(nm, [D, D], F32, kind="ExternalInput").ap()
    bfv = nc.dram_tensor("bfv", [1, D], F32, kind="ExternalInput").ap()
    biv = nc.dram_tensor("biv", [1, D], F32, kind="ExternalInput").ap()
    bcv = nc.dram_tensor("bcv", [1, D], F32, kind="ExternalInput").ap()
    bov = nc.dram_tensor("bov", [1, D], F32, kind="ExternalInput").ap()
    incv = nc.dram_tensor("incv", [1, D], F32, kind="ExternalInput").ap()
    ident = nc.dram_tensor("ident", [128, 128], F32, kind="ExternalInput").ap()
    out = nc.dram_tensor("out", [S, D], F32, kind="ExternalOutput").ap()
    xt_scr = nc.dram_tensor("xt_scr", [D, S], BF16,
                            kind="ExternalOutput" if dbg else "Internal").ap()
    dbg_c = (nc.dram_tensor("dbg_c", [128, 4096 + 96], F32,
                            kind="ExternalOutput").ap() if dbg else None)
    dbg_h = (nc.dram_tensor("dbg_h", [128, 32 * D], BF16,
                            kind="ExternalOutput").ap() if dbg else None)

    st = ExitStack()
    sb = lambda name, shape, dt: st.enter_context(nc.sbuf_tensor(name, shape, dt))
    psa = lambda name, shape, dt: st.enter_context(nc.psum_tensor(name, shape, dt))

    w_sb = {}
    for nm in ("wf", "uf", "wi", "ui", "wc", "uc"):
        w_sb[nm] = sb(nm + "_sb", [128, 8 * D], BF16)
    xtT_sl = sb("xtT_sl", [128, 2 * 8 * 512], BF16)
    cT = sb("cT", [128, 8 * 512], F32)
    cT_bf = sb("cT_bf", [128, 8 * 512], BF16)
    gat = sb("gat", [128, 3 * 512], F32)
    xin = sb("xin", [128, 2 * D], F32)
    xln_sb = sb("xln_sb", [128, 2 * D], BF16)
    x1t = sb("x1t", [128, 2 * D], F32)
    sqj = sb("sqj", [128, D], BF16)
    biases = sb("biases", [128, 4 * 8], F32)
    ident_sb = sb("ident_sb", [128, 128], BF16)
    onesf = sb("onesf", [128, 512], F32)
    bo2_sb = sb("bo2_sb", [1, D], BF16)
    onesr2 = sb("onesr2", [1, 128], BF16)
    epst = sb("epst", [128, 1], F32)
    s1 = sb("s1", [128, 2], F32)
    s2 = sb("s2", [128, 2], F32)
    mean = sb("mean", [128, 2], F32)
    var = sb("var", [128, 2], F32)
    msq = sb("msq", [128, 2], F32)
    rstd = sb("rstd", [128, 2], F32)
    hmean = sb("hmean", [128, 32], F32)
    hvar = sb("hvar", [128, 32], F32)
    hrstd = sb("hrstd", [128, 32], F32)
    hms = sb("hms", [128, 32], F32)

    _h_order = ("wf", "uf", "wi", "ui")
    h_view = lambda rt: w_sb[_h_order[rt // 8]][:, (rt % 8) * D:((rt % 8) + 1) * D]
    wo_sl = w_sb["wc"]
    uo_sl = w_sb["uc"]
    xr_sl = gat[:].bitcast(BF16)

    ps_tr = psa("ps_tr", [128, 1024], BF16)
    ps_g = psa("ps_g", [128, 2 * 512], F32)
    ps_h = psa("ps_h", [128, 2 * 512], F32)

    _chans = []
    def c(name, step=1):
        ch = Chan(st, nc, name, step)
        _chans.append(ch)
        return ch
    ch_xd = (c("xd0", 16), c("xd1", 16))
    ch_wd = c("wd", 16)
    ch_cd = c("cd", 16)
    ch_sum = c("sum")
    ch_rst = c("rst")
    ch_xln = c("xln")
    ch_tr = c("tr")
    ch_trev = c("trev")
    ch_trd = (c("trd0", 16), c("trd1", 16))
    ch_xtl = (c("xtl0", 16), c("xtl1", 16))
    ch_xrl = (c("xrl0", 16), c("xrl1", 16))
    ch_g = c("g")
    ch_gev = c("gev")
    ch_cup = c("cup")
    ch_cbf = c("cbf")
    ch_h = c("h")
    ch_hev = c("hev")
    ch_hst = c("hst")
    ch_ho = c("ho")
    ch_sto = (c("sto0", 16), c("sto1", 16))
    ch_vfree = c("vfree")

    P = Plan()

    def g_consts(g):
        ch_cd.inc(g.dma_start(out=ident_sb[:], in_=ident[:]))
        ch_cd.inc(g.dma_start(out=bo2_sb[:], in_=bov[:]))
        for i, bsrc in enumerate((bfv, biv, bcv, incv)):
            ch_cd.inc(g.dma_start(
                out=biases[:, i * 8:(i + 1) * 8],
                in_=bsrc.rearrange("o (mt p) -> p (o mt)", p=128),
                allow_slow_non_contiguous=True))
        for nm in ("wf", "uf", "wi", "ui", "wc", "uc"):
            ch_cd.inc(g.dma_start(
                out=w_sb[nm][:].rearrange("p (ft m) -> p ft m", ft=8),
                in_=wd[nm + "T"].rearrange("(ft p) m -> p ft m", p=128)))
    P("gpsimd", g_consts)
    ch_cd.bump(12)
    CONSTS = ch_cd.n

    NT = S // 128
    tr_hist = []
    trd_hist = []
    for tt in range(NT):
        buf = tt % 2
        xi = xin[:, buf * D:(buf + 1) * D]
        xl = xln_sb[:, buf * D:(buf + 1) * D]
        sl = slice(buf, buf + 1)

        chx = ch_xd[buf]

        def f_load(s, tt=tt, xi=xi, chx=chx, need=ch_vfree.n - 1):
            if tt >= 2:
                s.wait_ge(ch_vfree.sem, need)
            chx.inc(s.dma_start(out=xi, in_=xs[tt * 128:(tt + 1) * 128, :]))
        P("sync", f_load)
        chx.bump()

        def f_pre(v, tt=tt, chx=chx, xd=chx.n):
            v.wait_ge(chx.sem, xd)
            if tt == 0:
                v.memset(epst[:], EPS)
                v.memset(onesf[:], 1.0)
                v.memset(onesr2[:], 1.0)
        P("vector", f_pre)
        ln_stats(P, ch_sum, ch_rst, xi, sqj[:], sl, s1, s2, mean, var, msq,
                 rstd, epst[:], D)

        def f_apply(v, xi=xi, xl=xl, sl=sl, rs=ch_rst.n,
                    trneed=(tr_hist[tt - 2] if tt >= 2 else 0),
                    spneed=(trd_hist[tt - 2] if tt >= 2 else None)):
            v.wait_ge(ch_rst.sem, rs)
            if trneed:
                v.wait_ge(ch_tr.sem, trneed)
            if spneed:
                v.wait_ge(spneed[0].sem, spneed[1])
            ch_xln.inc(v.tensor_scalar(xl, xi, mean[:, sl], rstd[:, sl],
                                       AL.subtract, AL.mult))
            ch_vfree.inc(v.drain())
        P("vector", f_apply)
        ch_xln.bump()
        ch_vfree.bump()

        def f_tr(t, tt=tt, xl=xl, xc=ch_xln.n, tev=ch_trev.n, cd=CONSTS):
            t.wait_ge(ch_xln.sem, xc)
            if tt == 0:
                t.wait_ge(ch_cd.sem, cd)
            else:
                t.wait_ge(ch_trev.sem, tev)
            for ft in range(8):
                inst = t.transpose(ps_tr[:, ft * 128:(ft + 1) * 128],
                                   xl[:, ft * 128:(ft + 1) * 128], ident_sb[:])
            ch_tr.inc(inst)
        P("tensor", f_tr)
        ch_tr.bump()
        tr_hist.append(ch_tr.n)

        def f_trev(v, tt=tt, buf=buf, trc=ch_tr.n):
            v.wait_ge(ch_tr.sem, trc)
            ch_trev.inc(v.tensor_copy(xln_sb[:, buf * D:(buf + 1) * D],
                                      ps_tr[:]))
        P("vector", f_trev)
        ch_trev.bump()

        cht = ch_trd[buf]

        def f_sp(s, tt=tt, buf=buf, cht=cht, tev=ch_trev.n):
            s.wait_ge(ch_trev.sem, tev)
            cht.inc(s.dma_start(
                out=xt_scr[:, tt * 128:(tt + 1) * 128].rearrange(
                    "(ft p) r -> p ft r", p=128),
                in_=xln_sb[:, buf * D:(buf + 1) * D].rearrange(
                    "p (ft r) -> p ft r", ft=8)))
        P("sync", f_sp)
        cht.bump()
        trd_hist.append((cht, cht.n))
    LN3 = tuple((ch.sem, ch.n) for ch in ch_trd)

    def f_cinit(v, cd=CONSTS):
        v.wait_ge(ch_cd.sem, cd)
        v.drain()
        for mt in range(8):
            inst = v.tensor_scalar(cT[:, mt * 512:(mt + 1) * 512], onesf[:],
                                   biases[:, 24 + mt:24 + mt + 1], None, AL.mult)
        ch_cup.inc(inst)
    P("vector", f_cinit)
    ch_cup.bump()

    GATES = (("wf", "uf", 0, AF.Sigmoid), ("wi", "ui", 8, AF.Sigmoid),
             ("wc", "uc", 16, AF.Tanh))
    for t_step in range(8):
        buf = t_step % 2

        chxt = ch_xtl[buf]

        def f_xtl(s, t_step=t_step, buf=buf, chxt=chxt, ln=LN3, need=ch_g.n):
            for sem, v_ in ln:
                s.wait_ge(sem, v_)
            if t_step >= 2:
                s.wait_ge(ch_g.sem, need)
            chxt.inc(s.dma_start(
                out=xtT_sl[:, buf * 4096:(buf + 1) * 4096].rearrange(
                    "p (ft q) -> p ft q", ft=8),
                in_=xt_scr[:, t_step * 512:(t_step + 1) * 512].rearrange(
                    "(ft p) q -> p ft q", p=128)))
        P("sync", f_xtl)
        chxt.bump()
        XTL = (chxt, chxt.n)

        def f_cbf(v, cu=ch_cup.n):
            v.wait_ge(ch_cup.sem, cu)
            ch_cbf.inc(v.tensor_copy(cT_bf[:], cT[:]))
        P("vector", f_cbf)
        ch_cbf.bump()
        CBF = ch_cbf.n

        for mt in range(8):
            for gi, (wn, un, bcol, func) in enumerate(GATES):
                kglob = (t_step * 8 + mt) * 3 + gi
                pg = kglob % 2

                def f_gmm(t, wn=wn, un=un, mt=mt, pg=pg, buf=buf, xtl=XTL,
                          cbf=CBF, gev=(ch_gev.n - 1)):
                    t.wait_ge(xtl[0].sem, xtl[1])
                    t.wait_ge(ch_cbf.sem, cbf)
                    if gev > 0:
                        t.wait_ge(ch_gev.sem, gev)
                    for ft in range(8):
                        t.matmul(ps_g[:, pg * 512:(pg + 1) * 512],
                                 w_sb[wn][:, ft * D + mt * 128: ft * D + mt * 128 + 128],
                                 xtT_sl[:, buf * 4096 + ft * 512:
                                        buf * 4096 + (ft + 1) * 512],
                                 start=(ft == 0), stop=False)
                    for m2 in range(8):
                        inst = t.matmul(
                            ps_g[:, pg * 512:(pg + 1) * 512],
                            w_sb[un][:, m2 * D + mt * 128: m2 * D + mt * 128 + 128],
                            cT_bf[:, m2 * 512:(m2 + 1) * 512],
                            start=False, stop=(m2 == 7))
                    ch_g.inc(inst)
                P("tensor", f_gmm)
                ch_g.bump()

                def f_gev(sc, gi=gi, pg=pg, func=func, bcol=bcol, mt=mt,
                          gc=ch_g.n, cup=(ch_cup.n if gi == 0 else 0)):
                    sc.wait_ge(ch_g.sem, gc)
                    if cup:
                        sc.wait_ge(ch_cup.sem, cup)
                    ch_gev.inc(sc.activation(
                        gat[:, gi * 512:(gi + 1) * 512],
                        ps_g[:, pg * 512:(pg + 1) * 512], func,
                        bias=biases[:, bcol + mt:bcol + mt + 1]))
                P("scalar", f_gev)
                ch_gev.bump()

            def f_cup(v, mt=mt, ge=ch_gev.n):
                v.wait_ge(ch_gev.sem, ge)
                cslice = cT[:, mt * 512:(mt + 1) * 512]
                v.tensor_mul(cslice, cslice, gat[:, 0:512])
                v.tensor_mul(gat[:, 512:1024], gat[:, 512:1024], gat[:, 1024:1536])
                v.drain()
                ch_cup.inc(v.tensor_add(cslice, cslice, gat[:, 512:1024]))
            P("vector", f_cup)
            ch_cup.bump()
    SCAN = ch_cup.n
    SCAN_G = ch_g.n

    def f_cfin(v, sc_=SCAN):
        v.wait_ge(ch_cup.sem, sc_)
        ch_cbf.inc(v.tensor_copy(cT_bf[:], cT[:]))
    P("vector", f_cfin)
    ch_cbf.bump()
    CFIN = ch_cbf.n

    for fc in range(2):
        def f_wol(g, fc=fc, sg=SCAN_G):
            g.wait_ge(ch_g.sem, sg)
            ch_wd.inc(g.dma_start(
                out=wo_sl[:, fc * 4096:(fc + 1) * 4096].rearrange(
                    "p (ft q) -> p ft q", ft=8),
                in_=wd["woT"][:, fc * 512:(fc + 1) * 512].rearrange(
                    "(ft p) q -> p ft q", p=128)))
            ch_wd.inc(g.dma_start(
                out=uo_sl[:, fc * 4096:(fc + 1) * 4096].rearrange(
                    "p (ft q) -> p ft q", ft=8),
                in_=wd["uoT"][:, fc * 512:(fc + 1) * 512].rearrange(
                    "(ft p) q -> p ft q", p=128)))
        P("gpsimd", f_wol)
        ch_wd.bump(2)
    WOL = ch_wd.n

    for rt in range(NT):
        buf = rt % 2
        bsl = rt % 4

        chxr = ch_xrl[buf]

        def f_xrl(g, rt=rt, buf=buf, chxr=chxr, sc_=SCAN, need=(ch_h.n - 1)):
            g.wait_ge(ch_cup.sem, sc_)
            if rt >= 2 and need > 0:
                g.wait_ge(ch_h.sem, need)
            chxr.inc(g.dma_start(
                out=xr_sl[:, buf * 1024:(buf + 1) * 1024].rearrange(
                    "p (ft r) -> p ft r", ft=8),
                in_=xt_scr[:, rt * 128:(rt + 1) * 128].rearrange(
                    "(ft p) r -> p ft r", p=128)))
        P("gpsimd", f_xrl)
        chxr.bump()
        XRL = (chxr, chxr.n)

        for fc in range(2):
            ph = (rt * 2 + fc) % 2

            def f_hmm(t, rt=rt, buf=buf, fc=fc, ph=ph, bsl=bsl, xrl=XRL,
                      cfin=CFIN, wol=WOL, hev=(ch_hev.n - 1)):
                t.wait_ge(xrl[0].sem, xrl[1])
                t.wait_ge(ch_cbf.sem, cfin)
                t.wait_ge(ch_wd.sem, wol)
                if hev > 0:
                    t.wait_ge(ch_hev.sem, hev)
                for ft in range(8):
                    t.matmul(ps_h[:, ph * 512:(ph + 1) * 512],
                             xr_sl[:, buf * 1024 + ft * 128:
                                   buf * 1024 + (ft + 1) * 128],
                             wo_sl[:, fc * 4096 + ft * 512:
                                   fc * 4096 + (ft + 1) * 512],
                             start=(ft == 0), stop=False)
                for m2 in range(8):
                    t.matmul(ps_h[:, ph * 512:(ph + 1) * 512],
                             cT_bf[:, m2 * 512 + bsl * 128:
                                   m2 * 512 + (bsl + 1) * 128],
                             uo_sl[:, fc * 4096 + m2 * 512:
                                   fc * 4096 + (m2 + 1) * 512],
                             start=False, stop=False)
                inst = t.matmul(ps_h[:, ph * 512:(ph + 1) * 512], onesr2[:],
                                bo2_sb[0:1, fc * 512:(fc + 1) * 512],
                                start=False, stop=True)
                ch_h.inc(inst)
            P("tensor", f_hmm)
            ch_h.bump()

            def f_hev(sc, rt=rt, fc=fc, ph=ph, hc=ch_h.n):
                sc.wait_ge(ch_h.sem, hc)
                ch_hev.inc(sc.activation(
                    h_view(rt)[:, fc * 512:(fc + 1) * 512],
                    ps_h[:, ph * 512:(ph + 1) * 512], AF.Tanh))
            P("scalar", f_hev)
            ch_hev.bump()

        def f_hstat(v, rt=rt, he=ch_hev.n):
            v.wait_ge(ch_hev.sem, he)
            v.drain()
            hrow = h_view(rt)
            v.reduce_sum(hmean[:, rt:rt + 1], hrow, axis=AX)
            ch_hst.inc(v.scalar_tensor_tensor(
                sqj[:], hrow, 1.0, hrow, AL.mult, AL.mult,
                accum_out=hvar[:, rt:rt + 1]))
        P("vector", f_hstat)
        ch_hst.bump()
    HSTAT = ch_hst.n

    def f_hpre(v, hs=HSTAT):
        v.wait_ge(ch_hst.sem, hs)
        v.scalar_tensor_tensor(hms[:], hmean[:], 1.0 / (D * D), hmean[:],
                               AL.mult, AL.mult)
        v.tensor_scalar_mul(hvar[:], hvar[:], 1.0 / D)
        v.tensor_scalar_mul(hmean[:], hmean[:], 1.0 / D)
        v.drain()
        ch_hst.inc(v.tensor_sub(hvar[:], hvar[:], hms[:]))
    P("vector", f_hpre)
    ch_hst.bump()

    def f_hrstd(sc, hs=ch_hst.n):
        sc.wait_ge(ch_hst.sem, hs)
        sc.activation(hms[:], hvar[:], AF.Ln, bias=epst[:])
        sc.drain()
        ch_rst.inc(sc.activation(hrstd[:], hms[:], AF.Exp, scale=-0.5))
    P("scalar", f_hrstd)
    ch_rst.bump()
    HR = ch_rst.n

    for rt in range(NT):
        buf = rt % 2
        xi = xin[:, buf * D:(buf + 1) * D]
        ot = x1t[:, buf * D:(buf + 1) * D]

        chx2 = ch_xd[buf]

        def f_xl2(s, rt=rt, xi=xi, chx2=chx2, need=(ch_ho.n - 1)):
            if rt >= 2 and need > 0:
                s.wait_ge(ch_ho.sem, need)
            chx2.inc(s.dma_start(out=xi, in_=xs[rt * 128:(rt + 1) * 128, :]))
        P("sync", f_xl2)
        chx2.bump()
        XD2 = (chx2, chx2.n)

        def f_happ(v, rt=rt, xi=xi, ot=ot, hr=HR, xd=XD2,
                   stneed=(ch_sto[rt % 2], ch_sto[rt % 2].n)):
            v.wait_ge(ch_rst.sem, hr)
            v.wait_ge(xd[0].sem, xd[1])
            if stneed[1] > 0:
                v.wait_ge(stneed[0].sem, stneed[1])
            v.tensor_scalar(ot, h_view(rt), hmean[:, rt:rt + 1],
                            hrstd[:, rt:rt + 1], AL.subtract, AL.mult)
            v.drain()
            ch_ho.inc(v.tensor_add(ot, ot, xi))
        P("vector", f_happ)
        ch_ho.bump()

        def f_sto(s, rt=rt, ot=ot, chs=ch_sto[rt % 2], hc=ch_ho.n):
            s.wait_ge(ch_ho.sem, hc)
            chs.inc(s.dma_start(out=out[rt * 128:(rt + 1) * 128, :], in_=ot))
        P("sync", f_sto)
        ch_sto[rt % 2].bump()

    if dbg:
        ch_sto[0].bump(3)
        def f_dump(s, tots=tuple((ch.sem, ch.n) for ch in ch_sto),
                   chd=ch_sto[0], tot=ch_sto[0].n):
            for sem, v_ in tots[1:]:
                s.wait_ge(sem, v_)
            s.wait_ge(tots[0][0], tots[0][1] - 48)
            chd.inc(s.dma_start(out=dbg_c[:, 0:4096], in_=cT[:]))
            chd.inc(s.dma_start(out=dbg_c[:, 4096:4128], in_=hmean[:]))
            chd.inc(s.dma_start(out=dbg_c[:, 4128:4160], in_=hrstd[:]))
            s.wait_ge(chd.sem, tot)
        P("sync", f_dump)

        ch_sto[1].bump(32)
        def f_dumph(s, chd=ch_sto[1], tot=ch_sto[1].n, base=ch_sto[1].n - 512):
            for rt in range(NT):
                chd.inc(s.dma_start(out=dbg_h[:, rt * D:(rt + 1) * D],
                                    in_=h_view(rt)))
            s.wait_ge(chd.sem, tot)
        P("sync", f_dumph)

    def f_fin(s, tots=tuple((ch.sem, ch.n) for ch in ch_sto)):
        for sem, v_ in tots:
            s.wait_ge(sem, v_)
    P("sync", f_fin)

    with nc.allow_low_precision("bf16 matmul operands / attention weights; "
                                "fp32 accumulation and LN stats throughout"):
        with nc.Block() as block:
            P.emit(block)
    for _ch in _chans:
        assert _ch.emitted == _ch.n, (
            f"channel {_ch.name}: emitted {_ch.emitted} != planned {_ch.n}")
    st.close()
    return nc


def build_fused_program():
    LVL = 3
    nc = bass.Bass("TRN2", target_bir_lowering=False, debug=False)
    x = nc.dram_tensor("x", [B, 512, D], F32, kind="ExternalInput").ap()
    wqT = nc.dram_tensor("wqT", [D, D], F32, kind="ExternalInput").ap()
    wkT = nc.dram_tensor("wkT", [D, D], F32, kind="ExternalInput").ap()
    wvT = nc.dram_tensor("wvT", [D, D], F32, kind="ExternalInput").ap()
    woT = nc.dram_tensor("woT", [D, D], F32, kind="ExternalInput").ap()
    bo = nc.dram_tensor("bo", [1, D], F32, kind="ExternalInput").ap()
    xr = nc.dram_tensor("xr", [B, 512, D], F32, kind="ExternalInput").ap()
    ident = nc.dram_tensor("ident", [128, 128], F32, kind="ExternalInput").ap()
    bsel = nc.dram_tensor("bsel", [8, 128], F32, kind="ExternalInput").ap()
    rowneg = nc.dram_tensor("rowneg", [1, 128], F32, kind="ExternalInput").ap()
    rowpos = nc.dram_tensor("rowpos", [1, 128], F32, kind="ExternalInput").ap()
    maskq = nc.dram_tensor("maskq", [9, 512 * 16], F32, kind="ExternalInput").ap()
    maskk = nc.dram_tensor("maskk", [9, 512 * 16], F32, kind="ExternalInput").ap()
    x1 = nc.dram_tensor("x1", [S, D], F32, kind="Internal").ap()
    out = nc.dram_tensor("out", [S, D], F32, kind="ExternalOutput").ap()
    wd = {}
    for nm in ("wfT", "ufT", "wiT", "uiT", "wcT", "ucT", "woT2", "uoT"):
        wd[nm] = nc.dram_tensor(nm, [D, D], F32, kind="ExternalInput").ap()
    bfv = nc.dram_tensor("bfv", [1, D], F32, kind="ExternalInput").ap()
    biv = nc.dram_tensor("biv", [1, D], F32, kind="ExternalInput").ap()
    bcv = nc.dram_tensor("bcv", [1, D], F32, kind="ExternalInput").ap()
    bov = nc.dram_tensor("bov", [1, D], F32, kind="ExternalInput").ap()
    incv = nc.dram_tensor("incv", [1, D], F32, kind="ExternalInput").ap()
    xt_scr = nc.dram_tensor("xt_scr", [D, S], BF16, kind="Internal").ap()
    v_scr = nc.dram_tensor("v_scr", [2, 512, D], BF16, kind="Internal").ap()
    od2 = nc.dram_tensor("od2", [2, 512 * 16, 64], BF16, kind="Internal").ap()

    st = ExitStack()
    sb = lambda name, shape, dt: st.enter_context(nc.sbuf_tensor(name, shape, dt))
    psa = lambda name, shape, dt: st.enter_context(nc.psum_tensor(name, shape, dt))

    wq_sb = sb("wq_sb", [128, 8 * D], BF16)
    wk_sb = sb("wk_sb", [128, 8 * D], BF16)
    wv_sb = sb("wv_sb", [128, 8 * D], BF16)
    wo_sb = sb("wo_sb", [128, 8 * D], BF16)
    xin = sb("xin", [128, 2 * D], F32)
    xln_sb = sb("xln_sb", [128, 2 * D], BF16)
    xlnT_t = sb("xlnT_t", [128, 2 * 1024], BF16)
    qtok = sb("qtok", [128, 2 * D], BF16)
    ktok = sb("ktok", [128, 2 * D], BF16)
    vtok = sb("vtok", [128, 2 * D], BF16)
    qst = sb("qst", [128, 512 * 16], BF16)
    kst = sb("kst", [128, 512 * 16], BF16)
    vst = sb("vst", [128, 4 * 4 * 65], BF16)
    pt_sb = sb("pt_sb", [128, 2 * 512], BF16)
    otn = sb("otn", [128, 2 * 512], BF16)
    orow = sb("orow", [128, 2 * 256], BF16)
    ones64 = sb("ones64", [128, 64], BF16)
    zin = sb("zin", [128, 2 * D], BF16)
    zlT_t = sb("zlT_t", [128, 2 * 1024], BF16)
    zb = sb("zb", [128, 2 * D], F32)
    x1t = sb("x1t", [128, 2 * D], F32)
    sqj = sb("sqj", [128, D], BF16)
    bo_bc = sb("bo_bc", [128, D], F32)
    bo_sb = sb("bo_sb", [128, D], BF16)
    ident_sb = sb("ident_sb", [128, 128], BF16)
    bsel_sb = sb("bsel_sb", [8, 128], BF16)
    rneg_sb = sb("rneg_sb", [1, 128], BF16)
    rpos_sb = sb("rpos_sb", [1, 128], BF16)
    epst = sb("epst", [128, 1], F32)
    s1 = sb("s1", [128, 2], F32)
    s2 = sb("s2", [128, 2], F32)
    mean = sb("mean", [128, 2], F32)
    var = sb("var", [128, 2], F32)
    msq = sb("msq", [128, 2], F32)
    rlc = sb("rlc", [128, 4], F32)
    rstd = sb("rstd", [128, 2], F32)

    gat = sb("gat", [128, 3 * 512], F32)
    junk2 = gat[:].bitcast(BF16)[:, 0:1024]

    ps_p0 = psa("ps_p0", [128, 512], F32)
    ps_p1 = psa("ps_p1", [128, 512], F32)
    ps_a = psa("ps_a", [128, 2 * 512], F32)
    ps_ot = psa("ps_ot", [128, 512], F32)
    ps_rl = psa("ps_rl", [128, 512], F32)
    ps_tr = psa("ps_tr", [128, 1024], BF16)
    ps_trb = psa("ps_trb", [128, 1024], BF16)

    _chans = []
    def c(name, step=1):
        ch = Chan(st, nc, name, step)
        _chans.append(ch)
        return ch
    ch_xd = (c("xd0", 16), c("xd1", 16))
    ch_cd = c("cd", 16)
    ch_sum = c("sum")
    ch_rst = c("rst")
    ch_xln = c("xln")
    ch_tr = c("tr")
    ch_trev = c("trev")
    ch_trB = c("trB")
    ch_trevB = c("trevB")
    ch_pj = c("pj")
    ch_pjev = c("pjev")
    ch_vsp = (c("vsp0", 16), c("vsp1", 16))
    ch_stk = c("stk")
    ch_am = c("am")
    ch_exp = c("exp")
    ch_vld = tuple(c(f"vld{i}", 16) for i in range(4))
    ch_pv = c("pv")
    ch_rl = c("rl")
    ch_rlb = c("rlb")
    ch_rlbe = c("rlbe")
    ch_nrm = c("nrm")
    ch_orw = c("orw")
    ch_ods = (c("ods0", 16), c("ods1", 16))
    ch_zld = (c("zld0", 16), c("zld1", 16))
    ch_z = c("z")
    ch_zev = c("zev")
    ch_bob = c("bob")
    ch_x1 = c("x1c")
    ch_sto = (c("sto0", 16), c("sto1", 16))
    ch_vfree = c("vfree")

    ps_trx = (ps_tr, ps_trb)
    ch_trx = (ch_tr, ch_trB)
    ch_trevx = (ch_trev, ch_trevB)
    _trs = [0]
    def tr_slot():
        b = _trs[0] % 2
        _trs[0] += 1
        return b

    P = Plan()

    def g_consts(g):
        for out_ap, in_ap in ((ident_sb[:], ident[:]), (bsel_sb[:], bsel[:]),
                              (rneg_sb[:], rowneg[:]), (rpos_sb[:], rowpos[:]),
                              (bo_sb[64:65, :], bo[:])):
            ch_cd.inc(g.dma_start(out=out_ap, in_=in_ap))
        ch_cd.inc(g.dma_start(out=qst[64:73, :], in_=maskq[:]))
        ch_cd.inc(g.dma_start(out=kst[64:73, :], in_=maskk[:]))
    P("gpsimd", g_consts)
    ch_cd.bump(7)
    CONSTS = ch_cd.n
    ch_aw1 = c("aw1", 16)
    ch_aw2 = c("aw2", 16)
    WQKV_RDY = 3 * 16
    WO_RDY = 16

    def g_wqkv(g):
        for w_sb_, wdram in ((wq_sb, wqT), (wk_sb, wkT), (wv_sb, wvT)):
            ch_aw1.inc(g.dma_start(
                out=w_sb_[:].rearrange("p (ft m) -> p ft m", ft=8),
                in_=wdram.rearrange("(ft p) m -> p ft m", p=128)))

    def g_wo(g):
        ch_aw2.inc(g.dma_start(
            out=wo_sb[:].rearrange("p (ft m) -> p ft m", ft=8),
            in_=woT.rearrange("(ft p) m -> p ft m", p=128)))

    def f_ones_init(v, cd=CONSTS):
        v.wait_ge(ch_cd.sem, cd)
        v.memset(epst[:], EPS)
        v.memset(vst[:].rearrange("p (b e) -> p b e", b=16)[:, :, 64:65], 1.0)
        ch_bob.inc(v.memset(ones64[64:65, :], 1.0))
    P("vector", f_ones_init)
    ch_bob.bump()
    ONES_RDY = ch_bob.n

    for half in (0, 1):
        def f_bob_mm(t, half=half, onr=ONES_RDY, ze=ch_zev.n):
            t.wait_ge(ch_bob.sem, onr)
            if ze:
                t.wait_ge(ch_zev.sem, ze)
            ch_z.inc(t.matmul(ps_rl[0:64, :], ones64[64:65, :],
                              bo_sb[64:65, half * 512:(half + 1) * 512],
                              start=True, stop=True, tile_position=(64, 0)))
        P("tensor", f_bob_mm)
        ch_z.bump()

        def f_bob_ev(sc, half=half, zc=ch_z.n):
            sc.wait_ge(ch_z.sem, zc)
            inst = sc.copy(bo_bc[0:64, half * 512:(half + 1) * 512], ps_rl[0:64, :])
            ch_zev.inc(inst)
        P("scalar", f_bob_ev)
        ch_zev.bump()
    for half in (0, 1):
        def f_bob_mm2(t, half=half, ze=ch_zev.n):
            t.wait_ge(ch_zev.sem, ze)
            ch_z.inc(t.matmul(ps_rl[64:128, :], ones64[64:65, 0:64],
                              bo_sb[64:65, half * 512:(half + 1) * 512],
                              start=True, stop=True, tile_position=(64, 64)))
        P("tensor", f_bob_mm2)
        ch_z.bump()

        def f_bob_ev2(sc, half=half, zc=ch_z.n):
            sc.wait_ge(ch_z.sem, zc)
            inst = sc.copy(bo_bc[64:128, half * 512:(half + 1) * 512],
                           ps_rl[64:128, :])
            ch_zev.inc(inst)
            ch_bob.inc(sc.drain())
        P("scalar", f_bob_ev2)
        ch_zev.bump()
        ch_bob.bump()
    BOB = ch_bob.n
    PS_RL_FREE = ch_zev.n
    BOBZEV = ch_zev.n
    ps_otx = (ps_ot, ps_rl)
    psfree = {0: 0, 1: 0}

    NTIL = 4
    NPK = 64
    exp_hist = []
    nrm_hist = []
    PJ_EV = {}

    def emit_P2(n, rt, ODS_DONE):
        nonlocal PS_RL_FREE
        grt = n * 4 + rt
        zbuf = grt % 2
        zi = zin[:, zbuf * D:(zbuf + 1) * D]
        zt_ = zlT_t[:, zbuf * 1024:(zbuf + 1) * 1024]
        zbt = zb[:, zbuf * D:(zbuf + 1) * D]
        ot_ = x1t[:, zbuf * D:(zbuf + 1) * D]
        sl = slice(zbuf, zbuf + 1)
        chz = ch_zld[zbuf]

        def f_zld(s, n=n, rt=rt, zbuf=zbuf, chz=chz, ods=ODS_DONE,
                  need=(p2_ztr_hist[-2] if len(p2_ztr_hist) >= 2 else None)):
            s.wait_ge(ch_ods[0].sem, ods[0])
            s.wait_ge(ch_ods[1].sem, ods[1])
            if need:
                s.wait_ge(need[0].sem, need[1])
            rowz = zin[:].ap[0][0]
            for h1i in range(4):
                src = bass.AP(
                    tensor=od2.tensor,
                    offset=(n % 2) * 8192 * 64 + rt * 4 * 64 + h1i * 64,
                    ap=[[16384, 32], [1024, 16], [1, 64]])
                dstz = bass.AP(
                    tensor=zin[:].tensor,
                    offset=h1i * 32 * rowz + zbuf * D,
                    ap=[[rowz, 32], [64, 16], [1, 64]])
                chz.inc(s.dma_start(out=dstz, in_=src))
        P("gpsimd", f_zld)
        chz.bump(4)
        ZLD = chz.n

        trb = tr_slot()
        def f_ztr(t, zi=zi, trb=trb, chz=chz, zld=ZLD,
                  tev=ch_trevx[trb].n):
            t.wait_ge(chz.sem, zld)
            t.wait_ge(ch_trevx[trb].sem, tev)
            for ft in range(8):
                inst = t.transpose(
                    ps_trx[trb][:, ft * 128:(ft + 1) * 128],
                    zi[:, ft * 128:(ft + 1) * 128],
                    ident_sb[:])
            ch_trx[trb].inc(inst)
        P("tensor", f_ztr)
        ch_trx[trb].bump()
        p2_ztr_hist.append((ch_trx[trb], ch_trx[trb].n))

        def f_ztrev(v, zt_=zt_, trb=trb, trc=ch_trx[trb].n):
            v.wait_ge(ch_trx[trb].sem, trc)
            ch_trevx[trb].inc(v.tensor_copy(zt_, ps_trx[trb][:]))
        P("vector", f_ztrev)
        ch_trevx[trb].bump()
        ZT_RDY = (ch_trevx[trb], ch_trevx[trb].n)

        for fc in range(2):
            psp = ps_p0 if fc == 0 else ps_p1
            def f_zmm(t, zt_=zt_, fc=fc, psp=psp, zrdy=ZT_RDY,
                      pf=psfree[fc], bob=BOB):
                t.wait_ge(ch_aw2.sem, WO_RDY)
                t.wait_ge(zrdy[0].sem, zrdy[1])
                if pf:
                    t.wait_ge(ch_zev.sem, pf)
                t.wait_ge(ch_bob.sem, bob)
                for ft in range(8):
                    inst = t.matmul(
                        psp[:, :],
                        zt_[:, ft * 128:(ft + 1) * 128],
                        wo_sb[:, ft * D + fc * 512: ft * D + (fc + 1) * 512],
                        start=(ft == 0), stop=(ft == 7))
                ch_z.inc(inst)
            P("tensor", f_zmm)
            ch_z.bump()

            def f_zev(v, zbt=zbt, fc=fc, psp=psp, zc=ch_z.n):
                v.wait_ge(ch_z.sem, zc)
                inst = v.tensor_add(zbt[:, fc * 512:(fc + 1) * 512],
                                    psp[:, :],
                                    bo_bc[:, fc * 512:(fc + 1) * 512])
                ch_zev.inc(inst)
            P("vector", f_zev)
            ch_zev.bump()
            psfree[fc] = ch_zev.n

        ln_stats_sc(P, ch_rst, zbt, sqj[:], junk2, sl, s1, s2, mean,
                    msq, rstd, D, wait=(ch_zev, ch_zev.n))

        def f_xl(s, n=n, rt=rt, zbuf=zbuf, chx=ch_xd[zbuf],
                 need=xin_last[zbuf]):
            if need > 0:
                s.wait_ge(ch_vfree.sem, need)
            chx.inc(s.dma_start(
                out=xin[:, zbuf * D:(zbuf + 1) * D],
                in_=xr[n, rt * 128:(rt + 1) * 128, :]))
        P("gpsimd", f_xl)
        ch_xd[zbuf].bump()
        XL = (ch_xd[zbuf], ch_xd[zbuf].n)

        def f_out(v, zbt=zbt, ot_=ot_, zbuf=zbuf, sl=sl, rs=ch_rst.n,
                  xd=XL, stneed=(ch_sto[zbuf], ch_sto[zbuf].n)):
            v.wait_ge(ch_rst.sem, rs)
            v.wait_ge(xd[0].sem, xd[1])
            if stneed[1] > 0:
                v.wait_ge(stneed[0].sem, stneed[1])
            v.tensor_scalar(zbt, zbt, mean[:, sl], rstd[:, sl],
                            AL.subtract, AL.mult)
            v.drain()
            ch_x1.inc(v.tensor_add(ot_, zbt,
                                   xin[:, zbuf * D:(zbuf + 1) * D]))
            ch_vfree.inc(v.drain())
        P("vector", f_out)
        ch_x1.bump()
        ch_vfree.bump()
        xin_last[zbuf] = ch_vfree.n

        def f_sto(s, n=n, rt=rt, ot_=ot_, chs=ch_sto[zbuf], xc=ch_x1.n):
            s.wait_ge(ch_x1.sem, xc)
            chs.inc(s.dma_start(
                out=x1[n * 512 + rt * 128: n * 512 + (rt + 1) * 128, :],
                in_=ot_))
        P("gpsimd", f_sto)
        ch_sto[zbuf].bump()
        if rt == 3:
            p2_zld_hist[n] = (ch_zld[0].n, ch_zld[1].n)
            sto_hist[n] = ((ch_sto[0], ch_sto[0].n), (ch_sto[1], ch_sto[1].n))


    g_vld_hist = {}
    p2_zld_hist = {}
    p2_ztr_hist = []
    orw_hist = []
    tok_hist = {}
    xin_last = {0: 0, 1: 0}
    stk_vsp = {}
    ODS_hist = {}
    sto_hist = {}
    xln_hist = {0: None, 1: None}
    ch_trd = (c("trd0", 16), c("trd1", 16))

    def emit_LN3(tt):
        buf = tt % 2
        xi = xin[:, buf * D:(buf + 1) * D]
        xl = xln_sb[:, buf * D:(buf + 1) * D]
        sl = slice(buf, buf + 1)
        chx = ch_xd[buf]

        def f_l3(s, tt=tt, xi=xi, chx=chx, need=xin_last[buf],
                 stw=sto_hist[tt // 4]):
            if need:
                s.wait_ge(ch_vfree.sem, need)
            s.wait_ge(stw[0][0].sem, stw[0][1])
            s.wait_ge(stw[1][0].sem, stw[1][1])
            chx.inc(s.dma_start(out=xi, in_=x1[tt * 128:(tt + 1) * 128, :]))
        P("gpsimd", f_l3)
        chx.bump()

        ln_stats_sc(P, ch_rst, xi, sqj[:], sqj[:], sl, s1, s2, mean,
                    msq, rstd, D, wait=(chx, chx.n), dr=True)

        def f_ap3(v, xi=xi, xl=xl, sl=sl, rs=ch_rst.n, xh=xln_hist[buf]):
            v.wait_ge(ch_rst.sem, rs)
            if xh:
                for c_, n_ in xh:
                    v.wait_ge(c_.sem, n_)
            ch_xln.inc(v.tensor_scalar(xl, xi, mean[:, sl], rstd[:, sl],
                                       AL.subtract, AL.mult))
            ch_vfree.inc(v.drain())
        P("vector", f_ap3)
        ch_xln.bump()
        ch_vfree.bump()
        xin_last[buf] = ch_vfree.n

        trb = tr_slot()
        def f_t3(t, xl=xl, trb=trb, xc=ch_xln.n, tev=ch_trevx[trb].n):
            t.wait_ge(ch_xln.sem, xc)
            t.wait_ge(ch_trevx[trb].sem, tev)
            for ft in range(8):
                inst = t.transpose(ps_trx[trb][:, ft * 128:(ft + 1) * 128],
                                   xl[:, ft * 128:(ft + 1) * 128], ident_sb[:])
            ch_trx[trb].inc(inst)
        P("tensor", f_t3)
        ch_trx[trb].bump()

        def f_te3(v, buf=buf, trb=trb, trc=ch_trx[trb].n):
            v.wait_ge(ch_trx[trb].sem, trc)
            ch_trevx[trb].inc(v.tensor_copy(xln_sb[:, buf * D:(buf + 1) * D],
                                            ps_trx[trb][:]))
        P("vector", f_te3)
        ch_trevx[trb].bump()
        TRE = (ch_trevx[trb], ch_trevx[trb].n)

        def f_s3(s, tt=tt, buf=buf, cht=ch_trd[buf], tev=TRE):
            s.wait_ge(tev[0].sem, tev[1])
            cht.inc(s.dma_start(
                out=xt_scr[:, tt * 128:(tt + 1) * 128].rearrange(
                    "(ft p) r -> p ft r", p=128),
                in_=xln_sb[:, buf * D:(buf + 1) * D].rearrange(
                    "p (ft r) -> p ft r", ft=8)))
        P("sync", f_s3)
        ch_trd[buf].bump()
        xln_hist[buf] = ((ch_trd[buf], ch_trd[buf].n),)

    def emit_P1_front(n, tt):
        gti = n * NTIL + tt
        buf = gti % 2
        xi = xin[:, buf * D:(buf + 1) * D]
        xl = xln_sb[:, buf * D:(buf + 1) * D]
        xt = xlnT_t[:, buf * 1024:(buf + 1) * 1024]
        sl = slice(buf, buf + 1)
        chx = ch_xd[buf]

        def f_load(s, n=n, tt=tt, xi=xi, chx=chx,
                   need=xin_last[buf]):
            if need:
                s.wait_ge(ch_vfree.sem, need)
            chx.inc(s.dma_start(
                out=xi, in_=x[n, tt * 128:(tt + 1) * 128, :]))
        P("gpsimd", f_load)
        chx.bump()

        ln_stats_sc(P, ch_rst, xi, sqj[:], junk2, sl, s1, s2, mean,
                    msq, rstd, D, wait=(chx, chx.n))

        def f_apply(v, xi=xi, xl=xl, sl=sl, rs=ch_rst.n,
                    xh=xln_hist[buf]):
            v.wait_ge(ch_rst.sem, rs)
            if xh:
                for c_, n_ in xh:
                    v.wait_ge(c_.sem, n_)
            ch_xln.inc(v.tensor_scalar(xl, xi, mean[:, sl], rstd[:, sl],
                                       AL.subtract, AL.mult))
            ch_vfree.inc(v.drain())
        P("vector", f_apply)
        ch_xln.bump()
        ch_vfree.bump()
        xin_last[buf] = ch_vfree.n

        trb = tr_slot()
        def f_tr(t, xl=xl, trb=trb, xc=ch_xln.n, tev=ch_trevx[trb].n,
                 cd=CONSTS):
            t.wait_ge(ch_xln.sem, xc)
            t.wait_ge(ch_cd.sem, cd)
            if tev:
                t.wait_ge(ch_trevx[trb].sem, tev)
            for ft in range(8):
                inst = t.transpose(ps_trx[trb][:, ft * 128:(ft + 1) * 128],
                                   xl[:, ft * 128:(ft + 1) * 128],
                                   ident_sb[:])
            ch_trx[trb].inc(inst)
        P("tensor", f_tr)
        ch_trx[trb].bump()
        xln_hist[buf] = ((ch_trx[trb], ch_trx[trb].n),)

        def f_trev(v, xt=xt, trb=trb, trc=ch_trx[trb].n):
            v.wait_ge(ch_trx[trb].sem, trc)
            ch_trevx[trb].inc(v.tensor_copy(xt, ps_trx[trb][:]))
        P("vector", f_trev)
        ch_trevx[trb].bump()
        XT_RDY = (ch_trevx[trb], ch_trevx[trb].n)

        for w_sb_, dtok in ((wq_sb, qtok), (wk_sb, ktok), (wv_sb, vtok)):
            for fc in range(2):
                psp = ps_p0 if fc == 0 else ps_p1
                def f_pmm(t, w_sb_=w_sb_, fc=fc, psp=psp, xt=xt,
                          xrdy=XT_RDY, pf=psfree[fc]):
                    t.wait_ge(ch_aw1.sem, WQKV_RDY)
                    t.wait_ge(xrdy[0].sem, xrdy[1])
                    if pf:
                        t.wait_ge(ch_zev.sem, pf)
                    for ft in range(8):
                        inst = t.matmul(
                            psp[:, :],
                            xt[:, ft * 128:(ft + 1) * 128],
                            w_sb_[:, ft * D + fc * 512: ft * D + (fc + 1) * 512],
                            start=(ft == 0), stop=(ft == 7))
                    ch_pj.inc(inst)
                P("tensor", f_pmm)
                ch_pj.bump()

                def f_pev(v, dtok=dtok, buf=buf, fc=fc, psp=psp, pj=ch_pj.n,
                          vspn=(ch_vsp[buf].n if dtok is vtok else 0),
                          chvs=ch_vsp[buf]):
                    v.wait_ge(ch_pj.sem, pj)
                    if vspn:
                        v.wait_ge(chvs.sem, vspn)
                    inst = v.tensor_copy(
                        dtok[:, buf * D + fc * 512: buf * D + (fc + 1) * 512],
                        psp[:, :])
                    ch_zev.inc(inst)
                    ch_pjev.inc(v.drain())
                P("vector", f_pev)
                ch_zev.bump()
                ch_pjev.bump()
                psfree[fc] = ch_zev.n
        TOK_RDY = ch_pjev.n

        def f_vsp(s, n=n, tt=tt, buf=buf, chv=ch_vsp[buf], tok=TOK_RDY,
                  vldw=(g_vld_hist.get(n - 2) if n >= 2 else None)):
            s.wait_ge(ch_pjev.sem, tok)
            if vldw:
                for _i in range(4):
                    s.wait_ge(ch_vld[_i].sem, vldw[_i])
            chv.inc(s.dma_start(
                out=v_scr[n % 2, tt * 128:(tt + 1) * 128, :],
                in_=vtok[:, buf * D:(buf + 1) * D]))
        P("sync", f_vsp)
        ch_vsp[buf].bump()
        tok_hist[(n, tt)] = TOK_RDY

    def emit_P1_qtr(n, tt):
        gti = n * NTIL + tt
        buf = gti % 2
        TOK_RDY = tok_hist[(n, tt)]

        for src_tok, dst_st in ((qtok, qst), (ktok, kst)):
            for hg in range(2):
                trb = tr_slot()
                def f_qtr(t, src_tok=src_tok, buf=buf, hg=hg, trb=trb,
                          tok=TOK_RDY, tev=ch_trevx[trb].n):
                    t.wait_ge(ch_pjev.sem, tok)
                    t.wait_ge(ch_trevx[trb].sem, tev)
                    for hh in range(8):
                        h = hg * 8 + hh
                        inst = t.transpose(
                            ps_trx[trb][0:64, hh * 128:(hh + 1) * 128],
                            src_tok[:, buf * D + h * 64: buf * D + (h + 1) * 64],
                            ident_sb[:])
                    ch_trx[trb].inc(inst)
                P("tensor", f_qtr)
                ch_trx[trb].bump()

                def f_qtrev(v, dst_st=dst_st, tt=tt, hg=hg, trb=trb,
                            trc=ch_trx[trb].n):
                    v.wait_ge(ch_trx[trb].sem, trc)
                    out_ap = dst_st[0:64, :].rearrange(
                        "p (s h) -> p s h", h=16)[:, tt * 128:(tt + 1) * 128,
                                                  hg * 8:(hg + 1) * 8]
                    in_ap = ps_trx[trb][0:64, :].rearrange(
                        "p (hh s) -> p s hh", hh=8)
                    inst = v.tensor_copy(out_ap, in_ap)
                    ch_trevx[trb].inc(inst)
                    ch_stk.inc(v.drain())
                P("vector", f_qtrev)
                ch_trevx[trb].bump()
                ch_stk.bump()

    emit_P1_front(0, 0)
    emit_P1_front(0, 1)
    P("gpsimd", g_wqkv)
    ch_aw1.bump(3)
    emit_P1_qtr(0, 0)
    emit_P1_front(0, 2)
    P("gpsimd", g_wo)
    ch_aw2.bump(1)
    emit_P1_qtr(0, 1)
    emit_P1_front(0, 3)
    emit_P1_qtr(0, 2)
    emit_P1_qtr(0, 3)
    stk_vsp[0] = (ch_stk.n, (ch_vsp[0].n, ch_vsp[1].n))

    for n in range(B):
        STK_RDY, VSP_DONE = stk_vsp[n]
        for grp in range(NPK // 4):
            ggrp = n * 16 + grp
            gbuf = ggrp % 2
            obuf = ggrp % 2

            vslot = ggrp % 4
            def f_vld(s, n=n, grp=grp, vslot=vslot, chv=ch_vld[vslot],
                      vsp=VSP_DONE, need=(nrm_hist[ggrp - 4] if ggrp >= 4 else 0)):
                s.wait_ge(ch_vsp[0].sem, vsp[0])
                s.wait_ge(ch_vsp[1].sem, vsp[1])
                if need:
                    s.wait_ge(ch_nrm.sem, need)
                row = vst[:].ap[0][0]
                dst = bass.AP(tensor=vst[:].tensor, offset=vslot * 260,
                              ap=[[row, 128], [65, 4], [1, 64]])
                srcb = bass.AP(tensor=v_scr.tensor,
                               offset=(n % 2) * 512 * D + grp * 32 * D,
                               ap=[[64, 128], [8 * D, 4], [1, 64]])
                chv.inc(s.dma_start(out=dst, in_=srcb))
            P("gpsimd", f_vld)
            ch_vld[vslot].bump()
            VLD = ch_vld[vslot].n

            for pk4 in range(4):
                pk = grp * 4 + pk4
                abank = (pk % 8) // 4
                aoff = (pk % 4) * 0
                acol = abank * 512 + pk4 * 128

                def f_amm(t, n=n, pk=pk, acol=acol, stk=STK_RDY,
                          ex=(exp_hist[ggrp - 2] if ggrp >= 2 else 0)):
                    t.wait_ge(ch_stk.sem, stk)
                    if ex:
                        t.wait_ge(ch_exp.sem, ex)
                    inst = t.matmul(ps_a[:, acol:acol + 128],
                                    kst[0:73, pk * 128:(pk + 1) * 128],
                                    qst[0:73, pk * 128:(pk + 1) * 128],
                                    start=True, stop=True)
                    if pk % 4 == 3:
                        ch_am.inc(inst)
                P("tensor", f_amm)
                if pk % 4 == 3:
                    ch_am.bump()

            def f_exp(sc, gbuf=gbuf, abank=((grp % 2)), am=ch_am.n):
                sc.wait_ge(ch_am.sem, am)
                ch_exp.inc(sc.activation(
                    pt_sb[:, gbuf * 512:(gbuf + 1) * 512],
                    ps_a[:, abank * 512:(abank + 1) * 512],
                    AF.Exp, scale=0.125))
            P("scalar", f_exp)
            ch_exp.bump()
            exp_hist.append(ch_exp.n)
            EXPG = ch_exp.n

            psot = ps_otx[obuf]
            for pk4 in range(4):
                def f_pv(t, pk4=pk4, gbuf=gbuf, vslot=vslot, psot=psot,
                         expg=EXPG, vld=VLD,
                         nrm=(nrm_hist[ggrp - 2] if ggrp >= 2 else 0),
                         bz=(BOBZEV if ggrp == 1 else 0)):
                    t.wait_ge(ch_exp.sem, expg)
                    t.wait_ge(ch_vld[vslot].sem, vld)
                    if pk4 == 0 and nrm:
                        t.wait_ge(ch_nrm.sem, nrm)
                    if pk4 == 0 and bz:
                        t.wait_ge(ch_zev.sem, bz)
                    inst = t.matmul(
                        psot[0:65, pk4 * 128:(pk4 + 1) * 128],
                        vst[:, vslot * 260 + pk4 * 65: vslot * 260 + (pk4 + 1) * 65],
                        pt_sb[:, gbuf * 512 + pk4 * 128: gbuf * 512 + (pk4 + 1) * 128],
                        start=True, stop=True)
                    if pk4 == 3:
                        ch_pv.inc(inst)
                P("tensor", f_pv)
                if pk4 == 3:
                    ch_pv.bump()

            def f_oev(v, psot=psot, obuf=obuf, pv=ch_pv.n,
                      orw=(orw_hist[ggrp - 2] if ggrp >= 2 else 0)):
                v.wait_ge(ch_pv.sem, pv)
                if orw:
                    v.wait_ge(ch_orw.sem, orw)
                ch_nrm.inc(v.tensor_copy(otn[0:65, obuf * 512:(obuf + 1) * 512],
                                         psot[0:65, :]))
            P("vector", f_oev)
            ch_nrm.bump()
            nrm_hist.append(ch_nrm.n)

            trb = tr_slot()
            def f_otr(t, trb=trb, obuf=obuf, nm=ch_nrm.n,
                      tev=ch_trevx[trb].n):
                t.wait_ge(ch_nrm.sem, nm)
                t.wait_ge(ch_trevx[trb].sem, tev)
                for pk4 in range(4):
                    inst = t.transpose(
                        ps_trx[trb][:, pk4 * 66:pk4 * 66 + 65],
                        otn[0:65, obuf * 512 + pk4 * 128:
                            obuf * 512 + (pk4 + 1) * 128],
                        ident_sb[0:65, 0:65])
                ch_trx[trb].inc(inst)
            P("tensor", f_otr)
            ch_trx[trb].bump()

            def f_onrm(v, gbuf=gbuf, trb=trb, trc=ch_trx[trb].n,
                       chod=ch_ods[gbuf], ods=ch_ods[gbuf].n):
                v.wait_ge(ch_trx[trb].sem, trc)
                if ods:
                    v.wait_ge(chod.sem, ods)
                pst = ps_trx[trb]
                row = pst[:].ap[0][0]
                lcols = bass.AP(tensor=pst[:].tensor, offset=64,
                                ap=[[row, 128], [66, 4]])
                v.reciprocal(rlc[:], lcols)
                v.drain()
                for pk4 in range(4):
                    inst = v.tensor_scalar(
                        orow[:, gbuf * 256 + pk4 * 64:
                             gbuf * 256 + (pk4 + 1) * 64],
                        pst[:, pk4 * 66:pk4 * 66 + 64],
                        rlc[:, pk4:pk4 + 1], None, AL.mult)
                ch_trevx[trb].inc(inst)
                ch_orw.inc(v.drain())
            P("vector", f_onrm)
            ch_trevx[trb].bump()
            ch_orw.bump()
            orw_hist.append(ch_orw.n)

            def f_osp(s, n=n, grp=grp, gbuf=gbuf, cho=ch_ods[gbuf],
                      orw=ch_orw.n,
                      zldw=(p2_zld_hist.get(n - 2) if n >= 2 else None)):
                s.wait_ge(ch_orw.sem, orw)
                if zldw:
                    s.wait_ge(ch_zld[0].sem, zldw[0])
                    s.wait_ge(ch_zld[1].sem, zldw[1])
                cho.inc(s.dma_start(
                    out=od2[n % 2, grp * 512:(grp + 1) * 512, :].rearrange(
                        "(pk p) e -> p pk e", p=128),
                    in_=orow[:, gbuf * 256:(gbuf + 1) * 256].rearrange(
                        "p (pk e) -> p pk e", pk=4)))
            P("sync", f_osp)
            ch_ods[gbuf].bump()
            if grp % 4 == 1 and n + 1 < B:
                emit_P1_front(n + 1, grp // 4)
            if grp % 4 == 2 and n >= 2:
                emit_LN3(4 * (n - 2) + grp // 4)
            if grp % 4 == 2 and grp >= 6 and n + 1 < B:
                emit_P1_qtr(n + 1, grp // 4 - 1)
            if grp % 4 == 3 and n >= 1:
                emit_P2(n - 1, grp // 4, ODS_hist[n - 1])
        ODS_hist[n] = (ch_ods[0].n, ch_ods[1].n)
        g_vld_hist[n] = tuple(_c.n for _c in ch_vld)
        if n + 1 < B:
            emit_P1_qtr(n + 1, 3)
            stk_vsp[n + 1] = (ch_stk.n, (ch_vsp[0].n, ch_vsp[1].n))

    emit_P2(B - 1, 0, ODS_hist[B - 1])
    emit_LN3(24)
    emit_P2(B - 1, 1, ODS_hist[B - 1])
    emit_LN3(25)
    emit_P2(B - 1, 2, ODS_hist[B - 1])
    emit_LN3(26)
    emit_P2(B - 1, 3, ODS_hist[B - 1])
    for tt in range(27, 32):
        emit_LN3(tt)

    A_PJ = ch_pj.n
    A_AM = ch_am.n
    A_Z = ch_z.n
    A_STO = (ch_sto[0].n, ch_sto[1].n)


    w_sb = {"wf": wq_sb, "uf": wk_sb, "wi": wv_sb, "ui": wo_sb,
            "wc": qst, "uc": kst}
    _h_order = ("wf", "uf", "wi", "ui")
    h_view = lambda rt: w_sb[_h_order[rt // 8]][:, (rt % 8) * D:((rt % 8) + 1) * D]
    wo_sl = qst
    uo_sl = kst
    xtT_sl = sb("xtT_sl", [128, 2 * 8 * 512], BF16)
    cT = sb("cT", [128, 8 * 512], F32)
    cT_bf = zb[:].bitcast(BF16)
    xr_sl = gat[:].bitcast(BF16)
    biases = sb("biases", [128, 4 * 8], F32)
    onesf = bo_bc[:, 0:512]
    bo2_sb = ktok[0:1, 0:D]
    onesr2 = sb("onesr2", [1, 128], BF16)
    hmean = sb("hmean", [128, 32], F32)
    hvar = sb("hvar", [128, 32], F32)
    hrstd = sb("hrstd", [128, 32], F32)
    hms = sb("hms", [128, 32], F32)
    sqj2 = otn[:].bitcast(BF16)
    fjunk2 = qtok[:, 0:D]
    ps_g = (ps_a,)
    ps_h = (ps_p0, ps_p1)
    ch_xtl = (c("xtl0", 16), c("xtl1", 16))
    ch_xrl = (c("xrl0", 16), c("xrl1", 16))
    ch_g = c("g")
    ch_gev = c("gev")
    ch_cup = c("cup")
    ch_cbf = c("cbf")
    ch_h = c("h")
    ch_hev = c("hev")
    ch_hst = c("hst")
    ch_ho = c("ho")

    def g_fc1(g, apj=A_PJ):
        g.wait_ge(ch_pj.sem, apj)
        ch_cd.inc(g.dma_start(out=bo2_sb[:], in_=bov[:]))
        for i, bsrc in enumerate((bfv, biv, bcv, incv)):
            ch_cd.inc(g.dma_start(
                out=biases[:, i * 8:(i + 1) * 8],
                in_=bsrc.rearrange("o (mt p) -> p (o mt)", p=128),
                allow_slow_non_contiguous=True))
        for nm, dn in (("wf", "wfT"), ("uf", "ufT"), ("wi", "wiT")):
            ch_cd.inc(g.dma_start(
                out=w_sb[nm][:].rearrange("p (ft m) -> p ft m", ft=8),
                in_=wd[dn].rearrange("(ft p) m -> p ft m", p=128)))
    P("gpsimd", g_fc1)
    ch_cd.bump(8)

    def g_fc2(g, aam=A_AM):
        g.wait_ge(ch_am.sem, aam)
        for nm, dn in (("wc", "wcT"), ("uc", "ucT")):
            ch_cd.inc(g.dma_start(
                out=w_sb[nm][:].rearrange("p (ft m) -> p ft m", ft=8),
                in_=wd[dn].rearrange("(ft p) m -> p ft m", p=128)))
    P("gpsimd", g_fc2)
    ch_cd.bump(2)

    def g_fc3(g, az=A_Z):
        g.wait_ge(ch_z.sem, az)
        ch_cd.inc(g.dma_start(
            out=w_sb["ui"][:].rearrange("p (ft m) -> p ft m", ft=8),
            in_=wd["uiT"].rearrange("(ft p) m -> p ft m", p=128)))
    P("gpsimd", g_fc3)
    ch_cd.bump(1)
    FCONSTS = ch_cd.n

    NT2 = S // 128
    LN3 = tuple((ch.sem, ch.n) for ch in ch_trd)

    def f_fmemset(v):
        v.drain()
        v.memset(onesf[:], 1.0)
        v.memset(onesr2[:], 1.0)
    P("vector", f_fmemset)

    def f_cinit(v, fc_=FCONSTS):
        v.wait_ge(ch_cd.sem, fc_)
        v.drain()
        for mt in range(8):
            inst = v.tensor_scalar(cT[:, mt * 512:(mt + 1) * 512], onesf[:],
                                   biases[:, 24 + mt:24 + mt + 1], None, AL.mult)
        ch_cup.inc(inst)
    P("vector", f_cinit)
    ch_cup.bump()

    GATES = (("wf", "uf", 0, AF.Sigmoid), ("wi", "ui", 8, AF.Sigmoid),
             ("wc", "uc", 16, AF.Tanh))
    for t_step in range(8):
        buf = t_step % 2

        def f_xtl(s, t_step=t_step, buf=buf, chxt=ch_xtl[buf], ln=LN3,
                  need=ch_g.n):
            for sem_, v_ in ln:
                s.wait_ge(sem_, v_)
            if t_step >= 2:
                s.wait_ge(ch_g.sem, need)
            chxt.inc(s.dma_start(
                out=xtT_sl[:, buf * 4096:(buf + 1) * 4096].rearrange(
                    "p (ft q) -> p ft q", ft=8),
                in_=xt_scr[:, t_step * 512:(t_step + 1) * 512].rearrange(
                    "(ft p) q -> p ft q", p=128)))
        P("sync", f_xtl)
        ch_xtl[buf].bump()
        XTL = (ch_xtl[buf], ch_xtl[buf].n)

        def f_cbf(v, cu=ch_cup.n):
            v.wait_ge(ch_cup.sem, cu)
            ch_cbf.inc(v.tensor_copy(cT_bf[:], cT[:]))
        P("vector", f_cbf)
        ch_cbf.bump()
        CBF = ch_cbf.n

        for mt in range(8):
            for gi, (wn, un, bcol, func) in enumerate(GATES):
                kglob = (t_step * 8 + mt) * 3 + gi
                pg = kglob % 2

                def f_gmm(t, wn=wn, un=un, mt=mt, pg=pg, buf=buf, xtl=XTL,
                          cbf=CBF, gev=(ch_gev.n - 1)):
                    t.wait_ge(xtl[0].sem, xtl[1])
                    t.wait_ge(ch_cbf.sem, cbf)
                    if gev > 0:
                        t.wait_ge(ch_gev.sem, gev)
                    for ft in range(8):
                        t.matmul(ps_a[:, pg * 512:(pg + 1) * 512],
                                 w_sb[wn][:, ft * D + mt * 128: ft * D + mt * 128 + 128],
                                 xtT_sl[:, buf * 4096 + ft * 512:
                                        buf * 4096 + (ft + 1) * 512],
                                 start=(ft == 0), stop=False)
                    for m2 in range(8):
                        inst = t.matmul(
                            ps_a[:, pg * 512:(pg + 1) * 512],
                            w_sb[un][:, m2 * D + mt * 128: m2 * D + mt * 128 + 128],
                            cT_bf[:, m2 * 512:(m2 + 1) * 512],
                            start=False, stop=(m2 == 7))
                    ch_g.inc(inst)
                P("tensor", f_gmm)
                ch_g.bump()

                def f_gev(sc, gi=gi, pg=pg, func=func, bcol=bcol, mt=mt,
                          gc=ch_g.n, cup=(ch_cup.n if gi == 0 else 0)):
                    sc.wait_ge(ch_g.sem, gc)
                    if cup:
                        sc.wait_ge(ch_cup.sem, cup)
                    ch_gev.inc(sc.activation(
                        gat[:, gi * 512:(gi + 1) * 512],
                        ps_a[:, pg * 512:(pg + 1) * 512], func,
                        bias=biases[:, bcol + mt:bcol + mt + 1]))
                P("scalar", f_gev)
                ch_gev.bump()

            def f_cup(v, mt=mt, ge=ch_gev.n):
                v.wait_ge(ch_gev.sem, ge)
                cslice = cT[:, mt * 512:(mt + 1) * 512]
                v.tensor_mul(cslice, cslice, gat[:, 0:512])
                v.tensor_mul(gat[:, 512:1024], gat[:, 512:1024], gat[:, 1024:1536])
                v.drain()
                ch_cup.inc(v.tensor_add(cslice, cslice, gat[:, 512:1024]))
            P("vector", f_cup)
            ch_cup.bump()
    SCAN = ch_cup.n
    SCAN_G = ch_g.n

    def f_cfin(v, sc_=SCAN):
        v.wait_ge(ch_cup.sem, sc_)
        ch_cbf.inc(v.tensor_copy(cT_bf[:], cT[:]))
    P("vector", f_cfin)
    ch_cbf.bump()
    CFIN = ch_cbf.n

    for fc in range(2):
        def f_wol(g, fc=fc, sg=SCAN_G):
            g.wait_ge(ch_g.sem, sg)
            ch_cd.inc(g.dma_start(
                out=wo_sl[:, fc * 4096:(fc + 1) * 4096].rearrange(
                    "p (ft q) -> p ft q", ft=8),
                in_=wd["woT2"][:, fc * 512:(fc + 1) * 512].rearrange(
                    "(ft p) q -> p ft q", p=128)))
            ch_cd.inc(g.dma_start(
                out=uo_sl[:, fc * 4096:(fc + 1) * 4096].rearrange(
                    "p (ft q) -> p ft q", ft=8),
                in_=wd["uoT"][:, fc * 512:(fc + 1) * 512].rearrange(
                    "(ft p) q -> p ft q", p=128)))
        P("gpsimd", f_wol)
        ch_cd.bump(2)
    WOL = ch_cd.n

    for rt in range(NT2):
        buf = rt % 2
        bsl = rt % 4

        def f_xrl(g, rt=rt, buf=buf, chxr=ch_xrl[buf], sc_=SCAN,
                  need=(ch_h.n - 1)):
            g.wait_ge(ch_cup.sem, sc_)
            if rt >= 2 and need > 0:
                g.wait_ge(ch_h.sem, need)
            chxr.inc(g.dma_start(
                out=xr_sl[:, buf * 1024:(buf + 1) * 1024].rearrange(
                    "p (ft r) -> p ft r", ft=8),
                in_=xt_scr[:, rt * 128:(rt + 1) * 128].rearrange(
                    "(ft p) r -> p ft r", p=128)))
        P("gpsimd", f_xrl)
        ch_xrl[buf].bump()
        XRL = (ch_xrl[buf], ch_xrl[buf].n)

        for fc in range(2):
            ph = (rt * 2 + fc) % 2

            def f_hmm(t, rt=rt, buf=buf, fc=fc, ph=ph, bsl=bsl, xrl=XRL,
                      cfin=CFIN, wol=WOL, hev=(ch_hev.n - 1)):
                t.wait_ge(xrl[0].sem, xrl[1])
                t.wait_ge(ch_cbf.sem, cfin)
                t.wait_ge(ch_cd.sem, wol)
                if hev > 0:
                    t.wait_ge(ch_hev.sem, hev)
                psh = ps_h[ph]
                for ft in range(8):
                    t.matmul(psh[:, :],
                             xr_sl[:, buf * 1024 + ft * 128:
                                   buf * 1024 + (ft + 1) * 128],
                             wo_sl[:, fc * 4096 + ft * 512:
                                   fc * 4096 + (ft + 1) * 512],
                             start=(ft == 0), stop=False)
                for m2 in range(8):
                    t.matmul(psh[:, :],
                             cT_bf[:, m2 * 512 + bsl * 128:
                                   m2 * 512 + (bsl + 1) * 128],
                             uo_sl[:, fc * 4096 + m2 * 512:
                                   fc * 4096 + (m2 + 1) * 512],
                             start=False, stop=False)
                inst = t.matmul(psh[:, :], onesr2[:],
                                bo2_sb[0:1, fc * 512:(fc + 1) * 512],
                                start=False, stop=True)
                ch_h.inc(inst)
            P("tensor", f_hmm)
            ch_h.bump()

            def f_hev(sc, rt=rt, fc=fc, ph=ph, hc=ch_h.n):
                sc.wait_ge(ch_h.sem, hc)
                ch_hev.inc(sc.activation(
                    h_view(rt)[:, fc * 512:(fc + 1) * 512],
                    ps_h[ph][:, :], AF.Tanh))
            P("scalar", f_hev)
            ch_hev.bump()

        def f_hstat(v, rt=rt, he=ch_hev.n):
            v.wait_ge(ch_hev.sem, he)
            v.drain()
            hrow = h_view(rt)
            v.reduce_sum(hmean[:, rt:rt + 1], hrow, axis=AX)
            ch_hst.inc(v.scalar_tensor_tensor(
                sqj2[:], hrow, 1.0, hrow, AL.mult, AL.mult,
                accum_out=hvar[:, rt:rt + 1]))
        P("vector", f_hstat)
        ch_hst.bump()
    HSTAT = ch_hst.n

    def f_hpre(v, hs=HSTAT):
        v.wait_ge(ch_hst.sem, hs)
        v.scalar_tensor_tensor(hms[:], hmean[:], 1.0 / (D * D), hmean[:],
                               AL.mult, AL.mult)
        v.tensor_scalar_mul(hvar[:], hvar[:], 1.0 / D)
        v.tensor_scalar_mul(hmean[:], hmean[:], 1.0 / D)
        v.drain()
        ch_hst.inc(v.tensor_sub(hvar[:], hvar[:], hms[:]))
    P("vector", f_hpre)
    ch_hst.bump()

    def f_hrstd(sc, hs=ch_hst.n):
        sc.wait_ge(ch_hst.sem, hs)
        sc.activation(hms[:], hvar[:], AF.Ln, bias=epst[:])
        sc.drain()
        ch_rst.inc(sc.activation(hrstd[:], hms[:], AF.Exp, scale=-0.5))
    P("scalar", f_hrstd)
    ch_rst.bump()
    HR = ch_rst.n

    for rt in range(NT2):
        buf = rt % 2
        xi = xin[:, buf * D:(buf + 1) * D]
        ot = x1t[:, buf * D:(buf + 1) * D]

        def f_xl2(s, rt=rt, xi=xi, chx2=ch_xd[buf], need=(ch_ho.n - 1),
                  xl_=(xin_last[buf] if rt < 2 else 0)):
            if rt >= 2 and need > 0:
                s.wait_ge(ch_ho.sem, need)
            if xl_:
                s.wait_ge(ch_vfree.sem, xl_)
            chx2.inc(s.dma_start(out=xi, in_=x1[rt * 128:(rt + 1) * 128, :]))
        P("gpsimd", f_xl2)
        ch_xd[buf].bump()
        XD2 = (ch_xd[buf], ch_xd[buf].n)

        def f_happ(v, rt=rt, xi=xi, ot=ot, hr=HR, xd=XD2,
                   stneed=(ch_sto[rt % 2], ch_sto[rt % 2].n)):
            v.wait_ge(ch_rst.sem, hr)
            v.wait_ge(xd[0].sem, xd[1])
            if stneed[1] > 0:
                v.wait_ge(stneed[0].sem, stneed[1])
            v.tensor_scalar(ot, h_view(rt), hmean[:, rt:rt + 1],
                            hrstd[:, rt:rt + 1], AL.subtract, AL.mult)
            v.drain()
            ch_ho.inc(v.tensor_add(ot, ot, xi))
        P("vector", f_happ)
        ch_ho.bump()

        def f_sto(s, rt=rt, ot=ot, chs=ch_sto[rt % 2], hc=ch_ho.n):
            s.wait_ge(ch_ho.sem, hc)
            chs.inc(s.dma_start(out=out[rt * 128:(rt + 1) * 128, :], in_=ot))
        P("gpsimd", f_sto)
        ch_sto[rt % 2].bump()

    def f_fin(s, tots=tuple((ch.sem, ch.n) for ch in ch_sto)):
        for sem, v_ in tots:
            s.wait_ge(sem, v_)
    P("sync", f_fin)

    with nc.allow_low_precision("bf16 matmul operands / attention weights; "
                                "fp32 accumulation and LN stats throughout"):
        with nc.Block() as block:
            P.emit(block)
    for _ch in _chans:
        assert _ch.emitted == _ch.n, (
            f"channel {_ch.name}: emitted {_ch.emitted} != planned {_ch.n}")
    st.close()
    return nc





_CACHE = {}


def _programs():
    if "attn" not in _CACHE:
        _CACHE["attn"] = build_attn_program()
        _CACHE["ffn"] = build_ffn_program()
    return _CACHE["attn"], _CACHE["ffn"]


def _attn_consts():
    f32 = np.float32
    ident = np.eye(128, dtype=f32)
    bsel = np.zeros((8, 128), f32)
    for b in range(8):
        bsel[b, b * 16:(b + 1) * 16] = 20.0
    rowneg = np.full((1, 128), -20.0, f32)
    rowpos = np.full((1, 128), 20.0, f32)
    return ident, bsel, rowneg, rowpos


def _rows_for_core(j):
    r = np.arange(512)
    return 256 * (r // 32) + 32 * j + (r % 32)


def kernel(L0_attn_x, L0_attn_Wq, L0_attn_Wk, L0_attn_Wv, L0_attn_Wo,
           L0_attn_bo, L0_ffn_wf, L0_ffn_uf, L0_ffn_bf, L0_ffn_wi, L0_ffn_ui,
           L0_ffn_bi, L0_ffn_wc, L0_ffn_uc, L0_ffn_bc, L0_ffn_wo, L0_ffn_uo,
           L0_ffn_bo, L0_ffn_in_c):
    f32 = np.float32
    TT = lambda a: np.ascontiguousarray(np.asarray(a, f32).T)
    row = lambda a: np.ascontiguousarray(np.asarray(a, f32).reshape(1, -1))
    x = np.ascontiguousarray(np.asarray(L0_attn_x, f32))

    ident, bsel, rowneg, rowpos = _attn_consts()
    maskq = np.zeros((9, 8192), f32)
    maskq[0, :] = 20.0
    maskq[1:9, :] = np.tile(bsel, (1, 64))
    maskk = maskq.copy()
    maskk[0, :] = -20.0
    if "fused" not in _CACHE:
        _CACHE["fused"] = build_fused_program()
    nc = _CACHE["fused"]

    common = dict(wqT=TT(L0_attn_Wq), wkT=TT(L0_attn_Wk), wvT=TT(L0_attn_Wv),
                  woT=TT(L0_attn_Wo), bo=row(L0_attn_bo), ident=ident,
                  bsel=bsel, rowneg=rowneg, rowpos=rowpos,
                  maskq=maskq, maskk=maskk,
                  wfT=TT(L0_ffn_wf), ufT=TT(L0_ffn_uf), wiT=TT(L0_ffn_wi),
                  uiT=TT(L0_ffn_ui), wcT=TT(L0_ffn_wc), ucT=TT(L0_ffn_uc),
                  woT2=TT(L0_ffn_wo), uoT=TT(L0_ffn_uo),
                  bfv=row(L0_ffn_bf), biv=row(L0_ffn_bi), bcv=row(L0_ffn_bc),
                  bov=row(L0_ffn_bo), incv=row(L0_ffn_in_c))
    in_maps = []
    for j in range(NCORES):
        rows = _rows_for_core(j)
        in_maps.append(dict(
            common,
            x=np.ascontiguousarray(x[:, 512 * j:512 * (j + 1), :]),
            xr=np.ascontiguousarray(x[:, rows, :])))
    res = run_bass_kernel_spmd(nc, in_maps, core_ids=list(range(NCORES)))

    out = np.empty((B, S, D), f32)
    for j in range(NCORES):
        rows = _rows_for_core(j)
        out[:, rows, :] = res.results[j]["out"].reshape(B, 512, D)
    return out
